# version 18
# baseline (speedup 1.0000x reference)
import numpy as np
import ml_dtypes
import concourse.bass as bass
import concourse.mybir as mybir
from concourse.bass import IndirectOffsetOnAxis
from concourse.bass_utils import run_bass_kernel_spmd

F32 = mybir.dt.float32
BF16 = mybir.dt.bfloat16
I32 = mybir.dt.int32
AF = mybir.ActivationFunctionType
ALU = mybir.AluOpType
AXX = mybir.AxisListType.X

B, T, D = 4, 4096, 1024
H, HV, HD, WIN = 16, 4, 64, 512
NOWN, HALO, NKEY = 2048, 512, 2560
NT = NOWN // 128
NKT = NKEY // 128
NSEL = 2048
NBLK = NSEL // 8
RMS_EPS = 1.1920929e-07
LN_EPS = 1e-5
BIG = 1.0e30
ENGS = ("tensor", "scalar", "vector", "gpsimd", "sync")


class Sem:
    __slots__ = ("name", "step", "n", "h")

    def __init__(self, name, step):
        self.name, self.step, self.n, self.h = name, step, 0, None

    def inc(self):
        self.n += self.step
        return self

    def tok(self):
        return (self, self.n)


class Prog:
    def __init__(self):
        self.ops = {e: [] for e in ENGS}
        self.sems = []
        self.autobar = {}

    def sem(self, name, step):
        s = Sem(name, step)
        self.sems.append(s)
        return s

    def op(self, eng, meth, *args, waits=(), inc=None, **kw):
        waits = tuple(waits)
        ab = self.autobar.get(eng)
        if ab is not None and meth not in (None, "nop"):
            abv = ab.n - (ab.step if inc is ab else 0)
            if abv > 0:
                waits = waits + ((ab, abv),)
        for (s, v) in waits:
            assert v <= s.n, f"wait on future {s.name}:{v}>{s.n}"
        self.ops[eng].append([waits, meth, args, kw, inc])

    def tag_last(self, eng, sem):
        row = self.ops[eng][-1]
        assert row[4] is None
        row[4] = sem.inc()
        return sem.tok()

    def replay(self, eng_obj, ename):
        for waits, meth, args, kw, inc in self.ops[ename]:
            for (s, v) in waits:
                eng_obj.wait_ge(s.h, v)
            if meth is None:
                continue
            ins = getattr(eng_obj, meth)(*args, **kw)
            if inc is not None:
                ins.then_inc(inc.h, inc.step)


def _hostpack(inputs, core):
    x = inputs["L0_attn_x"]
    f32 = np.float32
    b_, half = core // 2, core % 2
    t0 = half * NOWN

    xb = np.zeros((NKEY, D), f32)
    xb[HALO:] = x[b_, t0:t0 + NOWN]
    if half == 1:
        xb[:HALO] = x[b_, t0 - HALO:t0]
    im = {}
    im["xT"] = np.ascontiguousarray(xb.T)
    im["xres"] = np.ascontiguousarray(x[b_, t0:t0 + NOWN])
    im["wq"] = inputs["L0_attn_w_q"]
    im["wk"] = inputs["L0_attn_w_k"]
    im["wv"] = inputs["L0_attn_w_v"]
    im["wo"] = inputs["L0_attn_w_o"]
    im["sw1T"] = np.ascontiguousarray(inputs["L0_ffn_sw1"].T)
    im["sw2T"] = np.ascontiguousarray(inputs["L0_ffn_sw2"].T)
    im["sw3T"] = np.ascontiguousarray(inputs["L0_ffn_sw3"].T)
    im["egwT"] = np.ascontiguousarray(
        np.concatenate([inputs["L0_ffn_sg_w"], inputs["L0_ffn_eg_w"]], axis=0).T)
    im["ew1T"] = np.ascontiguousarray(inputs["L0_ffn_ew1"].transpose(0, 2, 1))
    im["ew2T"] = np.ascontiguousarray(inputs["L0_ffn_ew2"].transpose(0, 2, 1))
    im["ew3T"] = np.ascontiguousarray(inputs["L0_ffn_ew3"].transpose(0, 2, 1))

    br = {}
    cur = [0]
    parts = []

    def badd(name, arr):
        a = np.asarray(arr, f32).reshape(-1)
        br[name] = (cur[0], a.size)
        cur[0] += a.size
        parts.append(a)
    badd("bq", inputs["L0_attn_b_q"])
    badd("bk", inputs["L0_attn_b_k"])
    badd("bv", inputs["L0_attn_b_v"])
    badd("bo", inputs["L0_attn_b_o"])
    badd("sb1", inputs["L0_ffn_sb1"])
    badd("sb2", inputs["L0_ffn_sb2"])
    badd("sb3", inputs["L0_ffn_sb3"])
    badd("egb", np.concatenate([inputs["L0_ffn_sg_b"], inputs["L0_ffn_eg_b"]]))
    badd("eb1", inputs["L0_ffn_eb1"])
    badd("eb2", inputs["L0_ffn_eb2"])
    badd("eb3", inputs["L0_ffn_eb3"])
    im["brows"] = np.concatenate(parts).reshape(1, -1)

    p = np.arange(128)
    maskL = np.where(p[None, :] < p[:, None], -BIG, 0.0).astype(f32)
    maskR = np.where(p[None, :] > p[:, None], -BIG, 0.0).astype(f32)
    ident = np.eye(128, dtype=f32)
    bd2 = np.zeros((128, 2), f32)
    bd2[0:64, 0] = 1.0
    bd2[64:128, 1] = 1.0
    onescolv = np.ones((128, 1), f32)
    slnb_c = np.asarray(inputs["L0_ffn_s_ln_b"], f32).reshape(4, 128).T
    selnb_c = np.asarray(inputs["L0_ffn_sel_ln_b"], f32).reshape(4, 128).T
    cbf = np.concatenate([maskL, maskR, ident, bd2, onescolv,
                          slnb_c, selnb_c], axis=1)
    im["cbf_src"] = np.ascontiguousarray(cbf)

    gi = t0 + np.arange(NOWN)
    nwin = np.minimum(gi, WIN) + 1
    corr2 = (0.5 * (T - nwin) + 0.5 * T).astype(f32).reshape(NT, 128).T
    ib = np.asarray(inputs["L0_ffn_index_b"]).astype(np.int64)
    it = np.asarray(inputs["L0_ffn_index_t"]).astype(np.int64)
    oc = ib * 2 + (it // NOWN)
    fwd = np.full(NOWN, NSEL, np.int64)
    selm = np.zeros(NOWN, f32)
    for m in range(NSEL):
        if oc[m] == core:
            r = int(it[m] % NOWN)
            fwd[r] = m
            selm[r] = 1.0
    selmask = selm.reshape(NT, 128).T
    pairsel = np.zeros((128, 1), f32)
    pairsel[core, 0] = 1.0
    pairsel[core ^ 1, 0] = 1.0
    expbr = np.broadcast_to(
        np.asarray(inputs["L0_ffn_exp_bias"], f32)[None, :], (128, 8)).copy()
    slng_c = np.asarray(inputs["L0_ffn_s_ln_g"], f32).reshape(4, 128).T
    selng_c = np.asarray(inputs["L0_ffn_sel_ln_g"], f32).reshape(4, 128).T
    sb1c = np.asarray(inputs["L0_ffn_sb1"], f32).reshape(4, 128).T
    sb2c = np.asarray(inputs["L0_ffn_sb2"], f32).reshape(4, 128).T
    egbc = np.zeros((128, 1), f32)
    egbc[0:9, 0] = np.concatenate([np.asarray(inputs["L0_ffn_sg_b"], f32),
                                   np.asarray(inputs["L0_ffn_eg_b"], f32)])
    eb1c = np.asarray(inputs["L0_ffn_eb1"], f32).reshape(8, 4, 128)
    eb1c = eb1c.transpose(2, 0, 1).reshape(128, 32)
    eb2c = np.asarray(inputs["L0_ffn_eb2"], f32).reshape(8, 4, 128)
    eb2c = eb2c.transpose(2, 0, 1).reshape(128, 32)
    haloc = np.zeros((128, 4), f32)
    if half == 0:
        haloc[:] = -BIG
    cfm = np.concatenate([corr2, selmask, pairsel, expbr, slng_c, selng_c,
                          ident, sb1c, sb2c, egbc, eb1c, eb2c, haloc], axis=1)
    im["cf_src"] = np.ascontiguousarray(cfm)
    im["idxg"] = np.ascontiguousarray(np.concatenate(
        [fwd.reshape(NT, 128).T, fwd.reshape(NT, 128).T],
        axis=1).astype(np.int32))
    halocol = np.zeros((1, HALO), f32)
    if half == 0:
        halocol[:] = -BIG
    im["halocol"] = halocol
    bdsel = np.zeros((16, 1024), f32)
    for occ in range(8):
        for pp_ in range(128):
            bdsel[2 * occ + pp_ // 64, 128 * occ + pp_] = 1.0
    im["bdsel"] = bdsel
    bdsq = np.zeros((128, 8, 16), f32)
    for occ in range(8):
        for pp_ in range(128):
            bdsq[pp_, occ, 2 * occ + pp_ // 64] = 1.0
    im["bdsq"] = bdsq.reshape(128, 128)
    im["rmsrow"] = np.concatenate([
        np.asarray(inputs["L0_attn_rms_q_w"], f32),
        np.asarray(inputs["L0_attn_rms_k_w"], f32)]).reshape(1, 128)
    return im, br


CBF = {"maskL": 0, "maskR": 128, "identb": 256,
       "bd2": 384, "onescol": 386, "slnb": 387, "selnb": 391}
NCBF = 395
CF = {"corr2": 0, "selmask": 16, "pairsel": 32, "expbr": 33,
      "slng": 41, "selng": 45, "identf": 49,
      "sb1c": 177, "sb2c": 181, "egbc": 185, "eb1c": 186, "eb2c": 218,
      "haloc": 250}
NCF = 254


def build(br, debug=False, upto=9):
    nc = bass.Bass("TRN2", target_bir_lowering=False, debug=False)
    P = Prog()

    d_xT = nc.dram_tensor("xT", [D, NKEY], F32, kind="ExternalInput").ap()
    d_xres = nc.dram_tensor("xres", [NOWN, D], F32, kind="ExternalInput").ap()
    d_wq = nc.dram_tensor("wq", [D, D], F32, kind="ExternalInput").ap()
    d_wk = nc.dram_tensor("wk", [D, D], F32, kind="ExternalInput").ap()
    d_wv = nc.dram_tensor("wv", [D, 256], F32, kind="ExternalInput").ap()
    d_wo = nc.dram_tensor("wo", [D, D], F32, kind="ExternalInput").ap()
    d_sw1T = nc.dram_tensor("sw1T", [D, 512], F32, kind="ExternalInput").ap()
    d_sw2T = nc.dram_tensor("sw2T", [D, 512], F32, kind="ExternalInput").ap()
    d_sw3T = nc.dram_tensor("sw3T", [512, D], F32, kind="ExternalInput").ap()
    d_egwT = nc.dram_tensor("egwT", [D, 9], F32, kind="ExternalInput").ap()
    d_ew1T = nc.dram_tensor("ew1T", [8, D, 512], F32, kind="ExternalInput").ap()
    d_ew2T = nc.dram_tensor("ew2T", [8, D, 512], F32, kind="ExternalInput").ap()
    d_ew3T = nc.dram_tensor("ew3T", [8, 512, D], F32, kind="ExternalInput").ap()
    NB = sum(n for (_, n) in br.values())
    d_brows = nc.dram_tensor("brows", [1, NB], F32, kind="ExternalInput").ap()
    d_cbf = nc.dram_tensor("cbf_src", [128, NCBF], F32, kind="ExternalInput").ap()
    d_cf = nc.dram_tensor("cf_src", [128, NCF], F32, kind="ExternalInput").ap()
    d_idx = nc.dram_tensor("idxg", [128, 32], I32, kind="ExternalInput").ap()
    d_halocol = nc.dram_tensor("halocol", [1, HALO], F32, kind="ExternalInput").ap()
    d_rms = nc.dram_tensor("rmsrow", [1, 128], F32, kind="ExternalInput").ap()
    d_bdsel = nc.dram_tensor("bdsel", [16, 1024], F32, kind="ExternalInput").ap()
    d_bdsq = nc.dram_tensor("bdsq", [128, 128], F32, kind="ExternalInput").ap()
    d_y = nc.dram_tensor("y", [NOWN, D], F32, kind="ExternalOutput").ap()

    d_fwdx = nc.dram_tensor("fwd_x", [NSEL + 8, 1024], BF16).ap()
    d_fwds = nc.dram_tensor("fwd_s", [NSEL + 8, 16], BF16).ap()
    d_rsx = nc.dram_tensor("rs_x", [NBLK, 1024], BF16).ap()
    d_rss = nc.dram_tensor("rs_s", [NBLK, 16], BF16).ap()
    d_agin = nc.dram_tensor("ag_in", [NBLK, D], BF16).ap()
    d_agout = nc.dram_tensor("ag_out", [NSEL + 1, D], BF16).ap()
    d_denin = nc.dram_tensor("den_in", [1, 16], F32).ap()
    d_denout = nc.dram_tensor("den_out", [8, 16], F32).ap()
    dbg = {}
    if debug:
        for nm, shp, dt in [
                ("dbg_qhat", [128, 8 * NOWN], BF16), ("dbg_khat", [128, 8 * NKEY], BF16),
                ("dbg_vhat", [128, NKT * 260], BF16), ("dbg_sigs", [128, NT * 16], F32),
                ("dbg_recden", [128, NT * 16], F32), ("dbg_c2", [16, NOWN], BF16),
                ("dbg_ahat", [128, 8 * NOWN], BF16), ("dbg_strm", [128, NT * D], F32),
                ("dbg_wsm", [128, NT], F32), ("dbg_zhat", [128, 8 * NOWN], BF16),
                ("dbg_sheb", [128, 4 * NOWN], BF16), ("dbg_eacc", [128, 2 * D], F32),
                ("dbg_s2", [128, NT * D], F32)]:
            dbg[nm] = nc.dram_tensor(nm, shp, dt, kind="ExternalOutput").ap()

    SB0 = 16512
    TOP = 229344
    R1 = SB0
    R2 = R1 + 40960
    R3 = R2 + 73728
    R4 = R3 + 10432
    R5 = R4 + 32768
    R6 = R5 + 16384
    SM = R6 + 26624
    A = {}

    def mk(name, shape, dtype, at):
        A[name] = nc.alloc_sbuf_tensor_at(name, list(shape), dtype, offset=at)
        return A[name]

    xhat = mk("xhat", [128, 8, NKEY], BF16, R1)
    ahat = mk("ahat", [128, 8, NOWN], BF16, R1)
    zeb = mk("zeb", [128, 4, NOWN], BF16, R1)
    sheb = mk("sheb", [128, 4, NOWN], BF16, R1 + 16384)

    qhat = mk("qhat", [128, 8, NOWN], BF16, R2)
    khat = mk("khat", [128, 8, NKEY], BF16, R2 + 32768)
    strm = mk("strm", [128, NT, D], F32, R2)
    xring = mk("xring", [128, 2, D], F32, R2 + 65536)

    vhat = mk("vhat", [128, NKT, 260], BF16, R3)
    xselfm = mk("xselfm", [128, 8, NBLK], BF16, R3)
    zebm = mk("zebm", [128, 4, NBLK], BF16, R3 + 4096)
    shebm = mk("shebm", [128, 4, NBLK], BF16, R3 + 6144)
    zebm2 = mk("zebm2", [128, 4, NBLK], BF16, R3 + 8192)

    sumsqK = mk("sumsqK", [16, NKEY], F32, R4)
    invkb = mk("invkb", [16, NKEY], BF16, R4 + 10240)
    sumsqQ = mk("sumsqQ", [16, NOWN], F32, R4 + 15360)
    sigP = mk("sigP", [128, 2, 640], BF16, R4 + 23552)
    c2fm = mk("c2fm", [16, NOWN], BF16, R4 + 26112)
    zhatfm = mk("zhatfm", [128, 8, NOWN], BF16, R4)
    ewS = [
        (mk("ew1S0", [128, 8, 512], BF16, R4),
         mk("ew2S0", [128, 8, 512], BF16, R4 + 8192),
         mk("ew3S0", [128, 4, 1024], BF16, R2 + 65536)),
        (mk("ew1S1", [128, 8, 512], BF16, R4 + 16384),
         mk("ew2S1", [128, 8, 512], BF16, R4 + 24576),
         mk("ew3S1", [128, 4, 1024], BF16, R1 + 32768)),
    ]

    wsb = mk("wsb", [128, 8, 1024], BF16, R5)
    sw1sb = mk("sw1sb", [128, 8, 512], BF16, R5)
    sw2sb = mk("sw2sb", [128, 8, 512], BF16, R5 + 8192)
    sw3sb = mk("sw3sb", [128, 4, 1024], BF16, R5)
    wvv = wsb[:].rearrange("p a o -> p (a o)")[:, 0:2048].rearrange(
        "p (dc o) -> p dc o", o=256)

    repscr = mk("repscr", [64, 512], F32, R6)
    sigs3 = mk("sigs3", [128, NT, 16], F32, R6 + 2048)
    recden = mk("recden", [128, NT, 16], F32, R6 + 3072)
    invq8 = mk("invq8", [128, NT, 16], F32, R6 + 4096)
    gring = mk("gring", [9, 2, 512], F32, R6)
    z1ring = mk("z1ring", [128, 2, 1040], BF16, R6 + 4160)
    xsel = mk("xsel", [128, 2, 1024], BF16, R6)
    xsels = mk("xsels", [128, 2, 16], BF16, R6 + 4128)
    indf = mk("indf", [128, 2, 8], F32, R6 + 4192)
    weff = mk("weff", [128, 2, 1], F32, R6 + 4256)
    frows = mk("frows", [1, 3, 512], F32, R6 + 8192)
    eacc = mk("eacc", [128, 2, D], F32, R5)
    otile = mk("otile", [128, 2, D], F32, R5)
    ssping = mk("ssping", [128, 2, 512], BF16, R6 + 16384)
    sqdummy = mk("sqdummy", [128, 1024], BF16, R6 + 16384)
    zz = mk("zz", [128, 1040], BF16, R6 + 16384)
    sqscr = mk("sqscr", [128, 2, 512], BF16, R6 + 18496)
    bias3e = mk("bias3e", [1, 1024], BF16, R6 + 20544)
    obounce = mk("obounce", [64, 2, 512], BF16, R6 + 20544)
    invrowb = mk("invrowb", [1, 512], BF16, R6 + 22592)
    minrowb = mk("minrowb", [1, 512], BF16, R6 + 23616)
    rscr = mk("rscr", [1, 2, NOWN], BF16, R4)
    rrall = mk("rrall", [16, NOWN], BF16, R4 + 8192)
    rrallb = mk("rrallb", [16, NOWN], BF16, R4 + 12288)
    rrinv = mk("rrinv", [16, NOWN], F32, R4)
    ttscr = mk("ttscr", [128, 512], BF16, R6 + 24640)
    gstt = mk("gstt", [128, 2, 12], F32, R6 + 25664)
    ln2c = mk("ln2c", [128, 6, 16], F32, R6 + 25760)
    retbuf = mk("retbuf", [128, 2, D], BF16, R6 + 4160)

    cur = [SM]

    def sbal(name, shape, dtype):
        at = cur[0]
        nb = int(np.prod(shape[1:])) * mybir.dt.size(dtype)
        cur[0] = (at + nb + 31) & ~31
        return mk(name, shape, dtype, at)

    cbf = sbal("cbf", [128, NCBF], BF16)
    cf = sbal("cf", [128, NCF], F32)
    idxs = sbal("idxs", [128, 32], I32)
    browsb = sbal("browsb", [1, 1024], BF16)
    brows2 = sbal("brows2", [1, 1024], BF16)
    onesrow = sbal("onesrow", [1, 640], BF16)
    halocolb = sbal("halocolb", [1, HALO], BF16)
    bdsel = sbal("bdsel", [16, 8, 128], BF16)
    wqwkrow2 = sbal("wqwkrow2", [1, 128], F32)
    wqwkcol = sbal("wqwkcol", [128, 1], F32)
    rmsq = sbal("rmsq", [1, 64], F32)
    rmsk = sbal("rmsk", [1, 64], F32)
    wqwk = sbal("wqwk", [1, 64], F32)
    pk = [R6 + 14336]

    def pkmk(name, shape, dtype):
        at = pk[0]
        nb = int(np.prod(shape[1:])) * mybir.dt.size(dtype)
        pk[0] = (at + nb + 31) & ~31
        assert pk[0] <= R6 + 16384, name
        return mk(name, shape, dtype, at)

    wsmcol = pkmk("wsmcol", [128, NT], F32)
    scr8 = pkmk("scr8", [128, 8], F32)
    scrsm = pkmk("scrsm", [128, 8], F32)
    mxcol = pkmk("mxcol", [128, 1], F32)
    t1col = pkmk("t1col", [128, 1], F32)
    wst = pkmk("wst", [128, 2], F32)
    dcol = pkmk("dcol", [128, 2], F32)
    sidescr = pkmk("sidescr", [128, 2, 16], BF16)
    densb = pkmk("densb", [8, 16], F32)
    denrow = pkmk("denrow", [1, 16], F32)
    sminv = pkmk("sminv", [16, 1], F32)
    rsumrep = pkmk("rsumrep", [128, 16], F32)
    c2scr = pkmk("c2scr", [128, 16], F32)
    lncol = pkmk("lncol", [128, 8], F32)
    onescf = pkmk("onescf", [128, 1], F32)
    cpool = pkmk("cpool", [128, 5], F32)
    bdsq = pkmk("bdsq", [128, 8, 16], BF16)
    rsrep1 = pkmk("rsrep1", [16, 128], F32)
    egwsb = pkmk("egwsb", [128, 8, 9], BF16)
    assert cur[0] <= TOP, f"SBUF overflow {cur[0]} > {TOP}"

    psum_all = nc.alloc_psum_tensor("psum_all", [128, 4096], F32)
    assert nc.psum_banks_remaining == 0
    PSB = [psum_all[:, 512 * i:512 * i + 512] for i in range(8)]
    psA = psum_all[:, 0:640]
    psB = psum_all[:, 1024:1664]
    psG = psum_all[:, 2048:2688]
    psC = PSB[0]
    psD = PSB[2]
    psE = PSB[4]
    psF = PSB[6]
    psG0 = PSB[1]
    SRING = [psA, psB, psG]

    sy = P.sem("s_sy", 16)
    sg = P.sem("s_sg", 16)
    ssd = P.sem("s_ssd", 16)
    syd = P.sem("s_syd", 1)
    sgd = P.sem("s_sgd", 1)
    ssdd = P.sem("s_ssdd", 1)

    def ybar():
        P.op("sync", "nop", waits=((sy, sy.n),), inc=syd.inc())
        return syd.tok()

    def gbar():
        P.op("gpsimd", "nop", waits=((sg, sg.n),), inc=sgd.inc())
        return sgd.tok()

    def sbar():
        P.op("scalar", "nop", waits=((ssd, ssd.n),), inc=ssdd.inc())
        return ssdd.tok()

    mm = P.sem("s_mm", 1)
    sc = P.sem("s_sc", 1)
    vc = P.sem("s_vc", 1)
    cc = P.sem("s_cc", 1)
    P.autobar = {"vector": vc, "scalar": sc}

    MASKL = cbf[:, CBF["maskL"]:CBF["maskL"] + 128]
    MASKR = cbf[:, CBF["maskR"]:CBF["maskR"] + 128]
    IDENTB = cbf[:, CBF["identb"]:CBF["identb"] + 128]
    BD2 = cbf[:, CBF["bd2"]:CBF["bd2"] + 2]
    ONESCB = cbf[:, CBF["onescol"]:CBF["onescol"] + 1]
    IDENTF = cf[:, CF["identf"]:CF["identf"] + 128]

    def cfc(name, o, n):
        return cf[:, CF[name] + o:CF[name] + o + n]

    def cbfc(name, o, n):
        return cbf[:, CBF[name] + o:CBF[name] + o + n]

    P.op("gpsimd", "dma_start", out=cbf[:], in_=d_cbf[:, :], inc=sg.inc())
    P.op("sync", "dma_start", out=cf[:], in_=d_cf[:, :], inc=sy.inc())
    P.op("sync", "dma_start", out=idxs[:], in_=d_idx[:, :], inc=sy.inc())
    P.op("sync", "dma_start", out=rmsq[:], in_=d_rms[0:1, 0:64], inc=sy.inc())
    P.op("sync", "dma_start", out=rmsk[:], in_=d_rms[0:1, 64:128], inc=sy.inc())
    P.op("gpsimd", "dma_start", out=halocolb[:], in_=d_halocol[:, :], inc=sg.inc())
    obq, nbq = br["bq"]
    P.op("gpsimd", "dma_start", out=browsb[0:1, 0:1024],
         in_=d_brows[0:1, obq:obq + nbq], inc=sg.inc())
    obk, nbk = br["bk"]
    P.op("gpsimd", "dma_start", out=brows2[0:1, 0:1024],
         in_=d_brows[0:1, obk:obk + nbk], inc=sg.inc())
    P.op("gpsimd", "dma_start", out=egwsb[:],
         in_=d_egwT[:, :].rearrange("(dc p) e -> p dc e", p=128), inc=sg.inc())
    P.op("gpsimd", "dma_start", out=bdsel[:],
         in_=d_bdsel[:, :].rearrange("k (oc p) -> k oc p", p=128), inc=sg.inc())
    P.op("gpsimd", "dma_start", out=bdsq[:],
         in_=d_bdsq[:, :].rearrange("p (oc j) -> p oc j", j=16), inc=sg.inc())
    xTfm = d_xT[:, :].rearrange("(dc p) t -> p dc t", p=128)
    P.op("gpsimd", "dma_start", out=wsb[:],
         in_=d_wq[:, :].rearrange("(dc p) o -> p dc o", p=128), inc=sg.inc())
    xtok = []
    for (xa, xb) in [(512, 1024), (1024, 1536), (1536, 2048), (2048, 2560),
                     (0, 512)]:
        P.op("gpsimd", "dma_start", out=xhat[:, :, xa:xb], in_=xTfm[:, :, xa:xb],
             inc=sg.inc())
        xtok.append(gbar())
    g_loads = xtok[3]
    y_loads = ybar()

    P.op("vector", "memset", onesrow[:], 1.0, inc=vc.inc())
    P.op("vector", "memset", onescf[:], 1.0, inc=vc.inc())
    P.op("vector", "memset", zz[:], 0.0, inc=vc.inc())
    P.op("vector", "memset", sidescr[:], 0.0, inc=vc.inc())
    P.op("vector", "memset",
         vhat[:].rearrange("p t (kv c) -> p t kv c", c=65)[:, :, :, 64:65],
         1.0, inc=vc.inc())
    P.op("vector", "memset", cpool[:, 0:1], RMS_EPS, inc=vc.inc())
    P.op("vector", "memset", cpool[:, 1:2], LN_EPS, inc=vc.inc())
    P.op("vector", "memset", cpool[:, 2:3], 1.0 / 64.0, inc=vc.inc())
    P.op("vector", "memset", cpool[:, 3:4], 0.125, inc=vc.inc())
    P.op("vector", "memset", cpool[:, 4:5], 1.0 / 512.0, inc=vc.inc())
    v_cpool = vc.tok()
    v_init = vc.tok()
    for i in range(16):
        P.op("gpsimd", "dma_start", out=d_fwdx[128 * i:128 * i + 128, :],
             in_=zz[:, 0:1024], waits=(v_init,) if i == 0 else (), inc=sg.inc())
    P.op("gpsimd", "dma_start", out=d_fwdx[2048:2056, :], in_=zz[0:8, 0:1024],
         inc=sg.inc())
    P.op("gpsimd", "dma_start",
         out=d_fwds[0:2048, :].rearrange("(a p) c -> p a c", p=128),
         in_=zz[:, 0:256].rearrange("p (a c) -> p a c", c=16), inc=sg.inc())
    P.op("gpsimd", "dma_start", out=d_fwds[2048:2056, :], in_=zz[0:8, 0:16],
         inc=sg.inc())
    P.op("gpsimd", "dma_start", out=d_agout[NSEL:NSEL + 1, :], in_=zz[0:1, 0:1024],
         inc=sg.inc())
    g_zf = gbar()

    P.op("vector", "tensor_mul", wqwk[:], rmsq[:], rmsk[:],
         waits=(y_loads,), inc=vc.inc())
    v_ww = vc.tok()
    P.op("gpsimd", "dma_start", out=wqwkrow2[0:1, 0:64], in_=wqwk[:],
         waits=(v_ww,), inc=sg.inc())
    P.op("gpsimd", "dma_start", out=wqwkrow2[0:1, 64:128], in_=wqwk[:],
         inc=sg.inc())
    g_ww = gbar()
    g_bdw = gbar()

    dbg_bar1 = dbg_bar2 = dbg_bar3 = dbg_bar4 = dbg_bar5 = None

    def finish_early():
        P.op("vector", "memset", otile[:, 0, :], 0.0, inc=vc.inc())
        vv = vc.tok()
        for t_ in range(NT):
            P.op("sync", "dma_start", out=d_y[128 * t_:128 * t_ + 128, :],
                 in_=otile[:, 0, :], waits=(vv,) if t_ == 0 else (),
                 inc=sy.inc())
        ybar()
        return _emit(nc, P)

    if upto <= 0:
        return finish_early()

    def proj_fm(wget, biasrow, tok_cols, out_write, sumsq_write, prewaits,
                tc_waits=None):
        last = {}
        pe_ev = {}
        it = 0
        for tci, (tc0, tn) in enumerate(tok_cols):
            pe = psE if tci % 2 == 0 else psF
            for oc in range(8):
                pp = psC if it % 2 == 0 else psD
                sq_ = sqscr[:, it % 2, 0:tn]
                tw = list(prewaits) if it < 2 else list(last[it - 2])
                if tc_waits and oc == 0 and tci in tc_waits:
                    tw = tw + [tc_waits[tci]]
                for dc in range(8):
                    P.op("tensor", "matmul", pp[:, 0:tn],
                         wget(dc, oc), xhat[:, dc, tc0:tc0 + tn],
                         start=(dc == 0), stop=False,
                         waits=tw if dc == 0 else ())
                P.op("tensor", "matmul", pp[:, 0:tn],
                     biasrow(oc), onesrow[0:1, 0:tn],
                     start=False, stop=True, inc=mm.inc())
                t_mm = mm.tok()
                P.op("scalar", "activation", out=sq_, in_=pp[:, 0:tn],
                     func=AF.Square, waits=(t_mm,), inc=sc.inc())
                s_sq = sc.tok()
                P.op("scalar", "copy", out=out_write(oc, tc0, tn),
                     in_=pp[:, 0:tn], inc=sc.inc())
                v_ev = sc.tok()
                ssw = [s_sq]
                if oc == 0 and tci >= 2:
                    ssw.append(pe_ev[tci - 2])
                P.op("tensor", "matmul", pe[0:16, 0:tn], bdsq[:, oc, :], sq_,
                     start=(oc == 0), stop=(oc == 7), waits=ssw,
                     inc=mm.inc() if oc == 7 else None)
                last[it] = (s_sq, v_ev)
                it += 1
            t_ss = mm.tok()
            P.op("scalar", "copy", out=sumsq_write(tc0, tn),
                 in_=pe[0:16, 0:tn], waits=(t_ss,), inc=sc.inc())
            pe_ev[tci] = sc.tok()
        return (sc.tok(), vc.tok())

    own_cols = [(HALO + 512 * t, 512) for t in range(4)]
    all_cols = [(512 * t, 512) for t in range(5)]

    qtail = proj_fm(
        lambda dc, oc: wsb[:, dc, 128 * oc:128 * oc + 128],
        lambda oc: browsb[0:1, 128 * oc:128 * oc + 128],
        own_cols,
        lambda oc, tc0, tn: qhat[:, oc, tc0 - HALO:tc0 - HALO + tn],
        lambda tc0, tn: sumsqQ[0:16, tc0 - HALO:tc0 - HALO + tn],
        [v_init, xtok[0]],
        tc_waits={1: xtok[1], 2: xtok[2], 3: xtok[3]})
    if upto <= 1.1:
        return finish_early()
    P.op("tensor", "transpose", psG0[0:128, 0:1], wqwkrow2[:, :],
         IDENTF[0:1, 0:1], waits=(g_ww,), inc=mm.inc())
    t_ww = mm.tok()
    P.op("scalar", "copy", out=wqwkcol[:], in_=psG0[0:128, 0:1], waits=(t_ww,),
         inc=sc.inc())
    s_wwc = sc.tok()
    q_mm_done = mm.tok()
    P.op("gpsimd", "dma_start", out=wsb[:],
         in_=d_wk[:, :].rearrange("(dc p) o -> p dc o", p=128),
         waits=(q_mm_done,), inc=sg.inc())
    g_wk = gbar()
    ktail = proj_fm(
        lambda dc, oc: wsb[:, dc, 128 * oc:128 * oc + 128],
        lambda oc: brows2[0:1, 128 * oc:128 * oc + 128],
        all_cols,
        lambda oc, tc0, tn: khat[:, oc, tc0:tc0 + tn],
        lambda tc0, tn: sumsqK[0:16, tc0:tc0 + tn],
        [g_wk, qtail[0], qtail[1]])
    if upto <= 1.2:
        return finish_early()
    k_mm_done = mm.tok()
    P.op("gpsimd", "dma_start", out=wvv,
         in_=d_wv[:, :].rearrange("(dc p) o -> p dc o", p=128),
         waits=(k_mm_done,), inc=sg.inc())
    obv, nbv = br["bv"]
    P.op("gpsimd", "dma_start", out=browsb[0:1, 0:256],
         in_=d_brows[0:1, obv:obv + nbv], inc=sg.inc())
    g_wv = gbar()

    if upto <= 1.3:
        return finish_early()
    q_sq_all = qtail[0]
    P.op("scalar", "activation", out=sumsqQ[:, :], in_=sumsqQ[:, :], func=AF.Sqrt,
         scale=cpool[0:16, 2:3], bias=cpool[0:16, 0:1],
         waits=(v_cpool, sc.tok()), inc=sc.inc())
    P.op("scalar", "activation", out=sumsqK[:, :], in_=sumsqK[:, :], func=AF.Sqrt,
         scale=cpool[0:16, 2:3], bias=cpool[0:16, 0:1], inc=sc.inc())
    s_sqrt = sc.tok()
    P.op("vector", "reciprocal", out=sumsqQ[:, :], in_=sumsqQ[:, :],
         waits=(s_sqrt,), inc=vc.inc())
    P.op("vector", "reciprocal", out=sumsqK[:, :], in_=sumsqK[:, :], inc=vc.inc())
    P.op("vector", "tensor_copy", out=invkb[:, :], in_=sumsqK[:, :],
         waits=(vc.tok(),), inc=vc.inc())
    v_inv = vc.tok()
    invq8_ev = {}
    for t in range(NT):
        pe = psE if t % 2 == 0 else psF
        tw = [v_inv] if t < 2 else [invq8_ev[t - 2]]
        P.op("tensor", "transpose", pe[0:128, 0:16],
             sumsqQ[0:16, 128 * t:128 * t + 128], IDENTF[0:16, 0:16],
             waits=tw, inc=mm.inc())
        t_tr = mm.tok()
        P.op("scalar", "activation", out=invq8[:, t, :], in_=pe[0:128, 0:16],
             func=AF.Copy, scale=cpool[:, 3:4], waits=(t_tr, v_cpool),
             inc=sc.inc())
        invq8_ev[t] = sc.tok()
    s_invq8 = sc.tok()

    if upto <= 1.4:
        return finish_early()
    klast = {}
    it = 0
    for oc in range(8):
        for tc in range(5):
            pp = psC if it % 2 == 0 else psD
            tw = [v_inv, g_bdw] if it < 2 else [klast[it - 2]]
            P.op("tensor", "matmul", pp[:, :], bdsel[:, oc, :],
                 invkb[:, 512 * tc:512 * tc + 512],
                 start=True, stop=True, waits=tw, inc=mm.inc())
            t_rep = mm.tok()
            P.op("vector", "scalar_tensor_tensor",
                 out=khat[:, oc, 512 * tc:512 * tc + 512],
                 in0=khat[:, oc, 512 * tc:512 * tc + 512], scalar=wqwkcol[:],
                 in1=pp[:, :], op0=ALU.mult, op1=ALU.mult,
                 waits=(t_rep, s_wwc), inc=vc.inc())
            klast[it] = vc.tok()
            it += 1
    v_khat = vc.tok()

    if debug:
        P.op("sync", "dma_start", out=dbg["dbg_qhat"][:, :],
             in_=qhat[:].rearrange("p a t -> p (a t)"), waits=(v_done,), inc=sy.inc())
        P.op("sync", "dma_start", out=dbg["dbg_khat"][:, :],
             in_=khat[:].rearrange("p a t -> p (a t)"), waits=(v_khat,), inc=sy.inc())
        P.op("sync", "dma_start", out=dbg["dbg_vhat"][:, :],
             in_=vhat[:].rearrange("p a t -> p (a t)"), waits=(v_done,), inc=sy.inc())
    dbg_bar1 = ybar() if debug else None

    if upto <= 2:
        return finish_early()

    slot_last = [None, None, None]
    rd_tok = {}
    r = 0
    it = 0
    for qt in range(NT):
        k0 = 128 * qt
        lo0 = max(0, 512 - 128 * qt)
        for hp in range(8):
            sS = [SRING[r], SRING[(r + 1) % 3]]
            t_mm = [None, None]
            tws = []
            for e in (0, 1):
                tw = list([v_khat, s_invq8] if it == 0 else [])
                if slot_last[(r + e) % 3] is not None:
                    tw.append(slot_last[(r + e) % 3])
                tws.append(tw)
            if lo0 < 512:
                for e in (0, 1):
                    hb = 64 * e
                    P.op("tensor", "matmul", sS[e][:, lo0:512],
                         qhat[hb:hb + 64, hp, 128 * qt:128 * qt + 128],
                         khat[hb:hb + 64, hp, k0 + lo0:k0 + 512],
                         start=True, stop=(qt < 4), waits=tws[e])
                    tws[e] = []
            for e in (0, 1):
                hb = 64 * e
                P.op("tensor", "matmul", sS[e][:, 512:640],
                     qhat[hb:hb + 64, hp, 128 * qt:128 * qt + 128],
                     khat[hb:hb + 64, hp, k0 + 512:k0 + 640],
                     start=True, stop=False, waits=tws[e])
            for e in (0, 1):
                sA = sS[e]
                if qt >= 4:
                    P.op("tensor", "matmul", sA[:, 0:128], IDENTB, MASKL,
                         start=False, stop=True)
                P.op("tensor", "matmul", sA[:, 512:640], IDENTB, MASKR,
                     start=False, stop=True, inc=mm.inc())
                t_mm[e] = mm.tok()
            for e in (0, 1):
                h = 2 * hp + e
                P.op("scalar", "activation", out=sigP[:, e, 0:640 - lo0],
                     in_=sS[e][:, lo0:640],
                     func=AF.Sigmoid, scale=invq8[:, qt, h:h + 1],
                     accum_out=sigs3[:, qt, h:h + 1],
                     waits=(t_mm[e], s_invq8) if it == 0 else (t_mm[e],),
                     inc=sc.inc())
                slot_last[(r + e) % 3] = sc.tok()
            if hp == 7:
                last_sig = sc.tok()
                P.op("vector", "tensor_scalar", out=recden[:, qt, :],
                     in0=sigs3[:, qt, :], scalar1=cfc("corr2", qt, 1),
                     scalar2=1.0 / T, op0=ALU.add, op1=ALU.mult,
                     waits=(last_sig,), inc=vc.inc())
                P.op("vector", "reciprocal", out=recden[:, qt, :],
                     in_=recden[:, qt, :], inc=vc.inc())
                rd_tok[qt] = vc.tok()
            if hp == 3 and qt >= 1:
                P.op("tensor", "matmul", PSB[7][0:1, 0:16], onescf[:],
                     recden[:, qt - 1, :], start=(qt == 1), stop=False,
                     waits=(rd_tok[qt - 1],))
            r = (r + 2) % 3
            it += 1
    P.op("tensor", "matmul", PSB[7][0:1, 0:16], onescf[:],
         recden[:, NT - 1, :], start=False, stop=True,
         waits=(rd_tok[NT - 1],), inc=mm.inc())
    t_part = mm.tok()
    s_pass1 = sc.tok()
    if debug:
        P.op("sync", "dma_start", out=dbg["dbg_sigs"][:, :],
             in_=sigs3[:].rearrange("p a t -> p (a t)"), waits=(s_pass1,), inc=sy.inc())

    vlast = {}
    for t in range(NKT):
        pp = psC if t % 2 == 0 else psD
        tw = [g_wv, s_pass1] if t < 2 else [vlast[t - 2]]
        for dc in range(8):
            P.op("tensor", "matmul", pp[:, 0:256],
                 xhat[:, dc, 128 * t:128 * t + 128], wvv[:, dc, :],
                 start=(dc == 0), stop=False, waits=tw if dc == 0 else ())
        P.op("tensor", "matmul", pp[:, 0:256], onesrow[0:1, 0:128],
             browsb[0:1, 0:256], start=False, stop=True, inc=mm.inc())
        t_mm = mm.tok()
        P.op("vector", "tensor_copy",
             out=vhat[:, t, :].rearrange("p (kv c) -> p kv c", c=65)[:, :, 0:64],
             in_=pp[:, 0:256].rearrange("p (kv c) -> p kv c", c=64),
             waits=(t_mm,), inc=vc.inc())
        vlast[t] = vc.tok()
    v_done = vc.tok()
    v_mm_done = mm.tok()
    P.op("gpsimd", "dma_start", out=wsb[:],
         in_=d_wo[:, :].rearrange("(dc p) o -> p dc o", p=128),
         waits=(v_mm_done,), inc=sg.inc())
    obo, nbo = br["bo"]
    P.op("gpsimd", "dma_start", out=brows2[0:1, 0:1024],
         in_=d_brows[0:1, obo:obo + nbo], inc=sg.inc())
    g_wo = gbar()

    P.op("scalar", "copy", out=denrow[:], in_=PSB[7][0:1, 0:16], waits=(t_part,),
         inc=sc.inc())
    s_den = sc.tok()
    P.op("sync", "dma_start", out=d_denin[:, :], in_=denrow[:], waits=(s_den,),
         inc=sy.inc())
    g_den = ybar()
    P.op("gpsimd", "collective_compute", "AllGather", ALU.bypass,
         replica_groups=[list(range(8))], ins=[d_denin.opt()],
         outs=[d_denout.opt()], waits=(g_den,), inc=cc.inc())
    c_den = cc.tok()
    P.op("sync", "dma_start", out=densb[:], in_=d_denout[:, :], waits=(c_den,),
         inc=sy.inc())
    g_dsb = ybar()
    P.op("tensor", "matmul", psF[0:16, 0:1], densb[:, :],
         cfc("pairsel", 0, 1)[0:8, :], start=True, stop=True, waits=(g_dsb, s_den),
         inc=mm.inc())
    t_sminv = mm.tok()
    P.op("vector", "reciprocal", out=sminv[:], in_=psF[0:16, 0:1],
         waits=(t_sminv,), inc=vc.inc())
    v_r1 = vc.tok()
    P.op("vector", "tensor_copy", out=rsrep1[:, :],
         in_=sminv[:].broadcast_to([16, 128]), waits=(v_r1,), inc=vc.inc())
    v_rs1 = vc.tok()
    P.op("tensor", "transpose", psF[0:128, 0:16], rsrep1[:, :],
         IDENTF[0:16, 0:16], waits=(v_rs1,), inc=mm.inc())
    t_rs = mm.tok()
    P.op("scalar", "copy", out=rsumrep[:], in_=psF[0:128, 0:16], waits=(t_rs,),
         inc=sc.inc())
    s_rs = sc.tok()
    P.op("vector", "tensor_tensor",
         out=recden[:].rearrange("p a t -> p (a t)"),
         in0=recden[:].rearrange("p a t -> p (a t)"),
         in1=invq8[:].rearrange("p a t -> p (a t)"), op=ALU.mult,
         waits=(s_invq8,), inc=vc.inc())
    P.op("vector", "tensor_tensor", out=recden[:, :, :],
         in0=recden[:, :, :],
         in1=rsumrep[:].rearrange("p (a t) -> p a t", a=1).broadcast_to(
             [128, NT, 16]),
         op=ALU.mult, waits=(s_rs,), inc=vc.inc())
    v_c2all = vc.tok()
    c2ev = {}
    for qt in range(NT):
        pe = psE if qt % 2 == 0 else psF
        P.op("tensor", "transpose", pe[0:16, 0:128], recden[:, qt, :], IDENTF,
             waits=((v_c2all,) if qt < 2 else (c2ev[qt - 2],)), inc=mm.inc())
        t_c2 = mm.tok()
        P.op("scalar", "copy", out=c2fm[0:16, 128 * qt:128 * qt + 128],
             in_=pe[0:16, 0:128], waits=(t_c2,), inc=sc.inc())
        c2ev[qt] = sc.tok()
    s_c2 = sc.tok()
    if debug:
        P.op("sync", "dma_start", out=dbg["dbg_recden"][:, :],
             in_=recden[:].rearrange("p a t -> p (a t)"), waits=(v_rec,), inc=sy.inc())
        P.op("sync", "dma_start", out=dbg["dbg_c2"][:, :], in_=c2fm[:, :],
             waits=(s_c2,), inc=sy.inc())

    q2last = {}
    it = 0
    for oc in range(8):
        for tc in range(4):
            pp = psC if it % 2 == 0 else psD
            tw = ([s_c2, vlast[NKT - 2], vlast[NKT - 1]]
                  if it < 2 else [q2last[it - 2]])
            if debug and it == 0:
                tw.append(dbg_bar1)
            P.op("tensor", "matmul", pp[:, :], bdsel[:, oc, :],
                 c2fm[:, 512 * tc:512 * tc + 512],
                 start=True, stop=True, waits=tw, inc=mm.inc())
            t_rep = mm.tok()
            P.op("vector", "tensor_tensor",
                 out=qhat[:, oc, 512 * tc:512 * tc + 512],
                 in0=qhat[:, oc, 512 * tc:512 * tc + 512], in1=pp[:, :],
                 op=ALU.mult, waits=(t_rep,), inc=vc.inc())
            q2last[it] = vc.tok()
            it += 1
    v_q2 = vc.tok()

    if upto <= 3:
        return finish_early()

    spans = {}
    kt_of_qc = {qc: [] for qc in range(4)}
    for kt in range(NKT):
        qlo = max(0, 128 * kt - 512)
        qhi = min(NOWN, 128 * kt + 128)
        spans[kt] = (qlo, qhi)
        for qc in range(4):
            if qlo < 512 * qc + 512 and qhi > 512 * qc:
                kt_of_qc[qc].append(kt)
    first_kt = {qc: ks[0] for qc, ks in kt_of_qc.items()}
    last_kt = {qc: ks[-1] for qc, ks in kt_of_qc.items()}

    ZROW = cbf[0:1, CBF["maskL"]:CBF["maskL"] + 128]
    SS = {0: psum_all[:, 0:640], 1: psum_all[:, 3072:3712]}
    OBK = {(0, 0): PSB[2], (0, 1): PSB[3], (1, 0): PSB[4], (1, 1): PSB[5]}
    rs_dma = {0: None, 1: None}
    ob_dma = {0: None, 1: None}
    explast = {}
    ev_tok = {}
    for hp in range(8):
        kv = hp // 2
        for kt in range(NKT):
            qlo, qhi = spans[kt]
            w = qhi - qlo
            n0 = min(w, 512)
            for eo in (0, 1):
                h = 2 * hp + eo
                hb = 64 * eo
                if hp == 0 and kt == 0:
                    tw = [v_q2, v_done]
                elif kt == 0:
                    tw = [explast[h - 2]]
                else:
                    tw = [explast[h]]
                P.op("tensor", "matmul", SS[eo][:, 0:n0],
                     khat[hb:hb + 64, hp, 128 * kt:128 * kt + 128],
                     qhat[hb:hb + 64, hp, qlo:qlo + n0],
                     start=True, stop=False, waits=tw)
            if w > 512:
                for eo in (0, 1):
                    hb = 64 * eo
                    P.op("tensor", "matmul", SS[eo][:, 512:w],
                         khat[hb:hb + 64, hp, 128 * kt:128 * kt + 128],
                         qhat[hb:hb + 64, hp, qlo + 512:qlo + w],
                         start=True, stop=False)
            t_mm = {}
            for eo in (0, 1):
                sA = SS[eo]
                b0open = True
                mr_in_b0 = (kt <= 15) and (w <= 512)
                ml = (kt >= 4)
                if ml and not mr_in_b0:
                    P.op("tensor", "matmul", sA[:, 0:128], IDENTB, MASKL,
                         start=False, stop=True)
                    b0open = False
                elif ml and mr_in_b0:
                    P.op("tensor", "matmul", sA[:, 0:128], IDENTB, MASKL,
                         start=False, stop=False)
                if mr_in_b0:
                    P.op("tensor", "matmul", sA[:, w - 128:w], IDENTB, MASKR,
                         start=False, stop=True)
                    b0open = False
                if b0open:
                    P.op("tensor", "matmul", sA[:, 0:n0],
                         halocolb[0:1, 0:128], onesrow[0:1, 0:n0],
                         start=False, stop=True)
                if w > 512:
                    P.op("tensor", "matmul", sA[:, 512:w], IDENTB, MASKR,
                         start=False, stop=True)
                t_mm[eo] = P.tag_last("tensor", mm)
            for qc in range(4):
                if kt != first_kt[qc]:
                    continue
                for eo in (0, 1):
                    h = 2 * hp + eo
                    prev = (h, qc - 2) if qc >= 2 else (h - 2, qc + 2)
                    zw = [] if prev in ev_tok else [v_q2, v_done]
                    if prev in ev_tok:
                        zw.append(ev_tok[prev])
                    P.op("tensor", "matmul", OBK[(eo, qc % 2)][0:65, 0:512],
                         ZROW[0:1, 0:65], onesrow[0:1, 0:512],
                         start=True, stop=False, waits=zw)
            for eo in (0, 1):
                h = 2 * hp + eo
                if kt < 4:
                    P.op("scalar", "activation", out=sigP[:, eo, 0:w],
                         in_=SS[eo][:, 0:w], func=AF.Exp,
                         bias=cfc("haloc", kt, 1),
                         waits=(t_mm[eo],), inc=sc.inc())
                else:
                    P.op("scalar", "activation", out=sigP[:, eo, 0:w],
                         in_=SS[eo][:, 0:w], func=AF.Exp,
                         waits=(t_mm[eo],), inc=sc.inc())
                explast[h] = sc.tok()
            for qc in range(4):
                if kt not in kt_of_qc[qc]:
                    continue
                lo = max(qlo, 512 * qc)
                hi = min(qhi, 512 * qc + 512)
                for eo in (0, 1):
                    h = 2 * hp + eo
                    P.op("tensor", "matmul",
                         OBK[(eo, qc % 2)][0:65, lo - 512 * qc:hi - 512 * qc],
                         vhat[:, kt, 65 * kv:65 * kv + 65],
                         sigP[:, eo, lo - qlo:hi - qlo],
                         start=False, stop=(kt == last_kt[qc]),
                         waits=[explast[h]])
                if kt != last_kt[qc]:
                    continue
                t_pv = P.tag_last("tensor", mm)
                for eo in (0, 1):
                    h = 2 * hp + eo
                    ob = OBK[(eo, qc % 2)]
                    rw = [t_pv] if eo == 0 else []
                    if qc == 0 and rs_dma[eo] is not None:
                        rw.append(rs_dma[eo])
                    P.op("scalar", "copy",
                         out=rscr[0:1, eo, 512 * qc:512 * qc + 512],
                         in_=ob[64:65, 0:512], waits=tuple(rw), inc=sc.inc())
                    if eo == 0:
                        P.op("scalar", "copy",
                             out=ahat[0:64, hp, 512 * qc:512 * qc + 512],
                             in_=ob[0:64, 0:512], inc=sc.inc())
                    else:
                        obw = [ob_dma[qc % 2]] if ob_dma[qc % 2] is not None else []
                        P.op("scalar", "copy", out=obounce[0:64, qc % 2, :],
                             in_=ob[0:64, 0:512], waits=obw, inc=sc.inc())
                        s_ob = sc.tok()
                        P.op("sync", "dma_start",
                             out=ahat[64:128, hp, 512 * qc:512 * qc + 512],
                             in_=obounce[0:64, qc % 2, :], waits=(s_ob,),
                             inc=sy.inc())
                        ob_dma[qc % 2] = ybar()
                    ev_tok[(h, qc)] = sc.tok()
                    if qc == 3:
                        P.op("sync", "dma_start", out=rrall[h:h + 1, :],
                             in_=rscr[0:1, eo, :], waits=(ev_tok[(h, qc)],),
                             inc=sy.inc())
                        rs_dma[eo] = ybar()
    s_ahat = sc.tok()
    g_ahat = ybar()
    P.op("vector", "reciprocal", out=rrinv[:, :], in_=rrall[:, :],
         waits=(s_ahat, rs_dma[0], rs_dma[1]), inc=vc.inc())
    P.op("vector", "tensor_copy", out=rrallb[:, :], in_=rrinv[:, :], inc=vc.inc())
    v_rr = vc.tok()
    norm_tok = {}
    it = 0
    for oc in range(8):
        for tc in range(4):
            pp = psC if it % 2 == 0 else psD
            tw = [v_rr, g_ahat]
            if it >= 2:
                tw.append(norm_tok[it - 2])
            P.op("tensor", "matmul", pp[:, :], bdsel[:, oc, :],
                 rrallb[:, 512 * tc:512 * tc + 512],
                 start=True, stop=True, waits=tw, inc=mm.inc())
            t_rep = mm.tok()
            P.op("vector", "tensor_tensor",
                 out=ahat[:, oc, 512 * tc:512 * tc + 512],
                 in0=ahat[:, oc, 512 * tc:512 * tc + 512], in1=pp[:, :],
                 op=ALU.mult, waits=(t_rep,), inc=vc.inc())
            norm_tok[it] = vc.tok()
            it += 1
    v_ahat = vc.tok()
    if debug:
        P.op("sync", "dma_start", out=dbg["dbg_ahat"][:, :],
             in_=ahat[:].rearrange("p a t -> p (a t)"),
             waits=(v_ahat, g_ahat), inc=sy.inc())
    dbg_bar2 = ybar() if debug else None

    p2_mm_done = mm.tok()

    if upto <= 4:
        return finish_early()

    P.op("sync", "dma_start", out=xring[:, 0, :], in_=d_xres[0:128, :],
         waits=(p2_mm_done,), inc=sy.inc())
    xres_tok = {0: ybar()}

    v_ln = {}
    zh_ev = []

    def emit_tr(t_):
        for g_ in range(2):
            tw_ = [v_ln[t_]] if g_ == 0 else []
            if zh_ev:
                tw_.append(zh_ev[-1])
            for j_ in range(4):
                dc_ = 4 * g_ + j_
                P.op("tensor", "transpose", psG0[:, 128 * j_:128 * j_ + 128],
                     strm[:, t_, 128 * dc_:128 * dc_ + 128], IDENTF,
                     waits=tw_ if j_ == 0 else (),
                     inc=mm.inc() if j_ == 3 else None)
            t_tr_ = mm.tok()
            P.op("scalar", "copy",
                 out=zhatfm[:, 4 * g_:4 * g_ + 4, 128 * t_:128 * t_ + 128],
                 in_=psG0[:, :].rearrange("p (a c) -> p a c", c=128),
                 waits=(t_tr_,), inc=sc.inc())
            zh_ev.append(sc.tok())

    ln_done = {}
    scat_done = {}
    for t in range(NT):
        if t + 1 < NT:
            P.op("sync", "dma_start", out=xring[:, (t + 1) % 2, :],
                 in_=d_xres[128 * (t + 1):128 * (t + 1) + 128, :],
                 waits=(ln_done[t - 1],) if t >= 1 else (), inc=sy.inc())
            xres_tok[t + 1] = ybar()
        banks = (psC, psD) if t % 2 == 0 else (psE, psF)
        t_ops = []
        for oci, pp in enumerate(banks):
            tw = [g_wo, g_ahat, v_ahat] if t < 2 else [ln_done[t - 2]]
            for dc in range(8):
                P.op("tensor", "matmul", pp[:, :],
                     ahat[:, dc, 128 * t:128 * t + 128],
                     wsb[:, dc, 512 * oci:512 * oci + 512],
                     start=(dc == 0), stop=False,
                     waits=tw if dc == 0 else ())
            P.op("tensor", "matmul", pp[:, :], onesrow[0:1, 0:128],
                 brows2[0:1, 512 * oci:512 * oci + 512],
                 start=False, stop=True, inc=mm.inc())
            t_ops.append(mm.tok())
        if t >= 1:
            emit_tr(t - 1)
        P.op("vector", "tensor_add", strm[:, t, 0:512], banks[0][:, :],
             xring[:, t % 2, 0:512], waits=(t_ops[0], xres_tok[t]), inc=vc.inc())
        P.op("vector", "tensor_add", strm[:, t, 512:1024], banks[1][:, :],
             xring[:, t % 2, 512:1024], waits=(t_ops[1],), inc=vc.inc())
        P.op("vector", "reduce_sum", out=lncol[:, 0:1], in_=strm[:, t, :],
             axis=AXX, inc=vc.inc())
        P.op("vector", "tensor_scalar", out=lncol[:, 1:2], in0=lncol[:, 0:1],
             scalar1=1.0 / D, scalar2=None, op0=ALU.mult, inc=vc.inc())
        v_m = vc.tok()
        P.op("scalar", "activation", out=sqdummy[:, :], in_=strm[:, t, :],
             func=AF.Square, accum_out=lncol[:, 2:3], waits=(v_m,), inc=sc.inc())
        s_sq = sc.tok()
        P.op("vector", "tensor_scalar", out=lncol[:, 3:4], in0=lncol[:, 2:3],
             scalar1=1.0 / D, scalar2=None, op0=ALU.mult, waits=(s_sq,),
             inc=vc.inc())
        P.op("vector", "tensor_mul", lncol[:, 4:5], lncol[:, 1:2], lncol[:, 1:2],
             inc=vc.inc())
        P.op("vector", "tensor_sub", lncol[:, 3:4], lncol[:, 3:4], lncol[:, 4:5],
             inc=vc.inc())
        v_var = vc.tok()
        P.op("scalar", "activation", out=lncol[:, 5:6], in_=lncol[:, 3:4],
             func=AF.Sqrt, bias=cpool[:, 1:2], waits=(v_var,), inc=sc.inc())
        s_sd = sc.tok()
        P.op("vector", "reciprocal", out=lncol[:, 6:7], in_=lncol[:, 5:6],
             waits=(s_sd,), inc=vc.inc())
        v_iv = vc.tok()
        P.op("vector", "tensor_scalar", out=strm[:, t, :], in0=strm[:, t, :],
             scalar1=lncol[:, 1:2], scalar2=lncol[:, 6:7],
             op0=ALU.subtract, op1=ALU.mult, waits=(v_iv,), inc=vc.inc())
        v_ln[t] = vc.tok()
        zw = list(scat_done[t - 2]) if t >= 2 else []
        P.op("vector", "tensor_copy", out=z1ring[:, t % 2, 0:1024],
             in_=strm[:, t, :], waits=zw, inc=vc.inc())
        ln_done[t] = vc.tok()
        P.op("gpsimd", "indirect_dma_start",
             out=d_fwdx[:, :],
             out_offset=IndirectOffsetOnAxis(idxs[:, t:t + 1], 0),
             in_=z1ring[:, t % 2, 0:1024], in_offset=None,
             waits=(ln_done[t],), inc=sg.inc())
        scat_done[t] = (gbar(),)
    emit_tr(NT - 1)
    s_zh = sc.tok()
    g_scat = gbar()
    if debug:
        P.op("sync", "dma_start", out=dbg["dbg_strm"][:, :],
             in_=strm[:].rearrange("p a t -> p (a t)"), waits=(ln_done[NT - 1],),
             inc=sy.inc())
        P.op("sync", "dma_start", out=dbg["dbg_zhat"][:, :],
             in_=zhatfm[:].rearrange("p a t -> p (a t)"),
             waits=(s_zh,), inc=sy.inc())
    dbg_bar3 = ybar() if debug else None

    g_evac = {}
    g_trd = {}
    side_tok = {}
    s_gt_tok = {}
    for tc in range(4):
        pg = psC if tc % 2 == 0 else psD
        tw = [s_zh, ln_done[NT - 1]]
        if tc >= 2:
            tw.append(g_evac[tc - 2])
        for dc in range(8):
            P.op("tensor", "matmul", pg[0:9, :], egwsb[:, dc, :],
                 zhatfm[:, dc, 512 * tc:512 * tc + 512],
                 start=(dc == 0), stop=(dc == 7), waits=tw if dc == 0 else (),
                 inc=mm.inc() if dc == 7 else None)
        t_g = mm.tok()
        grw = [t_g] if tc < 2 else [t_g, g_trd[tc - 2]]
        P.op("scalar", "activation", out=gring[0:9, tc % 2, :], in_=pg[0:9, :],
             func=AF.Sigmoid, bias=cfc("egbc", 0, 1)[0:9, :], waits=grw,
             inc=sc.inc())
        g_evac[tc] = sc.tok()
        for ti in range(4):
            t = 4 * tc + ti
            pe = psE if t % 2 == 0 else psF
            trw = [g_evac[tc]]
            if t >= 2:
                trw.append(s_gt_tok[t - 2])
            P.op("tensor", "transpose", pe[0:128, 0:9],
                 gring[0:9, tc % 2, 128 * ti:128 * ti + 128], IDENTF[0:9, 0:9],
                 waits=trw, inc=mm.inc())
            t_tr = mm.tok()
            P.op("scalar", "copy", out=gstt[:, t % 2, 0:9], in_=pe[0:128, 0:9],
                 waits=(t_tr,), inc=sc.inc())
            s_gt = sc.tok()
            s_gt_tok[t] = s_gt
            P.op("vector", "tensor_add", scr8[:], gstt[:, t % 2, 1:9],
                 cfc("expbr", 0, 8), waits=(s_gt,), inc=vc.inc())
            P.op("vector", "reduce_max", out=mxcol[:], in_=scr8[:], axis=AXX,
                 inc=vc.inc())
            vb = vc.tok()
            P.op("vector", "tensor_single_scalar", out=scr8[:], in_=scr8[:],
                 scalar=mxcol[:], op=ALU.is_equal, waits=(vb,), inc=vc.inc())
            P.op("vector", "scalar_tensor_tensor", out=scrsm[:], in0=scr8[:],
                 scalar=1.0, in1=gstt[:, t % 2, 1:9], op0=ALU.mult, op1=ALU.mult,
                 accum_out=t1col[:], inc=vc.inc())
            vb2 = vc.tok()
            P.op("vector", "tensor_scalar", out=dcol[:, 0:1],
                 in0=gstt[:, t % 2, 0:1], scalar1=t1col[:], scalar2=None,
                 op0=ALU.subtract, waits=(vb2,), inc=vc.inc())
            P.op("vector", "tensor_scalar", out=dcol[:, 1:2], in0=dcol[:, 0:1],
                 scalar1=-1.0, scalar2=None, op0=ALU.mult, inc=vc.inc())
            v_d = vc.tok()
            P.op("scalar", "activation", out=wst[:], in_=dcol[:], func=AF.Sigmoid,
                 waits=(v_d,), inc=sc.inc())
            s_w = sc.tok()
            P.op("vector", "tensor_scalar", out=wsmcol[:, t:t + 1],
                 in0=wst[:, 0:1], scalar1=1.0, scalar2=None, op0=ALU.subtract,
                 waits=(s_w,), inc=vc.inc())
            P.op("vector", "tensor_mul", wsmcol[:, t:t + 1], wsmcol[:, t:t + 1],
                 cfc("selmask", t, 1), inc=vc.inc())
            P.op("vector", "tensor_scalar", out=wsmcol[:, t:t + 1],
                 in0=wsmcol[:, t:t + 1], scalar1=1.0, scalar2=None, op0=ALU.add,
                 inc=vc.inc())
            sw_ = [side_tok[t - 2]] if t >= 2 else []
            P.op("vector", "tensor_copy", out=sidescr[:, t % 2, 0:8],
                 in_=scr8[:], waits=sw_, inc=vc.inc())
            P.op("vector", "tensor_copy", out=sidescr[:, t % 2, 8:9],
                 in_=wst[:, 1:2], inc=vc.inc())
            v_side = vc.tok()
            P.op("gpsimd", "indirect_dma_start",
                 out=d_fwds[:, :],
                 out_offset=IndirectOffsetOnAxis(idxs[:, t:t + 1], 0),
                 in_=sidescr[:, t % 2, :], in_offset=None,
                 waits=(v_side,), inc=sg.inc())
            side_tok[t] = gbar()
        g_trd[tc] = mm.tok()
    g_side = gbar()
    v_wsm = vc.tok()
    if debug:
        P.op("sync", "dma_start", out=dbg["dbg_wsm"][:, :], in_=wsmcol[:, :],
             waits=(v_wsm,), inc=sy.inc())

    P.op("gpsimd", "collective_compute", "ReduceScatter", ALU.add,
         replica_groups=[list(range(8))], ins=[d_fwds[0:NSEL, :].opt()],
         outs=[d_rss.opt()], waits=(g_side,), inc=cc.inc())
    P.op("gpsimd", "collective_compute", "ReduceScatter", ALU.add,
         replica_groups=[list(range(8))], ins=[d_fwdx[0:NSEL, :].opt()],
         outs=[d_rsx.opt()], inc=cc.inc())
    c_rs = cc.tok()

    o_mm_done = mm.tok()
    P.op("gpsimd", "dma_start", out=sw1sb[:],
         in_=d_sw1T[:, :].rearrange("(dc p) h -> p dc h", p=128),
         waits=(o_mm_done,), inc=sg.inc())
    P.op("gpsimd", "dma_start", out=sw2sb[:],
         in_=d_sw2T[:, :].rearrange("(dc p) h -> p dc h", p=128), inc=sg.inc())
    ob1, nb1 = br["sb1"]
    ob2, nb2 = br["sb2"]
    P.op("gpsimd", "dma_start", out=browsb[0:1, 0:512],
         in_=d_brows[0:1, ob1:ob1 + nb1], waits=(o_mm_done,), inc=sg.inc())
    P.op("gpsimd", "dma_start", out=browsb[0:1, 512:1024],
         in_=d_brows[0:1, ob2:ob2 + nb2], inc=sg.inc())
    g_sw12 = gbar()

    if upto <= 5:
        return finish_early()

    ffn_z = {}
    for tc in range(4):
        for hc in range(4):
            it = 4 * tc + hc
            p1 = psC if it % 2 == 0 else psD
            p2_ = psE if it % 2 == 0 else psF
            tw = [g_sw12, s_zh, v_wsm]
            if debug and it == 0:
                tw.append(dbg_bar2)
                tw.append(dbg_bar3)
            if it >= 2:
                tw.append(ffn_z[it - 2])
            for dc in range(8):
                P.op("tensor", "matmul", p1[:, :],
                     sw1sb[:, dc, 128 * hc:128 * hc + 128],
                     zhatfm[:, dc, 512 * tc:512 * tc + 512],
                     start=(dc == 0), stop=(dc == 7), waits=tw if dc == 0 else ())
            for dc in range(8):
                P.op("tensor", "matmul", p2_[:, :],
                     sw2sb[:, dc, 128 * hc:128 * hc + 128],
                     zhatfm[:, dc, 512 * tc:512 * tc + 512],
                     start=(dc == 0), stop=(dc == 7),
                     inc=mm.inc() if dc == 7 else None)
            t_x = mm.tok()
            P.op("scalar", "activation", out=ssping[:, it % 2, :], in_=p1[:, :],
                 func=AF.Silu, bias=cfc("sb1c", hc, 1), waits=(t_x,),
                 inc=sc.inc())
            s_sil = sc.tok()
            P.op("vector", "scalar_tensor_tensor",
                 out=zeb[:, hc, 512 * tc:512 * tc + 512], in0=p2_[:, :],
                 scalar=cfc("sb2c", hc, 1), in1=ssping[:, it % 2, :],
                 op0=ALU.add, op1=ALU.mult, waits=(s_sil,), inc=vc.inc())
            ffn_z[it] = vc.tok()
    v_z = vc.tok()

    P.op("gpsimd", "dma_start", out=sw3sb[:],
         in_=d_sw3T[:, :].rearrange("(hc p) o -> p hc o", p=128),
         waits=(mm.tok(),), inc=sg.inc())
    ob3, nb3 = br["sb3"]
    P.op("gpsimd", "dma_start", out=brows2[0:1, 0:1024],
         in_=d_brows[0:1, ob3:ob3 + nb3], inc=sg.inc())
    g_sw3 = gbar()
    for half in range(2):
        pe = psE if half == 0 else psF
        for hc in range(4):
            P.op("tensor", "matmul", pe[0:1, :],
                 cbfc("slnb", hc, 1), sw3sb[:, hc, 512 * half:512 * half + 512],
                 start=(hc == 0), stop=False,
                 waits=(g_sw3, ffn_z[14], ffn_z[15]) if hc == 0 else ())
        P.op("tensor", "matmul", pe[0:1, :], onesrow[0:1, 0:1],
             brows2[0:1, 512 * half:512 * half + 512], start=False, stop=True,
             inc=mm.inc())
        t_b = mm.tok()
        P.op("scalar", "copy", out=bias3e[0:1, 512 * half:512 * half + 512],
             in_=pe[0:1, :], waits=(t_b,), inc=sc.inc())
    s_b3 = sc.tok()

    sh_done = {}
    sqmm_tok = {}
    s2_done = {}
    rows_tok = {}

    def ffn_stats_a(tc):
        for hc in range(4):
            g = 4 * tc + hc
            sqw = [v_z]
            if g >= 2:
                sqw.append(sqmm_tok[g - 2])
            P.op("scalar", "activation", out=sqscr[:, hc % 2, :],
                 in_=zeb[:, hc, 512 * tc:512 * tc + 512], func=AF.Square,
                 waits=sqw, inc=sc.inc())
            s_sq = sc.tok()
            stw = [v_z, s_b3] if tc == 0 else [sh_done[tc - 1]]
            P.op("tensor", "matmul", psE[0:1, :], ONESCB,
                 zeb[:, hc, 512 * tc:512 * tc + 512],
                 start=(hc == 0), stop=(hc == 3),
                 waits=stw if hc == 0 else ())
            P.op("tensor", "matmul", psF[0:1, :], ONESCB, sqscr[:, hc % 2, :],
                 start=(hc == 0), stop=(hc == 3), waits=(s_sq,), inc=mm.inc())
            sqmm_tok[g] = mm.tok()
        t_st = mm.tok()
        P.op("scalar", "activation", out=frows[0:1, 0, :], in_=psE[0:1, :],
             func=AF.Copy, scale=cpool[0:1, 4:5], waits=(t_st,), inc=sc.inc())
        P.op("scalar", "activation", out=frows[0:1, 1, :], in_=psF[0:1, :],
             func=AF.Copy, scale=cpool[0:1, 4:5], inc=sc.inc())
        s_mu = sc.tok()
        P.op("vector", "tensor_mul", frows[0:1, 2, :], frows[0:1, 0, :],
             frows[0:1, 0, :], waits=(s_mu,), inc=vc.inc())
        P.op("vector", "tensor_sub", frows[0:1, 1, :], frows[0:1, 1, :],
             frows[0:1, 2, :], inc=vc.inc())
        v_v = vc.tok()
        P.op("scalar", "activation", out=frows[0:1, 1, :], in_=frows[0:1, 1, :],
             func=AF.Sqrt, bias=cpool[0:1, 1:2], waits=(v_v,), inc=sc.inc())
        s_s = sc.tok()
        P.op("vector", "reciprocal", out=frows[0:1, 1, :], in_=frows[0:1, 1, :],
             waits=(s_s,), inc=vc.inc())
        P.op("vector", "tensor_copy", out=invrowb[:], in_=frows[0:1, 1, :],
             inc=vc.inc())
        P.op("vector", "tensor_mul", frows[0:1, 2, :], frows[0:1, 0, :],
             frows[0:1, 1, :], inc=vc.inc())
        P.op("vector", "tensor_scalar", out=frows[0:1, 2, :],
             in0=frows[0:1, 2, :], scalar1=-1.0, scalar2=None, op0=ALU.mult,
             inc=vc.inc())
        P.op("vector", "tensor_copy", out=minrowb[:], in_=frows[0:1, 2, :],
             inc=vc.inc())
        rows_tok[tc] = vc.tok()

    def ffn_stats_b(tc):
        P.op("tensor", "matmul", psE[0:128, :], onesrow[0:1, 0:128], invrowb[:],
             start=True, stop=True, waits=(rows_tok[tc],))
        P.op("tensor", "matmul", psF[0:128, :], onesrow[0:1, 0:128], minrowb[:],
             start=True, stop=True)
        t_rep = P.tag_last("tensor", mm)
        for hc in range(4):
            P.op("vector", "scalar_tensor_tensor", out=ttscr[:, :],
                 in0=zeb[:, hc, 512 * tc:512 * tc + 512],
                 scalar=cfc("slng", hc, 1), in1=psE[0:128, :],
                 op0=ALU.mult, op1=ALU.mult,
                 waits=(t_rep,) if hc == 0 else (), inc=vc.inc())
            P.op("vector", "scalar_tensor_tensor",
                 out=sheb[:, hc, 512 * tc:512 * tc + 512],
                 in0=psF[0:128, :], scalar=cfc("slng", hc, 1),
                 in1=ttscr[:, :], op0=ALU.mult, op1=ALU.add, inc=vc.inc())
        sh_done[tc] = vc.tok()

    def ffn_s2(tc):
        for ti in range(4):
            t = 4 * tc + ti
            for oci in range(2):
                it = 2 * t + oci
                pp = psC if it % 2 == 0 else psD
                tw = [sh_done[tc], s_b3]
                if it >= 2:
                    tw.append(s2_done[it - 2])
                for hc in range(4):
                    P.op("tensor", "matmul", pp[:, :],
                         sheb[:, hc, 128 * t:128 * t + 128],
                         sw3sb[:, hc, 512 * oci:512 * oci + 512],
                         start=(hc == 0), stop=False, waits=tw if hc == 0 else ())
                P.op("tensor", "matmul", pp[:, :], onesrow[0:1, 0:128],
                     bias3e[0:1, 512 * oci:512 * oci + 512], start=False,
                     stop=True, inc=mm.inc())
                t_s = mm.tok()
                P.op("vector", "scalar_tensor_tensor",
                     out=strm[:, t, 512 * oci:512 * oci + 512], in0=pp[:, :],
                     scalar=wsmcol[:, t:t + 1],
                     in1=strm[:, t, 512 * oci:512 * oci + 512],
                     op0=ALU.mult, op1=ALU.add, waits=(t_s,), inc=vc.inc())
                s2_done[it] = vc.tok()

    for tc in range(4):
        ffn_stats_a(tc)
        if tc >= 1:
            ffn_s2(tc - 1)
        ffn_stats_b(tc)
    ffn_s2(3)
    v_sh = vc.tok()
    v_s2 = vc.tok()
    if debug:
        P.op("sync", "dma_start", out=dbg["dbg_s2"][:, :],
             in_=strm[:].rearrange("p a t -> p (a t)"), waits=(v_s2,), inc=sy.inc())
    dbg_bar4 = ybar() if debug else None

    if upto <= 6:
        return finish_early()

    P.op("gpsimd", "dma_start", out=xsel[:],
         in_=d_rsx[:, :].rearrange("(a p) c -> p a c", p=128),
         waits=(c_rs, v_s2), inc=sg.inc())
    P.op("gpsimd", "dma_start", out=xsels[:],
         in_=d_rss[:, :].rearrange("(a p) c -> p a c", p=128), inc=sg.inc())
    g_xsel = gbar()
    for ts in range(2):
        P.op("vector", "tensor_copy", out=indf[:, ts, :],
             in_=xsels[:, ts, 0:8], waits=(g_xsel,) if ts == 0 else (),
             inc=vc.inc())
        P.op("vector", "tensor_copy", out=weff[:, ts, :],
             in_=xsels[:, ts, 8:9], inc=vc.inc())
    v_ind = vc.tok()
    for ts in range(2):
        for dc in range(8):
            P.op("sync", "dma_start_transpose",
                 out=xselfm[:, dc, 128 * ts:128 * ts + 128],
                 in_=xsel[:, ts, 128 * dc:128 * dc + 128],
                 waits=(g_xsel,) if (ts == 0 and dc == 0) else (), inc=sy.inc())
    y_xsel = ybar()

    def load_expert12(e, slot, waits):
        e1, e2, _ = ewS[slot]
        P.op("gpsimd", "dma_start", out=e1[:],
             in_=d_ew1T[e, :, :].rearrange("(dc p) h -> p dc h", p=128),
             waits=waits, inc=sg.inc())
        P.op("gpsimd", "dma_start", out=e2[:],
             in_=d_ew2T[e, :, :].rearrange("(dc p) h -> p dc h", p=128),
             inc=sg.inc())
        return (gbar(),)

    zfree = mm.tok()
    sqmm_tok = {}
    g_e12 = {0: load_expert12(0, 0, (zfree,) + ((dbg_bar3,) if debug else ()))}
    g_e3 = {}
    t_h12 = {}
    t_out = {}
    h12_done = {}
    v_ze = {}
    st = {"s_be_prev": s_b3, "v_she_prev": v_s2, "prev_acc": {},
          "acc_last": None}

    def emit_h12(e):
        slot = e % 2
        e1, e2, _ = ewS[slot]
        zb = zebm if slot == 0 else zebm2
        h12 = {}
        for hc in range(4):
            p1 = PSB[0] if hc % 2 == 0 else PSB[1]
            p2_ = PSB[2] if hc % 2 == 0 else PSB[3]
            tw = [*g_e12[e], y_xsel]
            if hc >= 2:
                tw.append(h12[hc - 2])
            elif e >= 1:
                tw.append(h12_done[e - 1])
            for dc in range(8):
                P.op("tensor", "matmul", p1[:, 0:256],
                     e1[:, dc, 128 * hc:128 * hc + 128], xselfm[:, dc, :],
                     start=(dc == 0), stop=(dc == 7), waits=tw if dc == 0 else ())
            for dc in range(8):
                P.op("tensor", "matmul", p2_[:, 0:256],
                     e2[:, dc, 128 * hc:128 * hc + 128], xselfm[:, dc, :],
                     start=(dc == 0), stop=(dc == 7),
                     inc=mm.inc() if dc == 7 else None)
            t_h = mm.tok()
            P.op("scalar", "activation", out=ssping[:, hc % 2, 0:256],
                 in_=p1[:, 0:256], func=AF.Silu, bias=cfc("eb1c", 4 * e + hc, 1),
                 waits=(t_h,), inc=sc.inc())
            s_sil = sc.tok()
            P.op("vector", "scalar_tensor_tensor", out=zb[:, hc, :],
                 in0=p2_[:, 0:256], scalar=cfc("eb2c", 4 * e + hc, 1),
                 in1=ssping[:, hc % 2, 0:256], op0=ALU.add, op1=ALU.mult,
                 waits=(s_sil,), inc=vc.inc())
            h12[hc] = vc.tok()
        v_ze[e] = vc.tok()
        t_h12[e] = mm.tok()
        h12_done[e] = vc.tok()

    def tail_a(e):
        slot = e % 2
        _, _, e3 = ewS[slot]
        zb = zebm if slot == 0 else zebm2
        for hc in range(4):
            g = 4 * e + hc
            sqw = [v_ze[e]]
            if g >= 2:
                sqw.append(sqmm_tok[g - 2])
            P.op("scalar", "activation", out=sqscr[:, hc % 2, 0:256],
                 in_=zb[:, hc, :], func=AF.Square, waits=sqw, inc=sc.inc())
            s_sq = sc.tok()
            P.op("tensor", "matmul", PSB[4][0:1, 0:256], ONESCB, zb[:, hc, :],
                 start=(hc == 0), stop=(hc == 3),
                 waits=(v_ze[e], st["v_she_prev"]) if hc == 0 else ())
            P.op("tensor", "matmul", PSB[5][0:1, 0:256], ONESCB,
                 sqscr[:, hc % 2, 0:256],
                 start=(hc == 0), stop=(hc == 3), waits=(s_sq,), inc=mm.inc())
            sqmm_tok[g] = mm.tok()
        t_st = mm.tok()
        P.op("scalar", "activation", out=frows[0:1, 0, 0:256],
             in_=PSB[4][0:1, 0:256], func=AF.Copy, scale=cpool[0:1, 4:5],
             waits=(t_st,), inc=sc.inc())
        P.op("scalar", "activation", out=frows[0:1, 1, 0:256],
             in_=PSB[5][0:1, 0:256], func=AF.Copy, scale=cpool[0:1, 4:5],
             inc=sc.inc())
        s_mu = sc.tok()
        P.op("vector", "tensor_mul", frows[0:1, 2, 0:256], frows[0:1, 0, 0:256],
             frows[0:1, 0, 0:256], waits=(s_mu,), inc=vc.inc())
        P.op("vector", "tensor_sub", frows[0:1, 1, 0:256], frows[0:1, 1, 0:256],
             frows[0:1, 2, 0:256], inc=vc.inc())
        v_v = vc.tok()
        P.op("scalar", "activation", out=frows[0:1, 1, 0:256],
             in_=frows[0:1, 1, 0:256], func=AF.Sqrt, bias=cpool[0:1, 1:2],
             waits=(v_v,), inc=sc.inc())
        s_s = sc.tok()
        P.op("vector", "reciprocal", out=frows[0:1, 1, 0:256],
             in_=frows[0:1, 1, 0:256], waits=(s_s,), inc=vc.inc())
        P.op("vector", "tensor_copy", out=invrowb[0:1, 0:256],
             in_=frows[0:1, 1, 0:256], inc=vc.inc())
        P.op("vector", "tensor_mul", frows[0:1, 2, 0:256], frows[0:1, 0, 0:256],
             frows[0:1, 1, 0:256], inc=vc.inc())
        P.op("vector", "tensor_scalar", out=frows[0:1, 2, 0:256],
             in0=frows[0:1, 2, 0:256], scalar1=-1.0, scalar2=None, op0=ALU.mult,
             inc=vc.inc())
        P.op("vector", "tensor_copy", out=minrowb[0:1, 0:256],
             in_=frows[0:1, 2, 0:256], inc=vc.inc())
        st["v_rows"] = vc.tok()
        for half in range(2):
            pe = PSB[6] if half == 0 else PSB[7]
            bw = [g_e3[e]]
            if e == 0:
                bw += [v_s2, s_b3]
            else:
                bw.append(st["prev_acc"][2 + half])
            for hc in range(4):
                P.op("tensor", "matmul", pe[0:1, :], cbfc("selnb", hc, 1),
                     e3[:, hc, 512 * half:512 * half + 512],
                     start=(hc == 0), stop=False,
                     waits=bw if hc == 0 else ())
            P.op("tensor", "matmul", pe[0:1, :], onesrow[0:1, 0:1],
                 brows2[0:1, 512 * half:512 * half + 512], start=False,
                 stop=True, inc=mm.inc())
            t_b = mm.tok()
            P.op("scalar", "copy", out=bias3e[0:1, 512 * half:512 * half + 512],
                 in_=pe[0:1, :], waits=(t_b,), inc=sc.inc())
        st["s_be"] = sc.tok()
        st["s_be_prev"] = st["s_be"]

    def tail_b(e):
        slot = e % 2
        _, _, e3 = ewS[slot]
        zb = zebm if slot == 0 else zebm2
        P.op("tensor", "matmul", PSB[4][0:128, 0:256], onesrow[0:1, 0:128],
             invrowb[0:1, 0:256], start=True, stop=True, waits=(st["v_rows"],))
        P.op("tensor", "matmul", PSB[5][0:128, 0:256], onesrow[0:1, 0:128],
             minrowb[0:1, 0:256], start=True, stop=True)
        t_rep = P.tag_last("tensor", mm)
        for hc in range(4):
            P.op("vector", "scalar_tensor_tensor", out=ttscr[:, 0:256],
                 in0=zb[:, hc, :], scalar=cfc("selng", hc, 1),
                 in1=PSB[4][0:128, 0:256], op0=ALU.mult, op1=ALU.mult,
                 waits=(t_rep,) if hc == 0 else (), inc=vc.inc())
            P.op("vector", "scalar_tensor_tensor", out=shebm[:, hc, :],
                 in0=PSB[5][0:128, 0:256], scalar=cfc("selng", hc, 1),
                 in1=ttscr[:, 0:256], op0=ALU.mult, op1=ALU.add, inc=vc.inc())
        v_she = vc.tok()
        st["v_she_prev"] = v_she
        acc_now = {}
        for ts in range(2):
            for oci in range(2):
                it = 2 * ts + oci
                pp = PSB[6] if it % 2 == 0 else PSB[7]
                tw = [v_she, st["s_be"]]
                if e > 0:
                    tw.append(st["prev_acc"][it])
                if it >= 2:
                    tw.append(acc_now[it - 2])
                for hc in range(4):
                    P.op("tensor", "matmul", pp[:, :],
                         shebm[:, hc, 128 * ts:128 * ts + 128],
                         e3[:, hc, 512 * oci:512 * oci + 512],
                         start=(hc == 0), stop=False, waits=tw if hc == 0 else ())
                P.op("tensor", "matmul", pp[:, :], onesrow[0:1, 0:128],
                     bias3e[0:1, 512 * oci:512 * oci + 512], start=False,
                     stop=True, inc=mm.inc())
                t_o = mm.tok()
                if e == 0:
                    P.op("vector", "tensor_scalar",
                         out=eacc[:, ts, 512 * oci:512 * oci + 512],
                         in0=pp[:, :], scalar1=indf[:, ts, e:e + 1],
                         scalar2=None, op0=ALU.mult, waits=(t_o, v_ind),
                         inc=vc.inc())
                else:
                    P.op("vector", "scalar_tensor_tensor",
                         out=eacc[:, ts, 512 * oci:512 * oci + 512],
                         in0=pp[:, :], scalar=indf[:, ts, e:e + 1],
                         in1=eacc[:, ts, 512 * oci:512 * oci + 512],
                         op0=ALU.mult, op1=ALU.add, waits=(t_o,), inc=vc.inc())
                acc_now[it] = vc.tok()
        t_out[e] = mm.tok()
        st["prev_acc"] = acc_now
        st["acc_last"] = vc.tok()

    emit_h12(0)
    o3, _ = br["eb3"]
    for e in range(8):
        if e + 1 < 8:
            g_e12[e + 1] = load_expert12(
                e + 1, (e + 1) % 2, (t_h12[e - 1],) if e >= 1 else (zfree,))
        P.op("gpsimd", "dma_start", out=ewS[e % 2][2][:],
             in_=d_ew3T[e, :, :].rearrange("(hc p) o -> p hc o", p=128),
             waits=(t_h12[e],) if e < 2 else (t_out[e - 2],), inc=sg.inc())
        P.op("gpsimd", "dma_start", out=brows2[0:1, 0:1024],
             in_=d_brows[0:1, o3 + 1024 * e:o3 + 1024 * e + 1024],
             waits=(st["s_be_prev"],), inc=sg.inc())
        g_e3[e] = gbar()
        tail_a(e)
        if e + 1 < 8:
            emit_h12(e + 1)
        tail_b(e)
    moe_prev_out = st["acc_last"]
    for ts in range(2):
        P.op("vector", "tensor_scalar", out=eacc[:, ts, :], in0=eacc[:, ts, :],
             scalar1=weff[:, ts, 0:1], scalar2=None, op0=ALU.mult,
             waits=(moe_prev_out,), inc=vc.inc())
    v_eacc = vc.tok()
    if debug:
        P.op("sync", "dma_start", out=dbg["dbg_eacc"][:, :],
             in_=eacc[:].rearrange("p a t -> p (a t)"), waits=(v_eacc,), inc=sy.inc())
    dbg_bar5 = ybar() if debug else None
    P.op("gpsimd", "dma_start",
         out=d_agin[:, :].rearrange("(a p) t -> p a t", p=128),
         in_=eacc[:], waits=(v_eacc,), inc=sg.inc())
    g_agin = gbar()
    P.op("gpsimd", "collective_compute", "AllGather", ALU.bypass,
         replica_groups=[list(range(8))], ins=[d_agin.opt()],
         outs=[d_agout[0:NSEL, :].opt()], waits=(g_agin,), inc=cc.inc())
    c_ag = cc.tok()

    if upto <= 7:
        return finish_early()

    radd = {}
    v_iv2h = {}
    for tp in range(NT // 2):
        for sub in range(2):
            t = 2 * tp + sub
            gw = []
            if t == 0:
                gw = [c_ag] + ([dbg_bar4] if debug else [])
            if t >= 2:
                gw.append(radd[t - 2])
            P.op("gpsimd", "indirect_dma_start",
                 out=retbuf[:, t % 2, :], out_offset=None,
                 in_=d_agout[:, :],
                 in_offset=IndirectOffsetOnAxis(idxs[:, 16 + t:16 + t + 1], 0),
                 waits=gw, inc=sg.inc())
        g_t = gbar()
        for sub in range(2):
            t = 2 * tp + sub
            P.op("vector", "tensor_add", strm[:, t, :], strm[:, t, :],
                 retbuf[:, t % 2, :], waits=(g_t, v_s2) if t == 0 else
                 ((g_t,) if sub == 0 else ()),
                 inc=vc.inc())
            radd[t] = vc.tok()
            P.op("vector", "reduce_sum", out=ln2c[:, 0, t:t + 1],
                 in_=strm[:, t, :], axis=AXX, inc=vc.inc())
            P.op("scalar", "activation", out=sqdummy[:, :], in_=strm[:, t, :],
                 func=AF.Square, accum_out=ln2c[:, 1, t:t + 1],
                 waits=(radd[t],), inc=sc.inc())
        t = 2 * tp + 1
        if t in (7, 15):
            hf = t // 8
            cs = slice(8 * hf, 8 * hf + 8)
            s_st = sc.tok()
            P.op("vector", "tensor_scalar", out=ln2c[:, 2, cs],
                 in0=ln2c[:, 0, cs], scalar1=1.0 / D, scalar2=None,
                 op0=ALU.mult, waits=(s_st,), inc=vc.inc())
            P.op("vector", "tensor_scalar", out=ln2c[:, 3, cs],
                 in0=ln2c[:, 1, cs], scalar1=1.0 / D, scalar2=None,
                 op0=ALU.mult, inc=vc.inc())
            P.op("vector", "tensor_mul", ln2c[:, 4, cs], ln2c[:, 2, cs],
                 ln2c[:, 2, cs], inc=vc.inc())
            P.op("vector", "tensor_sub", ln2c[:, 3, cs], ln2c[:, 3, cs],
                 ln2c[:, 4, cs], inc=vc.inc())
            v_var = vc.tok()
            P.op("scalar", "activation", out=ln2c[:, 4, cs], in_=ln2c[:, 3, cs],
                 func=AF.Sqrt, bias=cpool[:, 1:2], waits=(v_var,), inc=sc.inc())
            s_sd = sc.tok()
            P.op("vector", "reciprocal", out=ln2c[:, 5, cs], in_=ln2c[:, 4, cs],
                 waits=(s_sd,), inc=vc.inc())
            v_iv2h[hf] = vc.tok()
    st_tok = {}
    for t in range(NT):
        ow = [v_iv2h[t // 8]]
        if debug and t == 0:
            ow.append(dbg_bar5)
        if t >= 2:
            ow.append(st_tok[t - 2])
        P.op("vector", "tensor_scalar", out=otile[:, t % 2, :],
             in0=strm[:, t, :], scalar1=ln2c[:, 2, t:t + 1],
             scalar2=ln2c[:, 5, t:t + 1],
             op0=ALU.subtract, op1=ALU.mult, waits=ow, inc=vc.inc())
        v_o = vc.tok()
        if t % 2 == 0:
            P.op("sync", "dma_start", out=d_y[128 * t:128 * t + 128, :],
                 in_=otile[:, t % 2, :], waits=(v_o,), inc=sy.inc())
            st_tok[t] = ybar()
        else:
            P.op("scalar", "dma_start", out=d_y[128 * t:128 * t + 128, :],
                 in_=otile[:, t % 2, :], waits=(v_o,), inc=ssd.inc())
            st_tok[t] = sbar()
    ybar()
    sbar()

    return _emit(nc, P)


def _emit(nc, P):
    from contextlib import ExitStack
    es = ExitStack()
    with es:
        for s in P.sems:
            s.h = es.enter_context(nc.semaphore(s.name))
        block = es.enter_context(nc.Block())

        @block.tensor
        def _(eng):
            P.replay(eng, "tensor")

        @block.scalar
        def _(eng):
            P.replay(eng, "scalar")

        @block.vector
        def _(eng):
            P.replay(eng, "vector")

        @block.gpsimd
        def _(eng):
            P.replay(eng, "gpsimd")

        @block.sync
        def _(eng):
            P.replay(eng, "sync")

    return nc


def kernel(**inputs):
    ims = []
    br = None
    for c in range(8):
        im, br = _hostpack(inputs, c)
        ims.append(im)
    nc = build(br)
    res = run_bass_kernel_spmd(nc, ims, core_ids=list(range(8)))
    out = np.zeros((B, T, D), np.float32)
    for c in range(8):
        b_, half = c // 2, c % 2
        out[b_, half * NOWN:(half + 1) * NOWN] = res.results[c]["y"]
    return out


# version 19
# speedup vs baseline: 1.0333x; 1.0333x over previous
import numpy as np
import ml_dtypes
import concourse.bass as bass
import concourse.mybir as mybir
from concourse.bass import IndirectOffsetOnAxis
from concourse.bass_utils import run_bass_kernel_spmd

F32 = mybir.dt.float32
BF16 = mybir.dt.bfloat16
I32 = mybir.dt.int32
AF = mybir.ActivationFunctionType
ALU = mybir.AluOpType
AXX = mybir.AxisListType.X

B, T, D = 4, 4096, 1024
H, HV, HD, WIN = 16, 4, 64, 512
NOWN, HALO, NKEY = 2048, 512, 2560
NT = NOWN // 128
NKT = NKEY // 128
NSEL = 2048
NBLK = NSEL // 8
RMS_EPS = 1.1920929e-07
LN_EPS = 1e-5
BIG = 1.0e30
ENGS = ("tensor", "scalar", "vector", "gpsimd", "sync")


class Sem:
    __slots__ = ("name", "step", "n", "h")

    def __init__(self, name, step):
        self.name, self.step, self.n, self.h = name, step, 0, None

    def inc(self):
        self.n += self.step
        return self

    def tok(self):
        return (self, self.n)


class Prog:
    def __init__(self):
        self.ops = {e: [] for e in ENGS}
        self.sems = []
        self.autobar = {}

    def sem(self, name, step):
        s = Sem(name, step)
        self.sems.append(s)
        return s

    def op(self, eng, meth, *args, waits=(), inc=None, **kw):
        waits = tuple(waits)
        ab = self.autobar.get(eng)
        if ab is not None and meth not in (None, "nop"):
            abv = ab.n - (ab.step if inc is ab else 0)
            if abv > 0:
                waits = waits + ((ab, abv),)
        for (s, v) in waits:
            assert v <= s.n, f"wait on future {s.name}:{v}>{s.n}"
        self.ops[eng].append([waits, meth, args, kw, inc])

    def tag_last(self, eng, sem):
        row = self.ops[eng][-1]
        assert row[4] is None
        row[4] = sem.inc()
        return sem.tok()

    def replay(self, eng_obj, ename):
        for waits, meth, args, kw, inc in self.ops[ename]:
            for (s, v) in waits:
                eng_obj.wait_ge(s.h, v)
            if meth is None:
                continue
            ins = getattr(eng_obj, meth)(*args, **kw)
            if inc is not None:
                ins.then_inc(inc.h, inc.step)


def _hostpack(inputs, core):
    x = inputs["L0_attn_x"]
    f32 = np.float32
    b_, half = core // 2, core % 2
    t0 = half * NOWN

    xb = np.zeros((NKEY, D), f32)
    xb[HALO:] = x[b_, t0:t0 + NOWN]
    if half == 1:
        xb[:HALO] = x[b_, t0 - HALO:t0]
    im = {}
    im["xT"] = np.ascontiguousarray(xb.T)
    im["xres"] = np.ascontiguousarray(x[b_, t0:t0 + NOWN])
    im["wq"] = inputs["L0_attn_w_q"]
    im["wk"] = inputs["L0_attn_w_k"]
    im["wv"] = inputs["L0_attn_w_v"]
    im["wo"] = inputs["L0_attn_w_o"]
    im["sw1T"] = np.ascontiguousarray(inputs["L0_ffn_sw1"].T)
    im["sw2T"] = np.ascontiguousarray(inputs["L0_ffn_sw2"].T)
    im["sw3T"] = np.ascontiguousarray(inputs["L0_ffn_sw3"].T)
    im["egwT"] = np.ascontiguousarray(
        np.concatenate([inputs["L0_ffn_sg_w"], inputs["L0_ffn_eg_w"]], axis=0).T)
    im["ew1T"] = np.ascontiguousarray(inputs["L0_ffn_ew1"].transpose(0, 2, 1))
    im["ew2T"] = np.ascontiguousarray(inputs["L0_ffn_ew2"].transpose(0, 2, 1))
    im["ew3T"] = np.ascontiguousarray(inputs["L0_ffn_ew3"].transpose(0, 2, 1))

    br = {}
    cur = [0]
    parts = []

    def badd(name, arr):
        a = np.asarray(arr, f32).reshape(-1)
        br[name] = (cur[0], a.size)
        cur[0] += a.size
        parts.append(a)
    badd("bq", inputs["L0_attn_b_q"])
    badd("bk", inputs["L0_attn_b_k"])
    badd("bv", inputs["L0_attn_b_v"])
    badd("bo", inputs["L0_attn_b_o"])
    badd("sb1", inputs["L0_ffn_sb1"])
    badd("sb2", inputs["L0_ffn_sb2"])
    badd("sb3", inputs["L0_ffn_sb3"])
    badd("egb", np.concatenate([inputs["L0_ffn_sg_b"], inputs["L0_ffn_eg_b"]]))
    badd("eb1", inputs["L0_ffn_eb1"])
    badd("eb2", inputs["L0_ffn_eb2"])
    badd("eb3", inputs["L0_ffn_eb3"])
    im["brows"] = np.concatenate(parts).reshape(1, -1)

    p = np.arange(128)
    maskL = np.where(p[None, :] < p[:, None], -BIG, 0.0).astype(f32)
    maskR = np.where(p[None, :] > p[:, None], -BIG, 0.0).astype(f32)
    ident = np.eye(128, dtype=f32)
    bd2 = np.zeros((128, 2), f32)
    bd2[0:64, 0] = 1.0
    bd2[64:128, 1] = 1.0
    onescolv = np.ones((128, 1), f32)
    slnb_c = np.asarray(inputs["L0_ffn_s_ln_b"], f32).reshape(4, 128).T
    selnb_c = np.asarray(inputs["L0_ffn_sel_ln_b"], f32).reshape(4, 128).T
    cbf = np.concatenate([maskL, maskR, ident, bd2, onescolv,
                          slnb_c, selnb_c], axis=1)
    im["cbf_src"] = np.ascontiguousarray(cbf)

    gi = t0 + np.arange(NOWN)
    nwin = np.minimum(gi, WIN) + 1
    corr2 = (0.5 * (T - nwin) + 0.5 * T).astype(f32).reshape(NT, 128).T
    ib = np.asarray(inputs["L0_ffn_index_b"]).astype(np.int64)
    it = np.asarray(inputs["L0_ffn_index_t"]).astype(np.int64)
    oc = ib * 2 + (it // NOWN)
    fwd = np.full(NOWN, NSEL, np.int64)
    selm = np.zeros(NOWN, f32)
    for m in range(NSEL):
        if oc[m] == core:
            r = int(it[m] % NOWN)
            fwd[r] = m
            selm[r] = 1.0
    selmask = selm.reshape(NT, 128).T
    pairsel = np.zeros((128, 1), f32)
    pairsel[core, 0] = 1.0
    pairsel[core ^ 1, 0] = 1.0
    expbr = np.broadcast_to(
        np.asarray(inputs["L0_ffn_exp_bias"], f32)[None, :], (128, 8)).copy()
    slng_c = np.asarray(inputs["L0_ffn_s_ln_g"], f32).reshape(4, 128).T
    selng_c = np.asarray(inputs["L0_ffn_sel_ln_g"], f32).reshape(4, 128).T
    sb1c = np.asarray(inputs["L0_ffn_sb1"], f32).reshape(4, 128).T
    sb2c = np.asarray(inputs["L0_ffn_sb2"], f32).reshape(4, 128).T
    egbc = np.zeros((128, 1), f32)
    egbc[0:9, 0] = np.concatenate([np.asarray(inputs["L0_ffn_sg_b"], f32),
                                   np.asarray(inputs["L0_ffn_eg_b"], f32)])
    eb1c = np.asarray(inputs["L0_ffn_eb1"], f32).reshape(8, 4, 128)
    eb1c = eb1c.transpose(2, 0, 1).reshape(128, 32)
    eb2c = np.asarray(inputs["L0_ffn_eb2"], f32).reshape(8, 4, 128)
    eb2c = eb2c.transpose(2, 0, 1).reshape(128, 32)
    haloc = np.zeros((128, 4), f32)
    if half == 0:
        haloc[:] = -BIG
    cfm = np.concatenate([corr2, selmask, pairsel, expbr, slng_c, selng_c,
                          ident, sb1c, sb2c, egbc, eb1c, eb2c, haloc], axis=1)
    im["cf_src"] = np.ascontiguousarray(cfm)
    im["idxg"] = np.ascontiguousarray(np.concatenate(
        [fwd.reshape(NT, 128).T, fwd.reshape(NT, 128).T],
        axis=1).astype(np.int32))
    halocol = np.zeros((1, HALO), f32)
    if half == 0:
        halocol[:] = -BIG
    im["halocol"] = halocol
    bdsel = np.zeros((16, 1024), f32)
    for occ in range(8):
        for pp_ in range(128):
            bdsel[2 * occ + pp_ // 64, 128 * occ + pp_] = 1.0
    im["bdsel"] = bdsel
    bdsq = np.zeros((128, 8, 16), f32)
    for occ in range(8):
        for pp_ in range(128):
            bdsq[pp_, occ, 2 * occ + pp_ // 64] = 1.0
    im["bdsq"] = bdsq.reshape(128, 128)
    im["rmsrow"] = np.concatenate([
        np.asarray(inputs["L0_attn_rms_q_w"], f32),
        np.asarray(inputs["L0_attn_rms_k_w"], f32)]).reshape(1, 128)
    return im, br


CBF = {"maskL": 0, "maskR": 128, "identb": 256,
       "bd2": 384, "onescol": 386, "slnb": 387, "selnb": 391}
NCBF = 395
CF = {"corr2": 0, "selmask": 16, "pairsel": 32, "expbr": 33,
      "slng": 41, "selng": 45, "identf": 49,
      "sb1c": 177, "sb2c": 181, "egbc": 185, "eb1c": 186, "eb2c": 218,
      "haloc": 250}
NCF = 254


def build(br, debug=False, upto=9):
    nc = bass.Bass("TRN2", target_bir_lowering=False, debug=False)
    P = Prog()

    d_xT = nc.dram_tensor("xT", [D, NKEY], F32, kind="ExternalInput").ap()
    d_xres = nc.dram_tensor("xres", [NOWN, D], F32, kind="ExternalInput").ap()
    d_wq = nc.dram_tensor("wq", [D, D], F32, kind="ExternalInput").ap()
    d_wk = nc.dram_tensor("wk", [D, D], F32, kind="ExternalInput").ap()
    d_wv = nc.dram_tensor("wv", [D, 256], F32, kind="ExternalInput").ap()
    d_wo = nc.dram_tensor("wo", [D, D], F32, kind="ExternalInput").ap()
    d_sw1T = nc.dram_tensor("sw1T", [D, 512], F32, kind="ExternalInput").ap()
    d_sw2T = nc.dram_tensor("sw2T", [D, 512], F32, kind="ExternalInput").ap()
    d_sw3T = nc.dram_tensor("sw3T", [512, D], F32, kind="ExternalInput").ap()
    d_egwT = nc.dram_tensor("egwT", [D, 9], F32, kind="ExternalInput").ap()
    d_ew1T = nc.dram_tensor("ew1T", [8, D, 512], F32, kind="ExternalInput").ap()
    d_ew2T = nc.dram_tensor("ew2T", [8, D, 512], F32, kind="ExternalInput").ap()
    d_ew3T = nc.dram_tensor("ew3T", [8, 512, D], F32, kind="ExternalInput").ap()
    NB = sum(n for (_, n) in br.values())
    d_brows = nc.dram_tensor("brows", [1, NB], F32, kind="ExternalInput").ap()
    d_cbf = nc.dram_tensor("cbf_src", [128, NCBF], F32, kind="ExternalInput").ap()
    d_cf = nc.dram_tensor("cf_src", [128, NCF], F32, kind="ExternalInput").ap()
    d_idx = nc.dram_tensor("idxg", [128, 32], I32, kind="ExternalInput").ap()
    d_halocol = nc.dram_tensor("halocol", [1, HALO], F32, kind="ExternalInput").ap()
    d_rms = nc.dram_tensor("rmsrow", [1, 128], F32, kind="ExternalInput").ap()
    d_bdsel = nc.dram_tensor("bdsel", [16, 1024], F32, kind="ExternalInput").ap()
    d_bdsq = nc.dram_tensor("bdsq", [128, 128], F32, kind="ExternalInput").ap()
    d_y = nc.dram_tensor("y", [NOWN, D], F32, kind="ExternalOutput").ap()

    d_fwdx = nc.dram_tensor("fwd_x", [NSEL + 8, 1024], BF16).ap()
    d_fwds = nc.dram_tensor("fwd_s", [NSEL + 8, 16], BF16).ap()
    d_rsx = nc.dram_tensor("rs_x", [NBLK, 1024], BF16).ap()
    d_rss = nc.dram_tensor("rs_s", [NBLK, 16], BF16).ap()
    d_agin = nc.dram_tensor("ag_in", [NBLK, D], BF16).ap()
    d_agout = nc.dram_tensor("ag_out", [NSEL + 1, D], BF16).ap()
    d_denin = nc.dram_tensor("den_in", [1, 16], F32).ap()
    d_denout = nc.dram_tensor("den_out", [8, 16], F32).ap()
    dbg = {}
    if debug:
        for nm, shp, dt in [
                ("dbg_qhat", [128, 8 * NOWN], BF16), ("dbg_khat", [128, 8 * NKEY], BF16),
                ("dbg_vhat", [128, NKT * 260], BF16), ("dbg_sigs", [128, NT * 16], F32),
                ("dbg_recden", [128, NT * 16], F32), ("dbg_c2", [16, NOWN], BF16),
                ("dbg_ahat", [128, 8 * NOWN], BF16), ("dbg_strm", [128, NT * D], F32),
                ("dbg_wsm", [128, NT], F32), ("dbg_zhat", [128, 8 * NOWN], BF16),
                ("dbg_sheb", [128, 4 * NOWN], BF16), ("dbg_eacc", [128, 2 * D], F32),
                ("dbg_s2", [128, NT * D], F32)]:
            dbg[nm] = nc.dram_tensor(nm, shp, dt, kind="ExternalOutput").ap()

    SB0 = 16512
    TOP = 229344
    R1 = SB0
    R2 = R1 + 40960
    R3 = R2 + 73728
    R4 = R3 + 10432
    R5 = R4 + 32768
    R6 = R5 + 16384
    SM = R6 + 26624
    A = {}

    def mk(name, shape, dtype, at):
        A[name] = nc.alloc_sbuf_tensor_at(name, list(shape), dtype, offset=at)
        return A[name]

    xhat = mk("xhat", [128, 8, NKEY], BF16, R1)
    ahat = mk("ahat", [128, 8, NOWN], BF16, R1)
    zeb = mk("zeb", [128, 4, NOWN], BF16, R1)
    sheb = mk("sheb", [128, 4, NOWN], BF16, R1 + 16384)

    qhat = mk("qhat", [128, 8, NOWN], BF16, R2)
    khat = mk("khat", [128, 8, NKEY], BF16, R2 + 32768)
    strm = mk("strm", [128, NT, D], F32, R2)
    xring = mk("xring", [128, 2, D], F32, R2 + 65536)

    vhat = mk("vhat", [128, NKT, 260], BF16, R3)
    xselfm = mk("xselfm", [128, 8, NBLK], BF16, R3)
    zebm = mk("zebm", [128, 4, NBLK], BF16, R3 + 4096)
    shebm = mk("shebm", [128, 4, NBLK], BF16, R3 + 6144)
    zebm2 = mk("zebm2", [128, 4, NBLK], BF16, R3 + 8192)

    sumsqK = mk("sumsqK", [16, NKEY], F32, R4)
    invkb = mk("invkb", [16, NKEY], BF16, R4 + 10240)
    sumsqQ = mk("sumsqQ", [16, NOWN], F32, R4 + 15360)
    sigP = mk("sigP", [128, 2, 640], BF16, R4 + 23552)
    c2fm = mk("c2fm", [16, NOWN], BF16, R4 + 26112)
    zhatfm = mk("zhatfm", [128, 8, NOWN], BF16, R4)
    ewS = [
        (mk("ew1S0", [128, 8, 512], BF16, R4),
         mk("ew2S0", [128, 8, 512], BF16, R4 + 8192),
         mk("ew3S0", [128, 4, 1024], BF16, R2 + 65536)),
        (mk("ew1S1", [128, 8, 512], BF16, R4 + 16384),
         mk("ew2S1", [128, 8, 512], BF16, R4 + 24576),
         mk("ew3S1", [128, 4, 1024], BF16, R1 + 32768)),
    ]

    wsb = mk("wsb", [128, 8, 1024], BF16, R5)
    sw1sb = mk("sw1sb", [128, 8, 512], BF16, R5)
    sw2sb = mk("sw2sb", [128, 8, 512], BF16, R5 + 8192)
    sw3sb = mk("sw3sb", [128, 4, 1024], BF16, R5)
    wvv = wsb[:].rearrange("p a o -> p (a o)")[:, 0:2048].rearrange(
        "p (dc o) -> p dc o", o=256)

    repscr = mk("repscr", [64, 512], F32, R6)
    sigs3 = mk("sigs3", [128, NT, 16], F32, R6 + 2048)
    recden = mk("recden", [128, NT, 16], F32, R6 + 3072)
    invq8 = mk("invq8", [128, NT, 16], F32, R6 + 4096)
    gring = mk("gring", [9, 2, 512], F32, R6)
    z1ring = mk("z1ring", [128, 2, 1040], BF16, R6 + 4160)
    xsel = mk("xsel", [128, 2, 1024], BF16, R6)
    xsels = mk("xsels", [128, 2, 16], BF16, R6 + 4128)
    indf = mk("indf", [128, 2, 8], F32, R6 + 4192)
    weff = mk("weff", [128, 2, 1], F32, R6 + 4256)
    frows = mk("frows", [1, 3, 512], F32, R6 + 8192)
    eacc = mk("eacc", [128, 2, D], F32, R5)
    otile = mk("otile", [128, 2, D], F32, R5)
    ssping = mk("ssping", [128, 2, 512], BF16, R6 + 16384)
    sqdummy = mk("sqdummy", [128, 1024], BF16, R6 + 16384)
    zz = mk("zz", [128, 1040], BF16, R6 + 16384)
    sqscr = mk("sqscr", [128, 2, 512], BF16, R6 + 18496)
    bias3e = mk("bias3e", [1, 1024], BF16, R6 + 20544)
    obounce = mk("obounce", [64, 2, 512], BF16, R6 + 20544)
    invrowb = mk("invrowb", [1, 512], BF16, R6 + 22592)
    minrowb = mk("minrowb", [1, 512], BF16, R6 + 23616)
    rscr = mk("rscr", [1, 2, NOWN], BF16, R4)
    rrall = mk("rrall", [16, NOWN], BF16, R4 + 8192)
    rrallb = mk("rrallb", [16, NOWN], BF16, R4 + 12288)
    rrinv = mk("rrinv", [16, NOWN], F32, R4)
    ttscr = mk("ttscr", [128, 512], BF16, R6 + 24640)
    gstt = mk("gstt", [128, 2, 12], F32, R6 + 25664)
    ln2c = mk("ln2c", [128, 6, 16], F32, R6 + 25760)
    retbuf = mk("retbuf", [128, 2, D], BF16, R6 + 4160)

    cur = [SM]

    def sbal(name, shape, dtype):
        at = cur[0]
        nb = int(np.prod(shape[1:])) * mybir.dt.size(dtype)
        cur[0] = (at + nb + 31) & ~31
        return mk(name, shape, dtype, at)

    cbf = sbal("cbf", [128, NCBF], BF16)
    cf = sbal("cf", [128, NCF], F32)
    idxs = sbal("idxs", [128, 32], I32)
    browsb = sbal("browsb", [1, 1024], BF16)
    brows2 = sbal("brows2", [1, 1024], BF16)
    onesrow = sbal("onesrow", [1, 640], BF16)
    halocolb = sbal("halocolb", [1, HALO], BF16)
    bdsel = sbal("bdsel", [16, 8, 128], BF16)
    wqwkrow2 = sbal("wqwkrow2", [1, 128], F32)
    wqwkcol = sbal("wqwkcol", [128, 1], F32)
    rmsq = sbal("rmsq", [1, 64], F32)
    rmsk = sbal("rmsk", [1, 64], F32)
    wqwk = sbal("wqwk", [1, 64], F32)
    pk = [R6 + 14336]

    def pkmk(name, shape, dtype):
        at = pk[0]
        nb = int(np.prod(shape[1:])) * mybir.dt.size(dtype)
        pk[0] = (at + nb + 31) & ~31
        assert pk[0] <= R6 + 16384, name
        return mk(name, shape, dtype, at)

    wsmcol = pkmk("wsmcol", [128, NT], F32)
    scr8 = pkmk("scr8", [128, 8], F32)
    scrsm = pkmk("scrsm", [128, 8], F32)
    mxcol = pkmk("mxcol", [128, 1], F32)
    t1col = pkmk("t1col", [128, 1], F32)
    wst = pkmk("wst", [128, 2], F32)
    dcol = pkmk("dcol", [128, 2], F32)
    sidescr = pkmk("sidescr", [128, 2, 16], BF16)
    densb = pkmk("densb", [8, 16], F32)
    denrow = pkmk("denrow", [1, 16], F32)
    sminv = pkmk("sminv", [16, 1], F32)
    rsumrep = pkmk("rsumrep", [128, 16], F32)
    c2scr = pkmk("c2scr", [128, 16], F32)
    lncol = pkmk("lncol", [128, 8], F32)
    onescf = pkmk("onescf", [128, 1], F32)
    cpool = pkmk("cpool", [128, 5], F32)
    bdsq = pkmk("bdsq", [128, 8, 16], BF16)
    rsrep1 = pkmk("rsrep1", [16, 128], F32)
    egwsb = pkmk("egwsb", [128, 8, 9], BF16)
    assert cur[0] <= TOP, f"SBUF overflow {cur[0]} > {TOP}"

    psum_all = nc.alloc_psum_tensor("psum_all", [128, 4096], F32)
    assert nc.psum_banks_remaining == 0
    PSB = [psum_all[:, 512 * i:512 * i + 512] for i in range(8)]
    psA = psum_all[:, 0:640]
    psB = psum_all[:, 1024:1664]
    psG = psum_all[:, 2048:2688]
    psC = PSB[0]
    psD = PSB[2]
    psE = PSB[4]
    psF = PSB[6]
    psG0 = PSB[1]
    SRING = [psA, psB, psG]

    sy = P.sem("s_sy", 16)
    sg = P.sem("s_sg", 16)
    ssd = P.sem("s_ssd", 16)
    syd = P.sem("s_syd", 1)
    sgd = P.sem("s_sgd", 1)
    ssdd = P.sem("s_ssdd", 1)

    def ybar():
        P.op("sync", "nop", waits=((sy, sy.n),), inc=syd.inc())
        return syd.tok()

    def gbar():
        P.op("gpsimd", "nop", waits=((sg, sg.n),), inc=sgd.inc())
        return sgd.tok()

    def sbar():
        P.op("scalar", "nop", waits=((ssd, ssd.n),), inc=ssdd.inc())
        return ssdd.tok()

    mm = P.sem("s_mm", 1)
    sc = P.sem("s_sc", 1)
    vc = P.sem("s_vc", 1)
    cc = P.sem("s_cc", 1)
    P.autobar = {"vector": vc, "scalar": sc}

    MASKL = cbf[:, CBF["maskL"]:CBF["maskL"] + 128]
    MASKR = cbf[:, CBF["maskR"]:CBF["maskR"] + 128]
    IDENTB = cbf[:, CBF["identb"]:CBF["identb"] + 128]
    BD2 = cbf[:, CBF["bd2"]:CBF["bd2"] + 2]
    ONESCB = cbf[:, CBF["onescol"]:CBF["onescol"] + 1]
    IDENTF = cf[:, CF["identf"]:CF["identf"] + 128]

    def cfc(name, o, n):
        return cf[:, CF[name] + o:CF[name] + o + n]

    def cbfc(name, o, n):
        return cbf[:, CBF[name] + o:CBF[name] + o + n]

    P.op("gpsimd", "dma_start", out=cbf[:], in_=d_cbf[:, :], inc=sg.inc())
    P.op("sync", "dma_start", out=cf[:], in_=d_cf[:, :], inc=sy.inc())
    P.op("sync", "dma_start", out=idxs[:], in_=d_idx[:, :], inc=sy.inc())
    P.op("sync", "dma_start", out=rmsq[:], in_=d_rms[0:1, 0:64], inc=sy.inc())
    P.op("sync", "dma_start", out=rmsk[:], in_=d_rms[0:1, 64:128], inc=sy.inc())
    P.op("gpsimd", "dma_start", out=halocolb[:], in_=d_halocol[:, :], inc=sg.inc())
    obq, nbq = br["bq"]
    P.op("gpsimd", "dma_start", out=browsb[0:1, 0:1024],
         in_=d_brows[0:1, obq:obq + nbq], inc=sg.inc())
    obk, nbk = br["bk"]
    P.op("gpsimd", "dma_start", out=brows2[0:1, 0:1024],
         in_=d_brows[0:1, obk:obk + nbk], inc=sg.inc())
    P.op("gpsimd", "dma_start", out=egwsb[:],
         in_=d_egwT[:, :].rearrange("(dc p) e -> p dc e", p=128), inc=sg.inc())
    P.op("gpsimd", "dma_start", out=bdsel[:],
         in_=d_bdsel[:, :].rearrange("k (oc p) -> k oc p", p=128), inc=sg.inc())
    P.op("gpsimd", "dma_start", out=bdsq[:],
         in_=d_bdsq[:, :].rearrange("p (oc j) -> p oc j", j=16), inc=sg.inc())
    xTfm = d_xT[:, :].rearrange("(dc p) t -> p dc t", p=128)
    P.op("gpsimd", "dma_start", out=wsb[:],
         in_=d_wq[:, :].rearrange("(dc p) o -> p dc o", p=128), inc=sg.inc())
    xtok = []
    for (xa, xb) in [(512, 1024), (1024, 1536), (1536, 2048), (2048, 2560),
                     (0, 512)]:
        P.op("gpsimd", "dma_start", out=xhat[:, :, xa:xb], in_=xTfm[:, :, xa:xb],
             inc=sg.inc())
        xtok.append(gbar())
    g_loads = xtok[3]
    y_loads = ybar()

    P.op("vector", "memset", onesrow[:], 1.0, inc=vc.inc())
    P.op("vector", "memset", onescf[:], 1.0, inc=vc.inc())
    P.op("vector", "memset", zz[:], 0.0, inc=vc.inc())
    P.op("vector", "memset", sidescr[:], 0.0, inc=vc.inc())
    P.op("vector", "memset",
         vhat[:].rearrange("p t (kv c) -> p t kv c", c=65)[:, :, :, 64:65],
         1.0, inc=vc.inc())
    P.op("vector", "memset", cpool[:, 0:1], RMS_EPS, inc=vc.inc())
    P.op("vector", "memset", cpool[:, 1:2], LN_EPS, inc=vc.inc())
    P.op("vector", "memset", cpool[:, 2:3], 1.0 / 64.0, inc=vc.inc())
    P.op("vector", "memset", cpool[:, 3:4], 0.125, inc=vc.inc())
    P.op("vector", "memset", cpool[:, 4:5], 1.0 / 512.0, inc=vc.inc())
    v_cpool = vc.tok()
    v_init = vc.tok()
    for i in range(16):
        P.op("gpsimd", "dma_start", out=d_fwdx[128 * i:128 * i + 128, :],
             in_=zz[:, 0:1024], waits=(v_init,) if i == 0 else (), inc=sg.inc())
    P.op("gpsimd", "dma_start", out=d_fwdx[2048:2056, :], in_=zz[0:8, 0:1024],
         inc=sg.inc())
    P.op("gpsimd", "dma_start",
         out=d_fwds[0:2048, :].rearrange("(a p) c -> p a c", p=128),
         in_=zz[:, 0:256].rearrange("p (a c) -> p a c", c=16), inc=sg.inc())
    P.op("gpsimd", "dma_start", out=d_fwds[2048:2056, :], in_=zz[0:8, 0:16],
         inc=sg.inc())
    P.op("gpsimd", "dma_start", out=d_agout[NSEL:NSEL + 1, :], in_=zz[0:1, 0:1024],
         inc=sg.inc())
    g_zf = gbar()

    P.op("vector", "tensor_mul", wqwk[:], rmsq[:], rmsk[:],
         waits=(y_loads,), inc=vc.inc())
    v_ww = vc.tok()
    P.op("gpsimd", "dma_start", out=wqwkrow2[0:1, 0:64], in_=wqwk[:],
         waits=(v_ww,), inc=sg.inc())
    P.op("gpsimd", "dma_start", out=wqwkrow2[0:1, 64:128], in_=wqwk[:],
         inc=sg.inc())
    g_ww = gbar()
    g_bdw = gbar()

    dbg_bar1 = dbg_bar2 = dbg_bar3 = dbg_bar4 = dbg_bar5 = None

    def finish_early():
        P.op("vector", "memset", otile[:, 0, :], 0.0, inc=vc.inc())
        vv = vc.tok()
        for t_ in range(NT):
            P.op("sync", "dma_start", out=d_y[128 * t_:128 * t_ + 128, :],
                 in_=otile[:, 0, :], waits=(vv,) if t_ == 0 else (),
                 inc=sy.inc())
        ybar()
        return _emit(nc, P)

    if upto <= 0:
        return finish_early()

    def proj_fm(wget, biasrow, tok_cols, out_write, sumsq_write, prewaits,
                tc_waits=None):
        last = {}
        pe_ev = {}
        it = 0
        for tci, (tc0, tn) in enumerate(tok_cols):
            pe = psE if tci % 2 == 0 else psF
            for oc in range(8):
                pp = psC if it % 2 == 0 else psD
                sq_ = sqscr[:, it % 2, 0:tn]
                tw = list(prewaits) if it < 2 else list(last[it - 2])
                if tc_waits and oc == 0 and tci in tc_waits:
                    tw = tw + [tc_waits[tci]]
                for dc in range(8):
                    P.op("tensor", "matmul", pp[:, 0:tn],
                         wget(dc, oc), xhat[:, dc, tc0:tc0 + tn],
                         start=(dc == 0), stop=False,
                         waits=tw if dc == 0 else ())
                P.op("tensor", "matmul", pp[:, 0:tn],
                     biasrow(oc), onesrow[0:1, 0:tn],
                     start=False, stop=True, inc=mm.inc())
                t_mm = mm.tok()
                P.op("scalar", "activation", out=sq_, in_=pp[:, 0:tn],
                     func=AF.Square, waits=(t_mm,), inc=sc.inc())
                s_sq = sc.tok()
                P.op("scalar", "copy", out=out_write(oc, tc0, tn),
                     in_=pp[:, 0:tn], inc=sc.inc())
                v_ev = sc.tok()
                ssw = [s_sq]
                if oc == 0 and tci >= 2:
                    ssw.append(pe_ev[tci - 2])
                P.op("tensor", "matmul", pe[0:16, 0:tn], bdsq[:, oc, :], sq_,
                     start=(oc == 0), stop=(oc == 7), waits=ssw,
                     inc=mm.inc() if oc == 7 else None)
                last[it] = (s_sq, v_ev)
                it += 1
            t_ss = mm.tok()
            P.op("scalar", "copy", out=sumsq_write(tc0, tn),
                 in_=pe[0:16, 0:tn], waits=(t_ss,), inc=sc.inc())
            pe_ev[tci] = sc.tok()
        return (sc.tok(), vc.tok())

    own_cols = [(HALO + 512 * t, 512) for t in range(4)]
    all_cols = [(512 * t, 512) for t in range(5)]

    qtail = proj_fm(
        lambda dc, oc: wsb[:, dc, 128 * oc:128 * oc + 128],
        lambda oc: browsb[0:1, 128 * oc:128 * oc + 128],
        own_cols,
        lambda oc, tc0, tn: qhat[:, oc, tc0 - HALO:tc0 - HALO + tn],
        lambda tc0, tn: sumsqQ[0:16, tc0 - HALO:tc0 - HALO + tn],
        [v_init, xtok[0]],
        tc_waits={1: xtok[1], 2: xtok[2], 3: xtok[3]})
    if upto <= 1.1:
        return finish_early()
    P.op("tensor", "transpose", psG0[0:128, 0:1], wqwkrow2[:, :],
         IDENTF[0:1, 0:1], waits=(g_ww,), inc=mm.inc())
    t_ww = mm.tok()
    P.op("scalar", "copy", out=wqwkcol[:], in_=psG0[0:128, 0:1], waits=(t_ww,),
         inc=sc.inc())
    s_wwc = sc.tok()
    q_mm_done = mm.tok()
    P.op("gpsimd", "dma_start", out=wsb[:],
         in_=d_wk[:, :].rearrange("(dc p) o -> p dc o", p=128),
         waits=(q_mm_done,), inc=sg.inc())
    g_wk = gbar()
    ktail = proj_fm(
        lambda dc, oc: wsb[:, dc, 128 * oc:128 * oc + 128],
        lambda oc: brows2[0:1, 128 * oc:128 * oc + 128],
        all_cols,
        lambda oc, tc0, tn: khat[:, oc, tc0:tc0 + tn],
        lambda tc0, tn: sumsqK[0:16, tc0:tc0 + tn],
        [g_wk, qtail[0], qtail[1]])
    if upto <= 1.2:
        return finish_early()
    k_mm_done = mm.tok()
    P.op("gpsimd", "dma_start", out=wvv,
         in_=d_wv[:, :].rearrange("(dc p) o -> p dc o", p=128),
         waits=(k_mm_done,), inc=sg.inc())
    obv, nbv = br["bv"]
    P.op("gpsimd", "dma_start", out=browsb[0:1, 0:256],
         in_=d_brows[0:1, obv:obv + nbv], inc=sg.inc())
    g_wv = gbar()

    if upto <= 1.3:
        return finish_early()
    q_sq_all = qtail[0]
    P.op("scalar", "activation", out=sumsqQ[:, :], in_=sumsqQ[:, :], func=AF.Sqrt,
         scale=cpool[0:16, 2:3], bias=cpool[0:16, 0:1],
         waits=(v_cpool, sc.tok()), inc=sc.inc())
    P.op("scalar", "activation", out=sumsqK[:, :], in_=sumsqK[:, :], func=AF.Sqrt,
         scale=cpool[0:16, 2:3], bias=cpool[0:16, 0:1], inc=sc.inc())
    s_sqrt = sc.tok()
    P.op("vector", "reciprocal", out=sumsqQ[:, :], in_=sumsqQ[:, :],
         waits=(s_sqrt,), inc=vc.inc())
    P.op("vector", "reciprocal", out=sumsqK[:, :], in_=sumsqK[:, :], inc=vc.inc())
    P.op("vector", "tensor_copy", out=invkb[:, :], in_=sumsqK[:, :],
         waits=(vc.tok(),), inc=vc.inc())
    v_inv = vc.tok()
    invq8_ev = {}
    for t in range(NT):
        pe = psE if t % 2 == 0 else psF
        tw = [v_inv] if t < 2 else [invq8_ev[t - 2]]
        P.op("tensor", "transpose", pe[0:128, 0:16],
             sumsqQ[0:16, 128 * t:128 * t + 128], IDENTF[0:16, 0:16],
             waits=tw, inc=mm.inc())
        t_tr = mm.tok()
        P.op("scalar", "activation", out=invq8[:, t, :], in_=pe[0:128, 0:16],
             func=AF.Copy, scale=cpool[:, 3:4], waits=(t_tr, v_cpool),
             inc=sc.inc())
        invq8_ev[t] = sc.tok()
    s_invq8 = sc.tok()

    if upto <= 1.4:
        return finish_early()
    klast = {}
    it = 0
    for oc in range(8):
        for tc in range(5):
            pp = psC if it % 2 == 0 else psD
            tw = [v_inv, g_bdw] if it < 2 else [klast[it - 2]]
            P.op("tensor", "matmul", pp[:, :], bdsel[:, oc, :],
                 invkb[:, 512 * tc:512 * tc + 512],
                 start=True, stop=True, waits=tw, inc=mm.inc())
            t_rep = mm.tok()
            P.op("vector", "scalar_tensor_tensor",
                 out=khat[:, oc, 512 * tc:512 * tc + 512],
                 in0=khat[:, oc, 512 * tc:512 * tc + 512], scalar=wqwkcol[:],
                 in1=pp[:, :], op0=ALU.mult, op1=ALU.mult,
                 waits=(t_rep, s_wwc), inc=vc.inc())
            klast[it] = vc.tok()
            it += 1
    v_khat = vc.tok()

    if debug:
        P.op("sync", "dma_start", out=dbg["dbg_qhat"][:, :],
             in_=qhat[:].rearrange("p a t -> p (a t)"), waits=(v_done,), inc=sy.inc())
        P.op("sync", "dma_start", out=dbg["dbg_khat"][:, :],
             in_=khat[:].rearrange("p a t -> p (a t)"), waits=(v_khat,), inc=sy.inc())
        P.op("sync", "dma_start", out=dbg["dbg_vhat"][:, :],
             in_=vhat[:].rearrange("p a t -> p (a t)"), waits=(v_done,), inc=sy.inc())
    dbg_bar1 = ybar() if debug else None

    if upto <= 2:
        return finish_early()

    slot_last = [None, None, None]
    rd_tok = {}
    r = 0
    it = 0
    for qt in range(NT):
        k0 = 128 * qt
        lo0 = max(0, 512 - 128 * qt)
        for hp in range(8):
            sS = [SRING[r], SRING[(r + 1) % 3]]
            t_mm = [None, None]
            tws = []
            for e in (0, 1):
                tw = list([v_khat, s_invq8] if it == 0 else [])
                if slot_last[(r + e) % 3] is not None:
                    tw.append(slot_last[(r + e) % 3])
                tws.append(tw)
            if lo0 < 512:
                for e in (0, 1):
                    hb = 64 * e
                    P.op("tensor", "matmul", sS[e][:, lo0:512],
                         qhat[hb:hb + 64, hp, 128 * qt:128 * qt + 128],
                         khat[hb:hb + 64, hp, k0 + lo0:k0 + 512],
                         start=True, stop=(qt < 4), waits=tws[e])
                    tws[e] = []
            for e in (0, 1):
                hb = 64 * e
                P.op("tensor", "matmul", sS[e][:, 512:640],
                     qhat[hb:hb + 64, hp, 128 * qt:128 * qt + 128],
                     khat[hb:hb + 64, hp, k0 + 512:k0 + 640],
                     start=True, stop=False, waits=tws[e])
            for e in (0, 1):
                sA = sS[e]
                if qt >= 4:
                    P.op("tensor", "matmul", sA[:, 0:128], IDENTB, MASKL,
                         start=False, stop=True)
                P.op("tensor", "matmul", sA[:, 512:640], IDENTB, MASKR,
                     start=False, stop=True, inc=mm.inc())
                t_mm[e] = mm.tok()
            for e in (0, 1):
                h = 2 * hp + e
                P.op("scalar", "activation", out=sigP[:, e, 0:640 - lo0],
                     in_=sS[e][:, lo0:640],
                     func=AF.Sigmoid, scale=invq8[:, qt, h:h + 1],
                     accum_out=sigs3[:, qt, h:h + 1],
                     waits=(t_mm[e], s_invq8) if it == 0 else (t_mm[e],),
                     inc=sc.inc())
                slot_last[(r + e) % 3] = sc.tok()
            if hp == 7:
                last_sig = sc.tok()
                P.op("vector", "tensor_scalar", out=recden[:, qt, :],
                     in0=sigs3[:, qt, :], scalar1=cfc("corr2", qt, 1),
                     scalar2=1.0 / T, op0=ALU.add, op1=ALU.mult,
                     waits=(last_sig,), inc=vc.inc())
                P.op("vector", "reciprocal", out=recden[:, qt, :],
                     in_=recden[:, qt, :], inc=vc.inc())
                rd_tok[qt] = vc.tok()
            if hp == 3 and qt >= 1:
                P.op("tensor", "matmul", PSB[7][0:1, 0:16], onescf[:],
                     recden[:, qt - 1, :], start=(qt == 1), stop=False,
                     waits=(rd_tok[qt - 1],))
            r = (r + 2) % 3
            it += 1
    P.op("tensor", "matmul", PSB[7][0:1, 0:16], onescf[:],
         recden[:, NT - 1, :], start=False, stop=True,
         waits=(rd_tok[NT - 1],), inc=mm.inc())
    t_part = mm.tok()
    s_pass1 = sc.tok()
    if debug:
        P.op("sync", "dma_start", out=dbg["dbg_sigs"][:, :],
             in_=sigs3[:].rearrange("p a t -> p (a t)"), waits=(s_pass1,), inc=sy.inc())

    vlast = {}
    for t in range(NKT):
        pp = psC if t % 2 == 0 else psD
        tw = [g_wv, s_pass1] if t < 2 else [vlast[t - 2]]
        for dc in range(8):
            P.op("tensor", "matmul", pp[:, 0:256],
                 xhat[:, dc, 128 * t:128 * t + 128], wvv[:, dc, :],
                 start=(dc == 0), stop=False, waits=tw if dc == 0 else ())
        P.op("tensor", "matmul", pp[:, 0:256], onesrow[0:1, 0:128],
             browsb[0:1, 0:256], start=False, stop=True, inc=mm.inc())
        t_mm = mm.tok()
        P.op("vector", "tensor_copy",
             out=vhat[:, t, :].rearrange("p (kv c) -> p kv c", c=65)[:, :, 0:64],
             in_=pp[:, 0:256].rearrange("p (kv c) -> p kv c", c=64),
             waits=(t_mm,), inc=vc.inc())
        vlast[t] = vc.tok()
    v_done = vc.tok()
    v_mm_done = mm.tok()
    P.op("gpsimd", "dma_start", out=wsb[:],
         in_=d_wo[:, :].rearrange("(dc p) o -> p dc o", p=128),
         waits=(v_mm_done,), inc=sg.inc())
    obo, nbo = br["bo"]
    P.op("gpsimd", "dma_start", out=brows2[0:1, 0:1024],
         in_=d_brows[0:1, obo:obo + nbo], inc=sg.inc())
    g_wo = gbar()

    P.op("scalar", "copy", out=denrow[:], in_=PSB[7][0:1, 0:16], waits=(t_part,),
         inc=sc.inc())
    s_den = sc.tok()
    P.op("sync", "dma_start", out=d_denin[:, :], in_=denrow[:], waits=(s_den,),
         inc=sy.inc())
    g_den = ybar()
    P.op("gpsimd", "collective_compute", "AllGather", ALU.bypass,
         replica_groups=[list(range(8))], ins=[d_denin.opt()],
         outs=[d_denout.opt()], waits=(g_den,), inc=cc.inc())
    c_den = cc.tok()
    P.op("sync", "dma_start", out=densb[:], in_=d_denout[:, :], waits=(c_den,),
         inc=sy.inc())
    g_dsb = ybar()
    P.op("tensor", "matmul", psF[0:16, 0:1], densb[:, :],
         cfc("pairsel", 0, 1)[0:8, :], start=True, stop=True, waits=(g_dsb, s_den),
         inc=mm.inc())
    t_sminv = mm.tok()
    P.op("vector", "reciprocal", out=sminv[:], in_=psF[0:16, 0:1],
         waits=(t_sminv,), inc=vc.inc())
    v_r1 = vc.tok()
    P.op("vector", "tensor_copy", out=rsrep1[:, :],
         in_=sminv[:].broadcast_to([16, 128]), waits=(v_r1,), inc=vc.inc())
    v_rs1 = vc.tok()
    P.op("tensor", "transpose", psF[0:128, 0:16], rsrep1[:, :],
         IDENTF[0:16, 0:16], waits=(v_rs1,), inc=mm.inc())
    t_rs = mm.tok()
    P.op("scalar", "copy", out=rsumrep[:], in_=psF[0:128, 0:16], waits=(t_rs,),
         inc=sc.inc())
    s_rs = sc.tok()
    P.op("vector", "tensor_tensor",
         out=recden[:].rearrange("p a t -> p (a t)"),
         in0=recden[:].rearrange("p a t -> p (a t)"),
         in1=invq8[:].rearrange("p a t -> p (a t)"), op=ALU.mult,
         waits=(s_invq8,), inc=vc.inc())
    P.op("vector", "tensor_tensor", out=recden[:, :, :],
         in0=recden[:, :, :],
         in1=rsumrep[:].rearrange("p (a t) -> p a t", a=1).broadcast_to(
             [128, NT, 16]),
         op=ALU.mult, waits=(s_rs,), inc=vc.inc())
    v_c2all = vc.tok()
    c2ev = {}
    for qt in range(NT):
        pe = psE if qt % 2 == 0 else psF
        P.op("tensor", "transpose", pe[0:16, 0:128], recden[:, qt, :], IDENTF,
             waits=((v_c2all,) if qt < 2 else (c2ev[qt - 2],)), inc=mm.inc())
        t_c2 = mm.tok()
        P.op("scalar", "copy", out=c2fm[0:16, 128 * qt:128 * qt + 128],
             in_=pe[0:16, 0:128], waits=(t_c2,), inc=sc.inc())
        c2ev[qt] = sc.tok()
    s_c2 = sc.tok()
    if debug:
        P.op("sync", "dma_start", out=dbg["dbg_recden"][:, :],
             in_=recden[:].rearrange("p a t -> p (a t)"), waits=(v_rec,), inc=sy.inc())
        P.op("sync", "dma_start", out=dbg["dbg_c2"][:, :], in_=c2fm[:, :],
             waits=(s_c2,), inc=sy.inc())

    q2last = {}
    it = 0
    for oc in range(8):
        for tc in range(4):
            pp = psC if it % 2 == 0 else psD
            tw = ([s_c2, vlast[NKT - 2], vlast[NKT - 1]]
                  if it < 2 else [q2last[it - 2]])
            if debug and it == 0:
                tw.append(dbg_bar1)
            P.op("tensor", "matmul", pp[:, :], bdsel[:, oc, :],
                 c2fm[:, 512 * tc:512 * tc + 512],
                 start=True, stop=True, waits=tw, inc=mm.inc())
            t_rep = mm.tok()
            P.op("vector", "tensor_tensor",
                 out=qhat[:, oc, 512 * tc:512 * tc + 512],
                 in0=qhat[:, oc, 512 * tc:512 * tc + 512], in1=pp[:, :],
                 op=ALU.mult, waits=(t_rep,), inc=vc.inc())
            q2last[it] = vc.tok()
            it += 1
    v_q2 = vc.tok()

    if upto <= 3:
        return finish_early()

    spans = {}
    kt_of_qc = {qc: [] for qc in range(4)}
    for kt in range(NKT):
        qlo = max(0, 128 * kt - 512)
        qhi = min(NOWN, 128 * kt + 128)
        spans[kt] = (qlo, qhi)
        for qc in range(4):
            if qlo < 512 * qc + 512 and qhi > 512 * qc:
                kt_of_qc[qc].append(kt)
    first_kt = {qc: ks[0] for qc, ks in kt_of_qc.items()}
    last_kt = {qc: ks[-1] for qc, ks in kt_of_qc.items()}

    ZROW = cbf[0:1, CBF["maskL"]:CBF["maskL"] + 128]
    SS = {0: psum_all[:, 0:640], 1: psum_all[:, 3072:3712]}
    OBK = {(0, 0): PSB[2], (0, 1): PSB[3], (1, 0): PSB[4], (1, 1): PSB[5]}
    rs_dma = {0: None, 1: None}
    ob_dma = {0: None, 1: None}
    explast = {}
    ev_tok = {}
    for hp in range(8):
        kv = hp // 2
        for kt in range(NKT):
            qlo, qhi = spans[kt]
            w = qhi - qlo
            n0 = min(w, 512)
            for eo in (0, 1):
                h = 2 * hp + eo
                hb = 64 * eo
                if hp == 0 and kt == 0:
                    tw = [v_q2, v_done]
                elif kt == 0:
                    tw = [explast[h - 2]]
                else:
                    tw = [explast[h]]
                P.op("tensor", "matmul", SS[eo][:, 0:n0],
                     khat[hb:hb + 64, hp, 128 * kt:128 * kt + 128],
                     qhat[hb:hb + 64, hp, qlo:qlo + n0],
                     start=True, stop=False, waits=tw)
            if w > 512:
                for eo in (0, 1):
                    hb = 64 * eo
                    P.op("tensor", "matmul", SS[eo][:, 512:w],
                         khat[hb:hb + 64, hp, 128 * kt:128 * kt + 128],
                         qhat[hb:hb + 64, hp, qlo + 512:qlo + w],
                         start=True, stop=False)
            t_mm = {}
            for eo in (0, 1):
                sA = SS[eo]
                b0open = True
                mr_in_b0 = (kt <= 15) and (w <= 512)
                ml = (kt >= 4)
                if ml and not mr_in_b0:
                    P.op("tensor", "matmul", sA[:, 0:128], IDENTB, MASKL,
                         start=False, stop=True)
                    b0open = False
                elif ml and mr_in_b0:
                    P.op("tensor", "matmul", sA[:, 0:128], IDENTB, MASKL,
                         start=False, stop=False)
                if mr_in_b0:
                    P.op("tensor", "matmul", sA[:, w - 128:w], IDENTB, MASKR,
                         start=False, stop=True)
                    b0open = False
                if b0open:
                    P.op("tensor", "matmul", sA[:, 0:n0],
                         halocolb[0:1, 0:128], onesrow[0:1, 0:n0],
                         start=False, stop=True)
                if w > 512:
                    P.op("tensor", "matmul", sA[:, 512:w], IDENTB, MASKR,
                         start=False, stop=True)
                t_mm[eo] = P.tag_last("tensor", mm)
            for qc in range(4):
                if kt != first_kt[qc]:
                    continue
                for eo in (0, 1):
                    h = 2 * hp + eo
                    prev = (h, qc - 2) if qc >= 2 else (h - 2, qc + 2)
                    zw = [] if prev in ev_tok else [v_q2, v_done]
                    if prev in ev_tok:
                        zw.append(ev_tok[prev])
                    P.op("tensor", "matmul", OBK[(eo, qc % 2)][0:65, 0:512],
                         ZROW[0:1, 0:65], onesrow[0:1, 0:512],
                         start=True, stop=False, waits=zw)
            for eo in (0, 1):
                h = 2 * hp + eo
                if kt < 4:
                    P.op("scalar", "activation", out=sigP[:, eo, 0:w],
                         in_=SS[eo][:, 0:w], func=AF.Exp,
                         bias=cfc("haloc", kt, 1),
                         waits=(t_mm[eo],), inc=sc.inc())
                else:
                    P.op("scalar", "activation", out=sigP[:, eo, 0:w],
                         in_=SS[eo][:, 0:w], func=AF.Exp,
                         waits=(t_mm[eo],), inc=sc.inc())
                explast[h] = sc.tok()
            for qc in range(4):
                if kt not in kt_of_qc[qc]:
                    continue
                lo = max(qlo, 512 * qc)
                hi = min(qhi, 512 * qc + 512)
                for eo in (0, 1):
                    h = 2 * hp + eo
                    P.op("tensor", "matmul",
                         OBK[(eo, qc % 2)][0:65, lo - 512 * qc:hi - 512 * qc],
                         vhat[:, kt, 65 * kv:65 * kv + 65],
                         sigP[:, eo, lo - qlo:hi - qlo],
                         start=False, stop=(kt == last_kt[qc]),
                         waits=[explast[h]])
                if kt != last_kt[qc]:
                    continue
                t_pv = P.tag_last("tensor", mm)
                for eo in (0, 1):
                    h = 2 * hp + eo
                    ob = OBK[(eo, qc % 2)]
                    rw = [t_pv] if eo == 0 else []
                    if qc == 0 and rs_dma[eo] is not None:
                        rw.append(rs_dma[eo])
                    P.op("scalar", "copy",
                         out=rscr[0:1, eo, 512 * qc:512 * qc + 512],
                         in_=ob[64:65, 0:512], waits=tuple(rw), inc=sc.inc())
                    if eo == 0:
                        P.op("scalar", "copy",
                             out=ahat[0:64, hp, 512 * qc:512 * qc + 512],
                             in_=ob[0:64, 0:512], inc=sc.inc())
                    else:
                        obw = [ob_dma[qc % 2]] if ob_dma[qc % 2] is not None else []
                        P.op("scalar", "copy", out=obounce[0:64, qc % 2, :],
                             in_=ob[0:64, 0:512], waits=obw, inc=sc.inc())
                        s_ob = sc.tok()
                        P.op("sync", "dma_start",
                             out=ahat[64:128, hp, 512 * qc:512 * qc + 512],
                             in_=obounce[0:64, qc % 2, :], waits=(s_ob,),
                             inc=sy.inc())
                        ob_dma[qc % 2] = ybar()
                    ev_tok[(h, qc)] = sc.tok()
                    if qc == 3:
                        P.op("sync", "dma_start", out=rrall[h:h + 1, :],
                             in_=rscr[0:1, eo, :], waits=(ev_tok[(h, qc)],),
                             inc=sy.inc())
                        rs_dma[eo] = ybar()
    s_ahat = sc.tok()
    g_ahat = ybar()
    P.op("vector", "reciprocal", out=rrinv[:, :], in_=rrall[:, :],
         waits=(s_ahat, rs_dma[0], rs_dma[1]), inc=vc.inc())
    P.op("vector", "tensor_copy", out=rrallb[:, :], in_=rrinv[:, :], inc=vc.inc())
    v_rr = vc.tok()
    norm_tok = {}
    it = 0
    for oc in range(8):
        for tc in range(4):
            pp = psC if it % 2 == 0 else psD
            tw = [v_rr, g_ahat]
            if it >= 2:
                tw.append(norm_tok[it - 2])
            P.op("tensor", "matmul", pp[:, :], bdsel[:, oc, :],
                 rrallb[:, 512 * tc:512 * tc + 512],
                 start=True, stop=True, waits=tw, inc=mm.inc())
            t_rep = mm.tok()
            P.op("vector", "tensor_tensor",
                 out=ahat[:, oc, 512 * tc:512 * tc + 512],
                 in0=ahat[:, oc, 512 * tc:512 * tc + 512], in1=pp[:, :],
                 op=ALU.mult, waits=(t_rep,), inc=vc.inc())
            norm_tok[it] = vc.tok()
            it += 1
    v_ahat = vc.tok()
    if debug:
        P.op("sync", "dma_start", out=dbg["dbg_ahat"][:, :],
             in_=ahat[:].rearrange("p a t -> p (a t)"),
             waits=(v_ahat, g_ahat), inc=sy.inc())
    dbg_bar2 = ybar() if debug else None

    p2_mm_done = mm.tok()

    if upto <= 4:
        return finish_early()

    P.op("sync", "dma_start", out=xring[:, 0, :], in_=d_xres[0:128, :],
         waits=(p2_mm_done,), inc=sy.inc())
    xres_tok = {0: ybar()}

    v_ln = {}
    zh_ev = []

    def emit_tr(t_):
        for g_ in range(2):
            tw_ = [v_ln[t_]] if g_ == 0 else []
            if zh_ev:
                tw_.append(zh_ev[-1])
            for j_ in range(4):
                dc_ = 4 * g_ + j_
                P.op("tensor", "transpose", psG0[:, 128 * j_:128 * j_ + 128],
                     strm[:, t_, 128 * dc_:128 * dc_ + 128], IDENTF,
                     waits=tw_ if j_ == 0 else (),
                     inc=mm.inc() if j_ == 3 else None)
            t_tr_ = mm.tok()
            P.op("scalar", "copy",
                 out=zhatfm[:, 4 * g_:4 * g_ + 4, 128 * t_:128 * t_ + 128],
                 in_=psG0[:, :].rearrange("p (a c) -> p a c", c=128),
                 waits=(t_tr_,), inc=sc.inc())
            zh_ev.append(sc.tok())

    ln_done = {}
    scat_done = {}
    for t in range(NT):
        if t + 1 < NT:
            P.op("sync", "dma_start", out=xring[:, (t + 1) % 2, :],
                 in_=d_xres[128 * (t + 1):128 * (t + 1) + 128, :],
                 waits=(ln_done[t - 1],) if t >= 1 else (), inc=sy.inc())
            xres_tok[t + 1] = ybar()
        banks = (psC, psD) if t % 2 == 0 else (psE, psF)
        t_ops = []
        for oci, pp in enumerate(banks):
            tw = [g_wo, g_ahat, v_ahat] if t < 2 else [ln_done[t - 2]]
            for dc in range(8):
                P.op("tensor", "matmul", pp[:, :],
                     ahat[:, dc, 128 * t:128 * t + 128],
                     wsb[:, dc, 512 * oci:512 * oci + 512],
                     start=(dc == 0), stop=False,
                     waits=tw if dc == 0 else ())
            P.op("tensor", "matmul", pp[:, :], onesrow[0:1, 0:128],
                 brows2[0:1, 512 * oci:512 * oci + 512],
                 start=False, stop=True, inc=mm.inc())
            t_ops.append(mm.tok())
        if t >= 1:
            emit_tr(t - 1)
        P.op("vector", "tensor_add", strm[:, t, 0:512], banks[0][:, :],
             xring[:, t % 2, 0:512], waits=(t_ops[0], xres_tok[t]), inc=vc.inc())
        P.op("vector", "tensor_add", strm[:, t, 512:1024], banks[1][:, :],
             xring[:, t % 2, 512:1024], waits=(t_ops[1],), inc=vc.inc())
        P.op("vector", "reduce_sum", out=lncol[:, 0:1], in_=strm[:, t, :],
             axis=AXX, inc=vc.inc())
        P.op("vector", "tensor_scalar", out=lncol[:, 1:2], in0=lncol[:, 0:1],
             scalar1=1.0 / D, scalar2=None, op0=ALU.mult, inc=vc.inc())
        v_m = vc.tok()
        P.op("scalar", "activation", out=sqdummy[:, :], in_=strm[:, t, :],
             func=AF.Square, accum_out=lncol[:, 2:3], waits=(v_m,), inc=sc.inc())
        s_sq = sc.tok()
        P.op("vector", "tensor_scalar", out=lncol[:, 3:4], in0=lncol[:, 2:3],
             scalar1=1.0 / D, scalar2=None, op0=ALU.mult, waits=(s_sq,),
             inc=vc.inc())
        P.op("vector", "tensor_mul", lncol[:, 4:5], lncol[:, 1:2], lncol[:, 1:2],
             inc=vc.inc())
        P.op("vector", "tensor_sub", lncol[:, 3:4], lncol[:, 3:4], lncol[:, 4:5],
             inc=vc.inc())
        v_var = vc.tok()
        P.op("scalar", "activation", out=lncol[:, 5:6], in_=lncol[:, 3:4],
             func=AF.Sqrt, bias=cpool[:, 1:2], waits=(v_var,), inc=sc.inc())
        s_sd = sc.tok()
        P.op("vector", "reciprocal", out=lncol[:, 6:7], in_=lncol[:, 5:6],
             waits=(s_sd,), inc=vc.inc())
        v_iv = vc.tok()
        P.op("vector", "tensor_scalar", out=strm[:, t, :], in0=strm[:, t, :],
             scalar1=lncol[:, 1:2], scalar2=lncol[:, 6:7],
             op0=ALU.subtract, op1=ALU.mult, waits=(v_iv,), inc=vc.inc())
        v_ln[t] = vc.tok()
        zw = list(scat_done[t - 2]) if t >= 2 else []
        P.op("vector", "tensor_copy", out=z1ring[:, t % 2, 0:1024],
             in_=strm[:, t, :], waits=zw, inc=vc.inc())
        ln_done[t] = vc.tok()
        P.op("gpsimd", "indirect_dma_start",
             out=d_fwdx[:, :],
             out_offset=IndirectOffsetOnAxis(idxs[:, t:t + 1], 0),
             in_=z1ring[:, t % 2, 0:1024], in_offset=None,
             waits=(ln_done[t],), inc=sg.inc())
        scat_done[t] = (gbar(),)
    emit_tr(NT - 1)
    s_zh = sc.tok()
    g_scat = gbar()
    if debug:
        P.op("sync", "dma_start", out=dbg["dbg_strm"][:, :],
             in_=strm[:].rearrange("p a t -> p (a t)"), waits=(ln_done[NT - 1],),
             inc=sy.inc())
        P.op("sync", "dma_start", out=dbg["dbg_zhat"][:, :],
             in_=zhatfm[:].rearrange("p a t -> p (a t)"),
             waits=(s_zh,), inc=sy.inc())
    dbg_bar3 = ybar() if debug else None

    g_evac = {}
    g_trd = {}
    side_tok = {}
    s_gt_tok = {}
    for tc in range(4):
        pg = PSB[1]
        tw = [s_zh, ln_done[NT - 1]]
        if tc >= 1:
            tw.append(g_evac[tc - 1])
        for dc in range(8):
            P.op("tensor", "matmul", pg[0:9, :], egwsb[:, dc, :],
                 zhatfm[:, dc, 512 * tc:512 * tc + 512],
                 start=(dc == 0), stop=(dc == 7), waits=tw if dc == 0 else (),
                 inc=mm.inc() if dc == 7 else None)
        t_g = mm.tok()
        grw = [t_g] if tc < 2 else [t_g, g_trd[tc - 2]]
        P.op("scalar", "activation", out=gring[0:9, tc % 2, :], in_=pg[0:9, :],
             func=AF.Sigmoid, bias=cfc("egbc", 0, 1)[0:9, :], waits=grw,
             inc=sc.inc())
        g_evac[tc] = sc.tok()
        for ti in range(4):
            t = 4 * tc + ti
            pe = PSB[3] if t % 2 == 0 else PSB[5]
            trw = [g_evac[tc]]
            if t >= 2:
                trw.append(s_gt_tok[t - 2])
            P.op("tensor", "transpose", pe[0:128, 0:9],
                 gring[0:9, tc % 2, 128 * ti:128 * ti + 128], IDENTF[0:9, 0:9],
                 waits=trw, inc=mm.inc())
            t_tr = mm.tok()
            P.op("scalar", "copy", out=gstt[:, t % 2, 0:9], in_=pe[0:128, 0:9],
                 waits=(t_tr,), inc=sc.inc())
            s_gt = sc.tok()
            s_gt_tok[t] = s_gt
            P.op("vector", "tensor_add", scr8[:], gstt[:, t % 2, 1:9],
                 cfc("expbr", 0, 8), waits=(s_gt,), inc=vc.inc())
            P.op("vector", "reduce_max", out=mxcol[:], in_=scr8[:], axis=AXX,
                 inc=vc.inc())
            vb = vc.tok()
            P.op("vector", "tensor_single_scalar", out=scr8[:], in_=scr8[:],
                 scalar=mxcol[:], op=ALU.is_equal, waits=(vb,), inc=vc.inc())
            P.op("vector", "scalar_tensor_tensor", out=scrsm[:], in0=scr8[:],
                 scalar=1.0, in1=gstt[:, t % 2, 1:9], op0=ALU.mult, op1=ALU.mult,
                 accum_out=t1col[:], inc=vc.inc())
            vb2 = vc.tok()
            P.op("vector", "tensor_scalar", out=dcol[:, 0:1],
                 in0=gstt[:, t % 2, 0:1], scalar1=t1col[:], scalar2=None,
                 op0=ALU.subtract, waits=(vb2,), inc=vc.inc())
            P.op("vector", "tensor_scalar", out=dcol[:, 1:2], in0=dcol[:, 0:1],
                 scalar1=-1.0, scalar2=None, op0=ALU.mult, inc=vc.inc())
            v_d = vc.tok()
            P.op("scalar", "activation", out=wst[:], in_=dcol[:], func=AF.Sigmoid,
                 waits=(v_d,), inc=sc.inc())
            s_w = sc.tok()
            P.op("vector", "tensor_scalar", out=wsmcol[:, t:t + 1],
                 in0=wst[:, 0:1], scalar1=1.0, scalar2=None, op0=ALU.subtract,
                 waits=(s_w,), inc=vc.inc())
            P.op("vector", "tensor_mul", wsmcol[:, t:t + 1], wsmcol[:, t:t + 1],
                 cfc("selmask", t, 1), inc=vc.inc())
            P.op("vector", "tensor_scalar", out=wsmcol[:, t:t + 1],
                 in0=wsmcol[:, t:t + 1], scalar1=1.0, scalar2=None, op0=ALU.add,
                 inc=vc.inc())
            sw_ = [side_tok[t - 2]] if t >= 2 else []
            P.op("vector", "tensor_copy", out=sidescr[:, t % 2, 0:8],
                 in_=scr8[:], waits=sw_, inc=vc.inc())
            P.op("vector", "tensor_copy", out=sidescr[:, t % 2, 8:9],
                 in_=wst[:, 1:2], inc=vc.inc())
            v_side = vc.tok()
            P.op("gpsimd", "indirect_dma_start",
                 out=d_fwds[:, :],
                 out_offset=IndirectOffsetOnAxis(idxs[:, t:t + 1], 0),
                 in_=sidescr[:, t % 2, :], in_offset=None,
                 waits=(v_side,), inc=sg.inc())
            side_tok[t] = gbar()
        g_trd[tc] = mm.tok()
    g_side = gbar()
    v_wsm = vc.tok()
    if debug:
        P.op("sync", "dma_start", out=dbg["dbg_wsm"][:, :], in_=wsmcol[:, :],
             waits=(v_wsm,), inc=sy.inc())

    P.op("gpsimd", "collective_compute", "ReduceScatter", ALU.add,
         replica_groups=[list(range(8))], ins=[d_fwds[0:NSEL, :].opt()],
         outs=[d_rss.opt()], waits=(g_side,), inc=cc.inc())
    P.op("gpsimd", "collective_compute", "ReduceScatter", ALU.add,
         replica_groups=[list(range(8))], ins=[d_fwdx[0:NSEL, :].opt()],
         outs=[d_rsx.opt()], inc=cc.inc())
    c_rs = cc.tok()

    o_mm_done = mm.tok()
    P.op("gpsimd", "dma_start", out=sw1sb[:],
         in_=d_sw1T[:, :].rearrange("(dc p) h -> p dc h", p=128),
         waits=(o_mm_done,), inc=sg.inc())
    P.op("gpsimd", "dma_start", out=sw2sb[:],
         in_=d_sw2T[:, :].rearrange("(dc p) h -> p dc h", p=128), inc=sg.inc())
    ob1, nb1 = br["sb1"]
    ob2, nb2 = br["sb2"]
    P.op("gpsimd", "dma_start", out=browsb[0:1, 0:512],
         in_=d_brows[0:1, ob1:ob1 + nb1], waits=(o_mm_done,), inc=sg.inc())
    P.op("gpsimd", "dma_start", out=browsb[0:1, 512:1024],
         in_=d_brows[0:1, ob2:ob2 + nb2], inc=sg.inc())
    g_sw12 = gbar()

    if upto <= 5:
        return finish_early()

    ffn_z = {}
    for tc in range(4):
        for hc in range(4):
            it = 4 * tc + hc
            p1 = psC if it % 2 == 0 else psD
            p2_ = psE if it % 2 == 0 else psF
            tw = [g_sw12, s_zh, ln_done[NT - 1]]
            if debug and it == 0:
                tw.append(dbg_bar2)
                tw.append(dbg_bar3)
            if it >= 2:
                tw.append(ffn_z[it - 2])
            for dc in range(8):
                P.op("tensor", "matmul", p1[:, :],
                     sw1sb[:, dc, 128 * hc:128 * hc + 128],
                     zhatfm[:, dc, 512 * tc:512 * tc + 512],
                     start=(dc == 0), stop=(dc == 7), waits=tw if dc == 0 else ())
            for dc in range(8):
                P.op("tensor", "matmul", p2_[:, :],
                     sw2sb[:, dc, 128 * hc:128 * hc + 128],
                     zhatfm[:, dc, 512 * tc:512 * tc + 512],
                     start=(dc == 0), stop=(dc == 7),
                     inc=mm.inc() if dc == 7 else None)
            t_x = mm.tok()
            P.op("scalar", "activation", out=ssping[:, it % 2, :], in_=p1[:, :],
                 func=AF.Silu, bias=cfc("sb1c", hc, 1), waits=(t_x,),
                 inc=sc.inc())
            s_sil = sc.tok()
            P.op("vector", "scalar_tensor_tensor",
                 out=zeb[:, hc, 512 * tc:512 * tc + 512], in0=p2_[:, :],
                 scalar=cfc("sb2c", hc, 1), in1=ssping[:, it % 2, :],
                 op0=ALU.add, op1=ALU.mult, waits=(s_sil,), inc=vc.inc())
            ffn_z[it] = vc.tok()
    v_z = vc.tok()

    P.op("gpsimd", "dma_start", out=sw3sb[:],
         in_=d_sw3T[:, :].rearrange("(hc p) o -> p hc o", p=128),
         waits=(mm.tok(),), inc=sg.inc())
    ob3, nb3 = br["sb3"]
    P.op("gpsimd", "dma_start", out=brows2[0:1, 0:1024],
         in_=d_brows[0:1, ob3:ob3 + nb3], inc=sg.inc())
    g_sw3 = gbar()
    for half in range(2):
        pe = psE if half == 0 else psF
        for hc in range(4):
            P.op("tensor", "matmul", pe[0:1, :],
                 cbfc("slnb", hc, 1), sw3sb[:, hc, 512 * half:512 * half + 512],
                 start=(hc == 0), stop=False,
                 waits=(g_sw3, ffn_z[14], ffn_z[15]) if hc == 0 else ())
        P.op("tensor", "matmul", pe[0:1, :], onesrow[0:1, 0:1],
             brows2[0:1, 512 * half:512 * half + 512], start=False, stop=True,
             inc=mm.inc())
        t_b = mm.tok()
        P.op("scalar", "copy", out=bias3e[0:1, 512 * half:512 * half + 512],
             in_=pe[0:1, :], waits=(t_b,), inc=sc.inc())
    s_b3 = sc.tok()

    sh_done = {}
    sqmm_tok = {}
    s2_done = {}
    rows_tok = {}

    def ffn_stats_a(tc):
        for hc in range(4):
            g = 4 * tc + hc
            sqw = [v_z]
            if g >= 2:
                sqw.append(sqmm_tok[g - 2])
            P.op("scalar", "activation", out=sqscr[:, hc % 2, :],
                 in_=zeb[:, hc, 512 * tc:512 * tc + 512], func=AF.Square,
                 waits=sqw, inc=sc.inc())
            s_sq = sc.tok()
            stw = [v_z, s_b3] if tc == 0 else [sh_done[tc - 1]]
            P.op("tensor", "matmul", psE[0:1, :], ONESCB,
                 zeb[:, hc, 512 * tc:512 * tc + 512],
                 start=(hc == 0), stop=(hc == 3),
                 waits=stw if hc == 0 else ())
            P.op("tensor", "matmul", psF[0:1, :], ONESCB, sqscr[:, hc % 2, :],
                 start=(hc == 0), stop=(hc == 3), waits=(s_sq,), inc=mm.inc())
            sqmm_tok[g] = mm.tok()
        t_st = mm.tok()
        P.op("scalar", "activation", out=frows[0:1, 0, :], in_=psE[0:1, :],
             func=AF.Copy, scale=cpool[0:1, 4:5], waits=(t_st,), inc=sc.inc())
        P.op("scalar", "activation", out=frows[0:1, 1, :], in_=psF[0:1, :],
             func=AF.Copy, scale=cpool[0:1, 4:5], inc=sc.inc())
        s_mu = sc.tok()
        P.op("vector", "tensor_mul", frows[0:1, 2, :], frows[0:1, 0, :],
             frows[0:1, 0, :], waits=(s_mu,), inc=vc.inc())
        P.op("vector", "tensor_sub", frows[0:1, 1, :], frows[0:1, 1, :],
             frows[0:1, 2, :], inc=vc.inc())
        v_v = vc.tok()
        P.op("scalar", "activation", out=frows[0:1, 1, :], in_=frows[0:1, 1, :],
             func=AF.Sqrt, bias=cpool[0:1, 1:2], waits=(v_v,), inc=sc.inc())
        s_s = sc.tok()
        P.op("vector", "reciprocal", out=frows[0:1, 1, :], in_=frows[0:1, 1, :],
             waits=(s_s,), inc=vc.inc())
        P.op("vector", "tensor_copy", out=invrowb[:], in_=frows[0:1, 1, :],
             inc=vc.inc())
        P.op("vector", "tensor_mul", frows[0:1, 2, :], frows[0:1, 0, :],
             frows[0:1, 1, :], inc=vc.inc())
        P.op("vector", "tensor_scalar", out=frows[0:1, 2, :],
             in0=frows[0:1, 2, :], scalar1=-1.0, scalar2=None, op0=ALU.mult,
             inc=vc.inc())
        P.op("vector", "tensor_copy", out=minrowb[:], in_=frows[0:1, 2, :],
             inc=vc.inc())
        rows_tok[tc] = vc.tok()

    def ffn_stats_b(tc):
        P.op("tensor", "matmul", psE[0:128, :], onesrow[0:1, 0:128], invrowb[:],
             start=True, stop=True, waits=(rows_tok[tc],))
        P.op("tensor", "matmul", psF[0:128, :], onesrow[0:1, 0:128], minrowb[:],
             start=True, stop=True)
        t_rep = P.tag_last("tensor", mm)
        for hc in range(4):
            P.op("vector", "scalar_tensor_tensor", out=ttscr[:, :],
                 in0=zeb[:, hc, 512 * tc:512 * tc + 512],
                 scalar=cfc("slng", hc, 1), in1=psE[0:128, :],
                 op0=ALU.mult, op1=ALU.mult,
                 waits=(t_rep,) if hc == 0 else (), inc=vc.inc())
            P.op("vector", "scalar_tensor_tensor",
                 out=sheb[:, hc, 512 * tc:512 * tc + 512],
                 in0=psF[0:128, :], scalar=cfc("slng", hc, 1),
                 in1=ttscr[:, :], op0=ALU.mult, op1=ALU.add, inc=vc.inc())
        sh_done[tc] = vc.tok()

    def ffn_s2(tc):
        for ti in range(4):
            t = 4 * tc + ti
            for oci in range(2):
                it = 2 * t + oci
                pp = psC if it % 2 == 0 else psD
                tw = [sh_done[tc], s_b3, v_wsm]
                if it >= 2:
                    tw.append(s2_done[it - 2])
                for hc in range(4):
                    P.op("tensor", "matmul", pp[:, :],
                         sheb[:, hc, 128 * t:128 * t + 128],
                         sw3sb[:, hc, 512 * oci:512 * oci + 512],
                         start=(hc == 0), stop=False, waits=tw if hc == 0 else ())
                P.op("tensor", "matmul", pp[:, :], onesrow[0:1, 0:128],
                     bias3e[0:1, 512 * oci:512 * oci + 512], start=False,
                     stop=True, inc=mm.inc())
                t_s = mm.tok()
                P.op("vector", "scalar_tensor_tensor",
                     out=strm[:, t, 512 * oci:512 * oci + 512], in0=pp[:, :],
                     scalar=wsmcol[:, t:t + 1],
                     in1=strm[:, t, 512 * oci:512 * oci + 512],
                     op0=ALU.mult, op1=ALU.add, waits=(t_s,), inc=vc.inc())
                s2_done[it] = vc.tok()

    for tc in range(4):
        ffn_stats_a(tc)
        if tc >= 1:
            ffn_s2(tc - 1)
        ffn_stats_b(tc)
    ffn_s2(3)
    v_sh = vc.tok()
    v_s2 = vc.tok()
    if debug:
        P.op("sync", "dma_start", out=dbg["dbg_s2"][:, :],
             in_=strm[:].rearrange("p a t -> p (a t)"), waits=(v_s2,), inc=sy.inc())
    dbg_bar4 = ybar() if debug else None

    if upto <= 6:
        return finish_early()

    P.op("gpsimd", "dma_start", out=xsel[:],
         in_=d_rsx[:, :].rearrange("(a p) c -> p a c", p=128),
         waits=(c_rs, v_s2), inc=sg.inc())
    P.op("gpsimd", "dma_start", out=xsels[:],
         in_=d_rss[:, :].rearrange("(a p) c -> p a c", p=128), inc=sg.inc())
    g_xsel = gbar()
    for ts in range(2):
        P.op("vector", "tensor_copy", out=indf[:, ts, :],
             in_=xsels[:, ts, 0:8], waits=(g_xsel,) if ts == 0 else (),
             inc=vc.inc())
        P.op("vector", "tensor_copy", out=weff[:, ts, :],
             in_=xsels[:, ts, 8:9], inc=vc.inc())
    v_ind = vc.tok()
    y_xdc = {}
    for dc in range(8):
        for ts in range(2):
            P.op("sync", "dma_start_transpose",
                 out=xselfm[:, dc, 128 * ts:128 * ts + 128],
                 in_=xsel[:, ts, 128 * dc:128 * dc + 128],
                 waits=(g_xsel,) if (ts == 0 and dc == 0) else (), inc=sy.inc())
        y_xdc[dc] = ybar()
    y_xsel = y_xdc[7]

    def load_expert12(e, slot, waits):
        e1, e2, _ = ewS[slot]
        P.op("gpsimd", "dma_start", out=e1[:],
             in_=d_ew1T[e, :, :].rearrange("(dc p) h -> p dc h", p=128),
             waits=waits, inc=sg.inc())
        P.op("gpsimd", "dma_start", out=e2[:],
             in_=d_ew2T[e, :, :].rearrange("(dc p) h -> p dc h", p=128),
             inc=sg.inc())
        return (gbar(),)

    zfree = mm.tok()
    sqmm_tok = {}
    g_e12 = {0: load_expert12(0, 0, (zfree,) + ((dbg_bar3,) if debug else ()))}
    g_e3 = {}
    t_h12 = {}
    t_out = {}
    h12_done = {}
    v_ze = {}
    st = {"s_be_prev": s_b3, "v_she_prev": v_s2, "prev_acc": {},
          "acc_last": None}

    def emit_h12(e):
        slot = e % 2
        e1, e2, _ = ewS[slot]
        zb = zebm if slot == 0 else zebm2
        h12 = {}
        for hc in range(4):
            p1 = PSB[0] if hc % 2 == 0 else PSB[1]
            p2_ = PSB[2] if hc % 2 == 0 else PSB[3]
            tw = [*g_e12[e]]
            if e >= 1:
                tw.append(y_xsel)
            if hc >= 2:
                tw.append(h12[hc - 2])
            elif e >= 1:
                tw.append(h12_done[e - 1])
            for dc in range(8):
                dw = tw if dc == 0 else []
                if e == 0 and hc == 0:
                    dw = list(dw) + [y_xdc[dc]]
                P.op("tensor", "matmul", p1[:, 0:256],
                     e1[:, dc, 128 * hc:128 * hc + 128], xselfm[:, dc, :],
                     start=(dc == 0), stop=(dc == 7), waits=dw)
            for dc in range(8):
                P.op("tensor", "matmul", p2_[:, 0:256],
                     e2[:, dc, 128 * hc:128 * hc + 128], xselfm[:, dc, :],
                     start=(dc == 0), stop=(dc == 7),
                     inc=mm.inc() if dc == 7 else None)
            t_h = mm.tok()
            P.op("scalar", "activation", out=ssping[:, hc % 2, 0:256],
                 in_=p1[:, 0:256], func=AF.Silu, bias=cfc("eb1c", 4 * e + hc, 1),
                 waits=(t_h,), inc=sc.inc())
            s_sil = sc.tok()
            P.op("vector", "scalar_tensor_tensor", out=zb[:, hc, :],
                 in0=p2_[:, 0:256], scalar=cfc("eb2c", 4 * e + hc, 1),
                 in1=ssping[:, hc % 2, 0:256], op0=ALU.add, op1=ALU.mult,
                 waits=(s_sil,), inc=vc.inc())
            h12[hc] = vc.tok()
        v_ze[e] = vc.tok()
        t_h12[e] = mm.tok()
        h12_done[e] = vc.tok()

    def tail_a(e):
        slot = e % 2
        _, _, e3 = ewS[slot]
        zb = zebm if slot == 0 else zebm2
        for hc in range(4):
            g = 4 * e + hc
            sqw = [v_ze[e]]
            if g >= 2:
                sqw.append(sqmm_tok[g - 2])
            P.op("scalar", "activation", out=sqscr[:, hc % 2, 0:256],
                 in_=zb[:, hc, :], func=AF.Square, waits=sqw, inc=sc.inc())
            s_sq = sc.tok()
            P.op("tensor", "matmul", PSB[4][0:1, 0:256], ONESCB, zb[:, hc, :],
                 start=(hc == 0), stop=(hc == 3),
                 waits=(v_ze[e], st["v_she_prev"]) if hc == 0 else ())
            P.op("tensor", "matmul", PSB[5][0:1, 0:256], ONESCB,
                 sqscr[:, hc % 2, 0:256],
                 start=(hc == 0), stop=(hc == 3), waits=(s_sq,), inc=mm.inc())
            sqmm_tok[g] = mm.tok()
        t_st = mm.tok()
        P.op("scalar", "activation", out=frows[0:1, 0, 0:256],
             in_=PSB[4][0:1, 0:256], func=AF.Copy, scale=cpool[0:1, 4:5],
             waits=(t_st,), inc=sc.inc())
        P.op("scalar", "activation", out=frows[0:1, 1, 0:256],
             in_=PSB[5][0:1, 0:256], func=AF.Copy, scale=cpool[0:1, 4:5],
             inc=sc.inc())
        s_mu = sc.tok()
        P.op("vector", "tensor_mul", frows[0:1, 2, 0:256], frows[0:1, 0, 0:256],
             frows[0:1, 0, 0:256], waits=(s_mu,), inc=vc.inc())
        P.op("vector", "tensor_sub", frows[0:1, 1, 0:256], frows[0:1, 1, 0:256],
             frows[0:1, 2, 0:256], inc=vc.inc())
        v_v = vc.tok()
        P.op("scalar", "activation", out=frows[0:1, 1, 0:256],
             in_=frows[0:1, 1, 0:256], func=AF.Sqrt, bias=cpool[0:1, 1:2],
             waits=(v_v,), inc=sc.inc())
        s_s = sc.tok()
        P.op("vector", "reciprocal", out=frows[0:1, 1, 0:256],
             in_=frows[0:1, 1, 0:256], waits=(s_s,), inc=vc.inc())
        P.op("vector", "tensor_copy", out=invrowb[0:1, 0:256],
             in_=frows[0:1, 1, 0:256], inc=vc.inc())
        P.op("vector", "tensor_mul", frows[0:1, 2, 0:256], frows[0:1, 0, 0:256],
             frows[0:1, 1, 0:256], inc=vc.inc())
        P.op("vector", "tensor_scalar", out=frows[0:1, 2, 0:256],
             in0=frows[0:1, 2, 0:256], scalar1=-1.0, scalar2=None, op0=ALU.mult,
             inc=vc.inc())
        P.op("vector", "tensor_copy", out=minrowb[0:1, 0:256],
             in_=frows[0:1, 2, 0:256], inc=vc.inc())
        st["v_rows"] = vc.tok()
        for half in range(2):
            pe = PSB[6] if half == 0 else PSB[7]
            bw = [g_e3[e]]
            if e == 0:
                bw += [v_s2, s_b3]
            else:
                bw.append(st["prev_acc"][2 + half])
            for hc in range(4):
                P.op("tensor", "matmul", pe[0:1, :], cbfc("selnb", hc, 1),
                     e3[:, hc, 512 * half:512 * half + 512],
                     start=(hc == 0), stop=False,
                     waits=bw if hc == 0 else ())
            P.op("tensor", "matmul", pe[0:1, :], onesrow[0:1, 0:1],
                 brows2[0:1, 512 * half:512 * half + 512], start=False,
                 stop=True, inc=mm.inc())
            t_b = mm.tok()
            P.op("scalar", "copy", out=bias3e[0:1, 512 * half:512 * half + 512],
                 in_=pe[0:1, :], waits=(t_b,), inc=sc.inc())
        st["s_be"] = sc.tok()
        st["s_be_prev"] = st["s_be"]

    def tail_b(e):
        slot = e % 2
        _, _, e3 = ewS[slot]
        zb = zebm if slot == 0 else zebm2
        P.op("tensor", "matmul", PSB[4][0:128, 0:256], onesrow[0:1, 0:128],
             invrowb[0:1, 0:256], start=True, stop=True, waits=(st["v_rows"],))
        P.op("tensor", "matmul", PSB[5][0:128, 0:256], onesrow[0:1, 0:128],
             minrowb[0:1, 0:256], start=True, stop=True)
        t_rep = P.tag_last("tensor", mm)
        for hc in range(4):
            P.op("vector", "scalar_tensor_tensor", out=ttscr[:, 0:256],
                 in0=zb[:, hc, :], scalar=cfc("selng", hc, 1),
                 in1=PSB[4][0:128, 0:256], op0=ALU.mult, op1=ALU.mult,
                 waits=(t_rep,) if hc == 0 else (), inc=vc.inc())
            P.op("vector", "scalar_tensor_tensor", out=shebm[:, hc, :],
                 in0=PSB[5][0:128, 0:256], scalar=cfc("selng", hc, 1),
                 in1=ttscr[:, 0:256], op0=ALU.mult, op1=ALU.add, inc=vc.inc())
        v_she = vc.tok()
        st["v_she_prev"] = v_she
        acc_now = {}
        for ts in range(2):
            for oci in range(2):
                it = 2 * ts + oci
                pp = PSB[6] if it % 2 == 0 else PSB[7]
                tw = [v_she, st["s_be"]]
                if e > 0:
                    tw.append(st["prev_acc"][it])
                if it >= 2:
                    tw.append(acc_now[it - 2])
                for hc in range(4):
                    P.op("tensor", "matmul", pp[:, :],
                         shebm[:, hc, 128 * ts:128 * ts + 128],
                         e3[:, hc, 512 * oci:512 * oci + 512],
                         start=(hc == 0), stop=False, waits=tw if hc == 0 else ())
                P.op("tensor", "matmul", pp[:, :], onesrow[0:1, 0:128],
                     bias3e[0:1, 512 * oci:512 * oci + 512], start=False,
                     stop=True, inc=mm.inc())
                t_o = mm.tok()
                if e == 0:
                    P.op("vector", "tensor_scalar",
                         out=eacc[:, ts, 512 * oci:512 * oci + 512],
                         in0=pp[:, :], scalar1=indf[:, ts, e:e + 1],
                         scalar2=None, op0=ALU.mult, waits=(t_o, v_ind),
                         inc=vc.inc())
                else:
                    P.op("vector", "scalar_tensor_tensor",
                         out=eacc[:, ts, 512 * oci:512 * oci + 512],
                         in0=pp[:, :], scalar=indf[:, ts, e:e + 1],
                         in1=eacc[:, ts, 512 * oci:512 * oci + 512],
                         op0=ALU.mult, op1=ALU.add, waits=(t_o,), inc=vc.inc())
                acc_now[it] = vc.tok()
        t_out[e] = mm.tok()
        st["prev_acc"] = acc_now
        st["acc_last"] = vc.tok()

    emit_h12(0)
    o3, _ = br["eb3"]
    for e in range(8):
        if e + 1 < 8:
            g_e12[e + 1] = load_expert12(
                e + 1, (e + 1) % 2, (t_h12[e - 1],) if e >= 1 else (zfree,))
        P.op("gpsimd", "dma_start", out=ewS[e % 2][2][:],
             in_=d_ew3T[e, :, :].rearrange("(hc p) o -> p hc o", p=128),
             waits=(t_h12[e],) if e < 2 else (t_out[e - 2],), inc=sg.inc())
        P.op("gpsimd", "dma_start", out=brows2[0:1, 0:1024],
             in_=d_brows[0:1, o3 + 1024 * e:o3 + 1024 * e + 1024],
             waits=(st["s_be_prev"],), inc=sg.inc())
        g_e3[e] = gbar()
        tail_a(e)
        if e + 1 < 8:
            emit_h12(e + 1)
        tail_b(e)
    moe_prev_out = st["acc_last"]
    for ts in range(2):
        P.op("vector", "tensor_scalar", out=eacc[:, ts, :], in0=eacc[:, ts, :],
             scalar1=weff[:, ts, 0:1], scalar2=None, op0=ALU.mult,
             waits=(moe_prev_out,), inc=vc.inc())
    v_eacc = vc.tok()
    if debug:
        P.op("sync", "dma_start", out=dbg["dbg_eacc"][:, :],
             in_=eacc[:].rearrange("p a t -> p (a t)"), waits=(v_eacc,), inc=sy.inc())
    dbg_bar5 = ybar() if debug else None
    P.op("gpsimd", "dma_start",
         out=d_agin[:, :].rearrange("(a p) t -> p a t", p=128),
         in_=eacc[:], waits=(v_eacc,), inc=sg.inc())
    g_agin = gbar()
    P.op("gpsimd", "collective_compute", "AllGather", ALU.bypass,
         replica_groups=[list(range(8))], ins=[d_agin.opt()],
         outs=[d_agout[0:NSEL, :].opt()], waits=(g_agin,), inc=cc.inc())
    c_ag = cc.tok()

    if upto <= 7:
        return finish_early()

    radd = {}
    v_iv2h = {}
    for tp in range(NT // 2):
        for sub in range(2):
            t = 2 * tp + sub
            gw = []
            if t == 0:
                gw = [c_ag] + ([dbg_bar4] if debug else [])
            if t >= 2:
                gw.append(radd[t - 2])
            P.op("gpsimd", "indirect_dma_start",
                 out=retbuf[:, t % 2, :], out_offset=None,
                 in_=d_agout[:, :],
                 in_offset=IndirectOffsetOnAxis(idxs[:, 16 + t:16 + t + 1], 0),
                 waits=gw, inc=sg.inc())
        g_t = gbar()
        for sub in range(2):
            t = 2 * tp + sub
            P.op("vector", "tensor_add", strm[:, t, :], strm[:, t, :],
                 retbuf[:, t % 2, :], waits=(g_t, v_s2) if t == 0 else
                 ((g_t,) if sub == 0 else ()),
                 inc=vc.inc())
            radd[t] = vc.tok()
            P.op("vector", "reduce_sum", out=ln2c[:, 0, t:t + 1],
                 in_=strm[:, t, :], axis=AXX, inc=vc.inc())
            P.op("scalar", "activation", out=sqdummy[:, :], in_=strm[:, t, :],
                 func=AF.Square, accum_out=ln2c[:, 1, t:t + 1],
                 waits=(radd[t],), inc=sc.inc())
        t = 2 * tp + 1
        if t in (7, 15):
            hf = t // 8
            cs = slice(8 * hf, 8 * hf + 8)
            s_st = sc.tok()
            P.op("vector", "tensor_scalar", out=ln2c[:, 2, cs],
                 in0=ln2c[:, 0, cs], scalar1=1.0 / D, scalar2=None,
                 op0=ALU.mult, waits=(s_st,), inc=vc.inc())
            P.op("vector", "tensor_scalar", out=ln2c[:, 3, cs],
                 in0=ln2c[:, 1, cs], scalar1=1.0 / D, scalar2=None,
                 op0=ALU.mult, inc=vc.inc())
            P.op("vector", "tensor_mul", ln2c[:, 4, cs], ln2c[:, 2, cs],
                 ln2c[:, 2, cs], inc=vc.inc())
            P.op("vector", "tensor_sub", ln2c[:, 3, cs], ln2c[:, 3, cs],
                 ln2c[:, 4, cs], inc=vc.inc())
            v_var = vc.tok()
            P.op("scalar", "activation", out=ln2c[:, 4, cs], in_=ln2c[:, 3, cs],
                 func=AF.Sqrt, bias=cpool[:, 1:2], waits=(v_var,), inc=sc.inc())
            s_sd = sc.tok()
            P.op("vector", "reciprocal", out=ln2c[:, 5, cs], in_=ln2c[:, 4, cs],
                 waits=(s_sd,), inc=vc.inc())
            v_iv2h[hf] = vc.tok()
    st_tok = {}
    for t in range(NT):
        ow = [v_iv2h[t // 8]]
        if debug and t == 0:
            ow.append(dbg_bar5)
        if t >= 2:
            ow.append(st_tok[t - 2])
        P.op("vector", "tensor_scalar", out=otile[:, t % 2, :],
             in0=strm[:, t, :], scalar1=ln2c[:, 2, t:t + 1],
             scalar2=ln2c[:, 5, t:t + 1],
             op0=ALU.subtract, op1=ALU.mult, waits=ow, inc=vc.inc())
        v_o = vc.tok()
        if t % 2 == 0:
            P.op("sync", "dma_start", out=d_y[128 * t:128 * t + 128, :],
                 in_=otile[:, t % 2, :], waits=(v_o,), inc=sy.inc())
            st_tok[t] = ybar()
        else:
            P.op("scalar", "dma_start", out=d_y[128 * t:128 * t + 128, :],
                 in_=otile[:, t % 2, :], waits=(v_o,), inc=ssd.inc())
            st_tok[t] = sbar()
    ybar()
    sbar()

    return _emit(nc, P)


def _emit(nc, P):
    from contextlib import ExitStack
    es = ExitStack()
    with es:
        for s in P.sems:
            s.h = es.enter_context(nc.semaphore(s.name))
        block = es.enter_context(nc.Block())

        @block.tensor
        def _(eng):
            P.replay(eng, "tensor")

        @block.scalar
        def _(eng):
            P.replay(eng, "scalar")

        @block.vector
        def _(eng):
            P.replay(eng, "vector")

        @block.gpsimd
        def _(eng):
            P.replay(eng, "gpsimd")

        @block.sync
        def _(eng):
            P.replay(eng, "sync")

    return nc


def kernel(**inputs):
    ims = []
    br = None
    for c in range(8):
        im, br = _hostpack(inputs, c)
        ims.append(im)
    nc = build(br)
    res = run_bass_kernel_spmd(nc, ims, core_ids=list(range(8)))
    out = np.zeros((B, T, D), np.float32)
    for c in range(8):
        b_, half = c // 2, c % 2
        out[b_, half * NOWN:(half + 1) * NOWN] = res.results[c]["y"]
    return out


# version 20
# speedup vs baseline: 1.0396x; 1.0061x over previous
import numpy as np
import ml_dtypes
import concourse.bass as bass
import concourse.mybir as mybir
from concourse.bass import IndirectOffsetOnAxis
from concourse.bass_utils import run_bass_kernel_spmd

F32 = mybir.dt.float32
BF16 = mybir.dt.bfloat16
I32 = mybir.dt.int32
AF = mybir.ActivationFunctionType
ALU = mybir.AluOpType
AXX = mybir.AxisListType.X

B, T, D = 4, 4096, 1024
H, HV, HD, WIN = 16, 4, 64, 512
NOWN, HALO, NKEY = 2048, 512, 2560
NT = NOWN // 128
NKT = NKEY // 128
NSEL = 2048
NBLK = NSEL // 8
RMS_EPS = 1.1920929e-07
LN_EPS = 1e-5
BIG = 1.0e30
ENGS = ("tensor", "scalar", "vector", "gpsimd", "sync")


class Sem:
    __slots__ = ("name", "step", "n", "h")

    def __init__(self, name, step):
        self.name, self.step, self.n, self.h = name, step, 0, None

    def inc(self):
        self.n += self.step
        return self

    def tok(self):
        return (self, self.n)


class Prog:
    def __init__(self):
        self.ops = {e: [] for e in ENGS}
        self.sems = []
        self.autobar = {}

    def sem(self, name, step):
        s = Sem(name, step)
        self.sems.append(s)
        return s

    def op(self, eng, meth, *args, waits=(), inc=None, **kw):
        waits = tuple(waits)
        ab = self.autobar.get(eng)
        if ab is not None and meth not in (None, "nop"):
            abv = ab.n - (ab.step if inc is ab else 0)
            if abv > 0:
                waits = waits + ((ab, abv),)
        for (s, v) in waits:
            assert v <= s.n, f"wait on future {s.name}:{v}>{s.n}"
        self.ops[eng].append([waits, meth, args, kw, inc])

    def tag_last(self, eng, sem):
        row = self.ops[eng][-1]
        assert row[4] is None
        row[4] = sem.inc()
        return sem.tok()

    def replay(self, eng_obj, ename):
        for waits, meth, args, kw, inc in self.ops[ename]:
            for (s, v) in waits:
                eng_obj.wait_ge(s.h, v)
            if meth is None:
                continue
            ins = getattr(eng_obj, meth)(*args, **kw)
            if inc is not None:
                ins.then_inc(inc.h, inc.step)


def _hostpack(inputs, core):
    x = inputs["L0_attn_x"]
    f32 = np.float32
    b_, half = core // 2, core % 2
    t0 = half * NOWN

    xb = np.zeros((NKEY, D), f32)
    xb[HALO:] = x[b_, t0:t0 + NOWN]
    if half == 1:
        xb[:HALO] = x[b_, t0 - HALO:t0]
    im = {}
    im["xT"] = np.ascontiguousarray(xb.T)
    im["xres"] = np.ascontiguousarray(x[b_, t0:t0 + NOWN])
    im["wq"] = inputs["L0_attn_w_q"]
    im["wk"] = inputs["L0_attn_w_k"]
    im["wv"] = inputs["L0_attn_w_v"]
    im["wo"] = inputs["L0_attn_w_o"]
    im["sw1T"] = np.ascontiguousarray(inputs["L0_ffn_sw1"].T)
    im["sw2T"] = np.ascontiguousarray(inputs["L0_ffn_sw2"].T)
    im["sw3T"] = np.ascontiguousarray(inputs["L0_ffn_sw3"].T)
    im["egwT"] = np.ascontiguousarray(
        np.concatenate([inputs["L0_ffn_sg_w"], inputs["L0_ffn_eg_w"]], axis=0).T)
    im["ew1T"] = np.ascontiguousarray(inputs["L0_ffn_ew1"].transpose(0, 2, 1))
    im["ew2T"] = np.ascontiguousarray(inputs["L0_ffn_ew2"].transpose(0, 2, 1))
    im["ew3T"] = np.ascontiguousarray(inputs["L0_ffn_ew3"].transpose(0, 2, 1))

    br = {}
    cur = [0]
    parts = []

    def badd(name, arr):
        a = np.asarray(arr, f32).reshape(-1)
        br[name] = (cur[0], a.size)
        cur[0] += a.size
        parts.append(a)
    badd("bq", inputs["L0_attn_b_q"])
    badd("bk", inputs["L0_attn_b_k"])
    badd("bv", inputs["L0_attn_b_v"])
    badd("bo", inputs["L0_attn_b_o"])
    badd("sb1", inputs["L0_ffn_sb1"])
    badd("sb2", inputs["L0_ffn_sb2"])
    badd("sb3", inputs["L0_ffn_sb3"])
    badd("egb", np.concatenate([inputs["L0_ffn_sg_b"], inputs["L0_ffn_eg_b"]]))
    badd("eb1", inputs["L0_ffn_eb1"])
    badd("eb2", inputs["L0_ffn_eb2"])
    badd("eb3", inputs["L0_ffn_eb3"])
    im["brows"] = np.concatenate(parts).reshape(1, -1)

    p = np.arange(128)
    maskL = np.where(p[None, :] < p[:, None], -BIG, 0.0).astype(f32)
    maskR = np.where(p[None, :] > p[:, None], -BIG, 0.0).astype(f32)
    ident = np.eye(128, dtype=f32)
    bd2 = np.zeros((128, 2), f32)
    bd2[0:64, 0] = 1.0
    bd2[64:128, 1] = 1.0
    onescolv = np.ones((128, 1), f32)
    slnb_c = np.asarray(inputs["L0_ffn_s_ln_b"], f32).reshape(4, 128).T
    selnb_c = np.asarray(inputs["L0_ffn_sel_ln_b"], f32).reshape(4, 128).T
    cbf = np.concatenate([maskL, maskR, ident, bd2, onescolv,
                          slnb_c, selnb_c], axis=1)
    im["cbf_src"] = np.ascontiguousarray(cbf)

    gi = t0 + np.arange(NOWN)
    nwin = np.minimum(gi, WIN) + 1
    corr2 = (0.5 * (T - nwin) + 0.5 * T).astype(f32).reshape(NT, 128).T
    ib = np.asarray(inputs["L0_ffn_index_b"]).astype(np.int64)
    it = np.asarray(inputs["L0_ffn_index_t"]).astype(np.int64)
    oc = ib * 2 + (it // NOWN)
    fwd = np.full(NOWN, NSEL, np.int64)
    selm = np.zeros(NOWN, f32)
    for m in range(NSEL):
        if oc[m] == core:
            r = int(it[m] % NOWN)
            fwd[r] = m
            selm[r] = 1.0
    selmask = selm.reshape(NT, 128).T
    pairsel = np.zeros((128, 1), f32)
    pairsel[core, 0] = 1.0
    pairsel[core ^ 1, 0] = 1.0
    expbr = np.broadcast_to(
        np.asarray(inputs["L0_ffn_exp_bias"], f32)[None, :], (128, 8)).copy()
    slng_c = np.asarray(inputs["L0_ffn_s_ln_g"], f32).reshape(4, 128).T
    selng_c = np.asarray(inputs["L0_ffn_sel_ln_g"], f32).reshape(4, 128).T
    sb1c = np.asarray(inputs["L0_ffn_sb1"], f32).reshape(4, 128).T
    sb2c = np.asarray(inputs["L0_ffn_sb2"], f32).reshape(4, 128).T
    egbc = np.zeros((128, 1), f32)
    egbc[0:9, 0] = np.concatenate([np.asarray(inputs["L0_ffn_sg_b"], f32),
                                   np.asarray(inputs["L0_ffn_eg_b"], f32)])
    eb1c = np.asarray(inputs["L0_ffn_eb1"], f32).reshape(8, 4, 128)
    eb1c = eb1c.transpose(2, 0, 1).reshape(128, 32)
    eb2c = np.asarray(inputs["L0_ffn_eb2"], f32).reshape(8, 4, 128)
    eb2c = eb2c.transpose(2, 0, 1).reshape(128, 32)
    haloc = np.zeros((128, 4), f32)
    if half == 0:
        haloc[:] = -BIG
    cfm = np.concatenate([corr2, selmask, pairsel, expbr, slng_c, selng_c,
                          ident, sb1c, sb2c, egbc, eb1c, eb2c, haloc], axis=1)
    im["cf_src"] = np.ascontiguousarray(cfm)
    im["idxg"] = np.ascontiguousarray(np.concatenate(
        [fwd.reshape(NT, 128).T, fwd.reshape(NT, 128).T],
        axis=1).astype(np.int32))
    halocol = np.zeros((1, HALO), f32)
    if half == 0:
        halocol[:] = -BIG
    im["halocol"] = halocol
    bdsel = np.zeros((16, 1024), f32)
    for occ in range(8):
        for pp_ in range(128):
            bdsel[2 * occ + pp_ // 64, 128 * occ + pp_] = 1.0
    im["bdsel"] = bdsel
    bdsq = np.zeros((128, 8, 16), f32)
    for occ in range(8):
        for pp_ in range(128):
            bdsq[pp_, occ, 2 * occ + pp_ // 64] = 1.0
    im["bdsq"] = bdsq.reshape(128, 128)
    im["rmsrow"] = np.concatenate([
        np.asarray(inputs["L0_attn_rms_q_w"], f32),
        np.asarray(inputs["L0_attn_rms_k_w"], f32)]).reshape(1, 128)
    return im, br


CBF = {"maskL": 0, "maskR": 128, "identb": 256,
       "bd2": 384, "onescol": 386, "slnb": 387, "selnb": 391}
NCBF = 395
CF = {"corr2": 0, "selmask": 16, "pairsel": 32, "expbr": 33,
      "slng": 41, "selng": 45, "identf": 49,
      "sb1c": 177, "sb2c": 181, "egbc": 185, "eb1c": 186, "eb2c": 218,
      "haloc": 250}
NCF = 254


def build(br, debug=False, upto=9):
    nc = bass.Bass("TRN2", target_bir_lowering=False, debug=False)
    P = Prog()

    d_xT = nc.dram_tensor("xT", [D, NKEY], F32, kind="ExternalInput").ap()
    d_xres = nc.dram_tensor("xres", [NOWN, D], F32, kind="ExternalInput").ap()
    d_wq = nc.dram_tensor("wq", [D, D], F32, kind="ExternalInput").ap()
    d_wk = nc.dram_tensor("wk", [D, D], F32, kind="ExternalInput").ap()
    d_wv = nc.dram_tensor("wv", [D, 256], F32, kind="ExternalInput").ap()
    d_wo = nc.dram_tensor("wo", [D, D], F32, kind="ExternalInput").ap()
    d_sw1T = nc.dram_tensor("sw1T", [D, 512], F32, kind="ExternalInput").ap()
    d_sw2T = nc.dram_tensor("sw2T", [D, 512], F32, kind="ExternalInput").ap()
    d_sw3T = nc.dram_tensor("sw3T", [512, D], F32, kind="ExternalInput").ap()
    d_egwT = nc.dram_tensor("egwT", [D, 9], F32, kind="ExternalInput").ap()
    d_ew1T = nc.dram_tensor("ew1T", [8, D, 512], F32, kind="ExternalInput").ap()
    d_ew2T = nc.dram_tensor("ew2T", [8, D, 512], F32, kind="ExternalInput").ap()
    d_ew3T = nc.dram_tensor("ew3T", [8, 512, D], F32, kind="ExternalInput").ap()
    NB = sum(n for (_, n) in br.values())
    d_brows = nc.dram_tensor("brows", [1, NB], F32, kind="ExternalInput").ap()
    d_cbf = nc.dram_tensor("cbf_src", [128, NCBF], F32, kind="ExternalInput").ap()
    d_cf = nc.dram_tensor("cf_src", [128, NCF], F32, kind="ExternalInput").ap()
    d_idx = nc.dram_tensor("idxg", [128, 32], I32, kind="ExternalInput").ap()
    d_halocol = nc.dram_tensor("halocol", [1, HALO], F32, kind="ExternalInput").ap()
    d_rms = nc.dram_tensor("rmsrow", [1, 128], F32, kind="ExternalInput").ap()
    d_bdsel = nc.dram_tensor("bdsel", [16, 1024], F32, kind="ExternalInput").ap()
    d_bdsq = nc.dram_tensor("bdsq", [128, 128], F32, kind="ExternalInput").ap()
    d_y = nc.dram_tensor("y", [NOWN, D], F32, kind="ExternalOutput").ap()

    d_fwdx = nc.dram_tensor("fwd_x", [NSEL + 8, 1024], BF16).ap()
    d_fwds = nc.dram_tensor("fwd_s", [NSEL + 8, 16], BF16).ap()
    d_rsx = nc.dram_tensor("rs_x", [NBLK, 1024], BF16).ap()
    d_rss = nc.dram_tensor("rs_s", [NBLK, 16], BF16).ap()
    d_agin = nc.dram_tensor("ag_in", [NBLK, D], BF16).ap()
    d_agout = nc.dram_tensor("ag_out", [NSEL + 1, D], BF16).ap()
    d_denin = nc.dram_tensor("den_in", [1, 16], F32).ap()
    d_denout = nc.dram_tensor("den_out", [8, 16], F32).ap()
    dbg = {}
    if debug:
        for nm, shp, dt in [
                ("dbg_qhat", [128, 8 * NOWN], BF16), ("dbg_khat", [128, 8 * NKEY], BF16),
                ("dbg_vhat", [128, NKT * 260], BF16), ("dbg_sigs", [128, NT * 16], F32),
                ("dbg_recden", [128, NT * 16], F32), ("dbg_c2", [16, NOWN], BF16),
                ("dbg_ahat", [128, 8 * NOWN], BF16), ("dbg_strm", [128, NT * D], F32),
                ("dbg_wsm", [128, NT], F32), ("dbg_zhat", [128, 8 * NOWN], BF16),
                ("dbg_sheb", [128, 4 * NOWN], BF16), ("dbg_eacc", [128, 2 * D], F32),
                ("dbg_s2", [128, NT * D], F32)]:
            dbg[nm] = nc.dram_tensor(nm, shp, dt, kind="ExternalOutput").ap()

    SB0 = 16512
    TOP = 229344
    R1 = SB0
    R2 = R1 + 40960
    R3 = R2 + 73728
    R4 = R3 + 10432
    R5 = R4 + 32768
    R6 = R5 + 16384
    SM = R6 + 26624
    A = {}

    def mk(name, shape, dtype, at):
        A[name] = nc.alloc_sbuf_tensor_at(name, list(shape), dtype, offset=at)
        return A[name]

    xhat = mk("xhat", [128, 8, NKEY], BF16, R1)
    ahat = mk("ahat", [128, 8, NOWN], BF16, R1)
    zeb = mk("zeb", [128, 4, NOWN], BF16, R1)
    sheb = mk("sheb", [128, 4, NOWN], BF16, R1 + 16384)

    qhat = mk("qhat", [128, 8, NOWN], BF16, R2)
    khat = mk("khat", [128, 8, NKEY], BF16, R2 + 32768)
    strm = mk("strm", [128, NT, D], F32, R2)
    xring = mk("xring", [128, 2, D], F32, R2 + 65536)

    vhat = mk("vhat", [128, NKT, 260], BF16, R3)
    xselfm = mk("xselfm", [128, 8, NBLK], BF16, R3)
    zebm = mk("zebm", [128, 4, NBLK], BF16, R3 + 4096)
    shebm = mk("shebm", [128, 4, NBLK], BF16, R3 + 6144)
    zebm2 = mk("zebm2", [128, 4, NBLK], BF16, R3 + 8192)

    sumsqK = mk("sumsqK", [16, NKEY], F32, R4)
    invkb = mk("invkb", [16, NKEY], BF16, R4 + 10240)
    sumsqQ = mk("sumsqQ", [16, NOWN], F32, R4 + 15360)
    sigP = mk("sigP", [128, 2, 640], BF16, R4 + 23552)
    c2fm = mk("c2fm", [16, NOWN], BF16, R4 + 26112)
    zhatfm = mk("zhatfm", [128, 8, NOWN], BF16, R4)
    ewS = [
        (mk("ew1S0", [128, 8, 512], BF16, R4),
         mk("ew2S0", [128, 8, 512], BF16, R4 + 8192),
         mk("ew3S0", [128, 4, 1024], BF16, R2 + 65536)),
        (mk("ew1S1", [128, 8, 512], BF16, R4 + 16384),
         mk("ew2S1", [128, 8, 512], BF16, R4 + 24576),
         mk("ew3S1", [128, 4, 1024], BF16, R1 + 32768)),
    ]

    wsb = mk("wsb", [128, 8, 1024], BF16, R5)
    sw1sb = mk("sw1sb", [128, 8, 512], BF16, R1 + 32768)
    sw2sb = mk("sw2sb", [128, 8, 512], BF16, R3)
    sw3sb = mk("sw3sb", [128, 4, 1024], BF16, R5)
    wvv = wsb[:].rearrange("p a o -> p (a o)")[:, 0:2048].rearrange(
        "p (dc o) -> p dc o", o=256)

    repscr = mk("repscr", [64, 512], F32, R6)
    sigs3 = mk("sigs3", [128, NT, 16], F32, R6 + 2048)
    recden = mk("recden", [128, NT, 16], F32, R6 + 3072)
    invq8 = mk("invq8", [128, NT, 16], F32, R6 + 4096)
    gring = mk("gring", [9, 2, 512], F32, R6)
    z1ring = mk("z1ring", [128, 2, 1040], BF16, R6 + 4160)
    xsel = mk("xsel", [128, 2, 1024], BF16, R6)
    xsels = mk("xsels", [128, 2, 16], BF16, R6 + 4128)
    indf = mk("indf", [128, 2, 8], F32, R6 + 4192)
    weff = mk("weff", [128, 2, 1], F32, R6 + 4256)
    frows = mk("frows", [1, 3, 512], F32, R6 + 8192)
    eacc = mk("eacc", [128, 2, D], F32, R5)
    otile = mk("otile", [128, 2, D], F32, R5)
    ssping = mk("ssping", [128, 2, 512], BF16, R6 + 16384)
    sqdummy = mk("sqdummy", [128, 1024], BF16, R6 + 16384)
    zz = mk("zz", [128, 1040], BF16, R6 + 16384)
    sqscr = mk("sqscr", [128, 2, 512], BF16, R6 + 18496)
    bias3e = mk("bias3e", [1, 1024], BF16, R6 + 20544)
    obounce = mk("obounce", [64, 2, 512], BF16, R6 + 20544)
    invrowb = mk("invrowb", [1, 512], BF16, R6 + 22592)
    minrowb = mk("minrowb", [1, 512], BF16, R6 + 23616)
    rscr = mk("rscr", [1, 2, NOWN], BF16, R4)
    rrall = mk("rrall", [16, NOWN], BF16, R4 + 8192)
    rrallb = mk("rrallb", [16, NOWN], BF16, R4 + 12288)
    rrinv = mk("rrinv", [16, NOWN], F32, R4)
    ttscr = mk("ttscr", [128, 512], BF16, R6 + 24640)
    gstt = mk("gstt", [128, 2, 12], F32, R6 + 25664)
    ln2c = mk("ln2c", [128, 6, 16], F32, R6 + 25760)
    retbuf = mk("retbuf", [128, 2, D], BF16, R6 + 4160)

    cur = [SM]

    def sbal(name, shape, dtype):
        at = cur[0]
        nb = int(np.prod(shape[1:])) * mybir.dt.size(dtype)
        cur[0] = (at + nb + 31) & ~31
        return mk(name, shape, dtype, at)

    cbf = sbal("cbf", [128, NCBF], BF16)
    cf = sbal("cf", [128, NCF], F32)
    idxs = sbal("idxs", [128, 32], I32)
    browsb = sbal("browsb", [1, 1024], BF16)
    brows2 = sbal("brows2", [1, 1024], BF16)
    onesrow = sbal("onesrow", [1, 640], BF16)
    halocolb = sbal("halocolb", [1, HALO], BF16)
    bdsel = sbal("bdsel", [16, 8, 128], BF16)
    wqwkrow2 = sbal("wqwkrow2", [1, 128], F32)
    wqwkcol = sbal("wqwkcol", [128, 1], F32)
    rmsq = sbal("rmsq", [1, 64], F32)
    rmsk = sbal("rmsk", [1, 64], F32)
    wqwk = sbal("wqwk", [1, 64], F32)
    pk = [R6 + 14336]

    def pkmk(name, shape, dtype):
        at = pk[0]
        nb = int(np.prod(shape[1:])) * mybir.dt.size(dtype)
        pk[0] = (at + nb + 31) & ~31
        assert pk[0] <= R6 + 16384, name
        return mk(name, shape, dtype, at)

    wsmcol = pkmk("wsmcol", [128, NT], F32)
    scr8 = pkmk("scr8", [128, 8], F32)
    scrsm = pkmk("scrsm", [128, 8], F32)
    mxcol = pkmk("mxcol", [128, 1], F32)
    t1col = pkmk("t1col", [128, 1], F32)
    wst = pkmk("wst", [128, 2], F32)
    dcol = pkmk("dcol", [128, 2], F32)
    sidescr = pkmk("sidescr", [128, 2, 16], BF16)
    densb = pkmk("densb", [8, 16], F32)
    denrow = pkmk("denrow", [1, 16], F32)
    sminv = pkmk("sminv", [16, 1], F32)
    rsumrep = pkmk("rsumrep", [128, 16], F32)
    c2scr = pkmk("c2scr", [128, 16], F32)
    lncol = pkmk("lncol", [128, 8], F32)
    onescf = pkmk("onescf", [128, 1], F32)
    cpool = pkmk("cpool", [128, 5], F32)
    bdsq = pkmk("bdsq", [128, 8, 16], BF16)
    rsrep1 = pkmk("rsrep1", [16, 128], F32)
    egwsb = pkmk("egwsb", [128, 8, 9], BF16)
    assert cur[0] <= TOP, f"SBUF overflow {cur[0]} > {TOP}"

    psum_all = nc.alloc_psum_tensor("psum_all", [128, 4096], F32)
    assert nc.psum_banks_remaining == 0
    PSB = [psum_all[:, 512 * i:512 * i + 512] for i in range(8)]
    psA = psum_all[:, 0:640]
    psB = psum_all[:, 1024:1664]
    psG = psum_all[:, 2048:2688]
    psC = PSB[0]
    psD = PSB[2]
    psE = PSB[4]
    psF = PSB[6]
    psG0 = PSB[1]
    SRING = [psA, psB, psG]

    sy = P.sem("s_sy", 16)
    sg = P.sem("s_sg", 16)
    ssd = P.sem("s_ssd", 16)
    syd = P.sem("s_syd", 1)
    sgd = P.sem("s_sgd", 1)
    ssdd = P.sem("s_ssdd", 1)

    def ybar():
        P.op("sync", "nop", waits=((sy, sy.n),), inc=syd.inc())
        return syd.tok()

    def gbar():
        P.op("gpsimd", "nop", waits=((sg, sg.n),), inc=sgd.inc())
        return sgd.tok()

    def sbar():
        P.op("scalar", "nop", waits=((ssd, ssd.n),), inc=ssdd.inc())
        return ssdd.tok()

    mm = P.sem("s_mm", 1)
    sc = P.sem("s_sc", 1)
    vc = P.sem("s_vc", 1)
    cc = P.sem("s_cc", 1)
    P.autobar = {"vector": vc, "scalar": sc}

    MASKL = cbf[:, CBF["maskL"]:CBF["maskL"] + 128]
    MASKR = cbf[:, CBF["maskR"]:CBF["maskR"] + 128]
    IDENTB = cbf[:, CBF["identb"]:CBF["identb"] + 128]
    BD2 = cbf[:, CBF["bd2"]:CBF["bd2"] + 2]
    ONESCB = cbf[:, CBF["onescol"]:CBF["onescol"] + 1]
    IDENTF = cf[:, CF["identf"]:CF["identf"] + 128]

    def cfc(name, o, n):
        return cf[:, CF[name] + o:CF[name] + o + n]

    def cbfc(name, o, n):
        return cbf[:, CBF[name] + o:CBF[name] + o + n]

    P.op("gpsimd", "dma_start", out=cbf[:], in_=d_cbf[:, :], inc=sg.inc())
    P.op("sync", "dma_start", out=cf[:], in_=d_cf[:, :], inc=sy.inc())
    P.op("sync", "dma_start", out=idxs[:], in_=d_idx[:, :], inc=sy.inc())
    P.op("sync", "dma_start", out=rmsq[:], in_=d_rms[0:1, 0:64], inc=sy.inc())
    P.op("sync", "dma_start", out=rmsk[:], in_=d_rms[0:1, 64:128], inc=sy.inc())
    P.op("gpsimd", "dma_start", out=halocolb[:], in_=d_halocol[:, :], inc=sg.inc())
    obq, nbq = br["bq"]
    P.op("gpsimd", "dma_start", out=browsb[0:1, 0:1024],
         in_=d_brows[0:1, obq:obq + nbq], inc=sg.inc())
    obk, nbk = br["bk"]
    P.op("gpsimd", "dma_start", out=brows2[0:1, 0:1024],
         in_=d_brows[0:1, obk:obk + nbk], inc=sg.inc())
    P.op("gpsimd", "dma_start", out=egwsb[:],
         in_=d_egwT[:, :].rearrange("(dc p) e -> p dc e", p=128), inc=sg.inc())
    P.op("gpsimd", "dma_start", out=bdsel[:],
         in_=d_bdsel[:, :].rearrange("k (oc p) -> k oc p", p=128), inc=sg.inc())
    P.op("gpsimd", "dma_start", out=bdsq[:],
         in_=d_bdsq[:, :].rearrange("p (oc j) -> p oc j", j=16), inc=sg.inc())
    xTfm = d_xT[:, :].rearrange("(dc p) t -> p dc t", p=128)
    P.op("gpsimd", "dma_start", out=wsb[:],
         in_=d_wq[:, :].rearrange("(dc p) o -> p dc o", p=128), inc=sg.inc())
    xtok = []
    for (xa, xb) in [(512, 1024), (1024, 1536), (1536, 2048), (2048, 2560),
                     (0, 512)]:
        P.op("gpsimd", "dma_start", out=xhat[:, :, xa:xb], in_=xTfm[:, :, xa:xb],
             inc=sg.inc())
        xtok.append(gbar())
    g_loads = xtok[3]
    y_loads = ybar()

    P.op("vector", "memset", onesrow[:], 1.0, inc=vc.inc())
    P.op("vector", "memset", onescf[:], 1.0, inc=vc.inc())
    P.op("vector", "memset", zz[:], 0.0, inc=vc.inc())
    P.op("vector", "memset", sidescr[:], 0.0, inc=vc.inc())
    P.op("vector", "memset",
         vhat[:].rearrange("p t (kv c) -> p t kv c", c=65)[:, :, :, 64:65],
         1.0, inc=vc.inc())
    P.op("vector", "memset", cpool[:, 0:1], RMS_EPS, inc=vc.inc())
    P.op("vector", "memset", cpool[:, 1:2], LN_EPS, inc=vc.inc())
    P.op("vector", "memset", cpool[:, 2:3], 1.0 / 64.0, inc=vc.inc())
    P.op("vector", "memset", cpool[:, 3:4], 0.125, inc=vc.inc())
    P.op("vector", "memset", cpool[:, 4:5], 1.0 / 512.0, inc=vc.inc())
    v_cpool = vc.tok()
    v_init = vc.tok()
    for i in range(16):
        P.op("gpsimd", "dma_start", out=d_fwdx[128 * i:128 * i + 128, :],
             in_=zz[:, 0:1024], waits=(v_init,) if i == 0 else (), inc=sg.inc())
    P.op("gpsimd", "dma_start", out=d_fwdx[2048:2056, :], in_=zz[0:8, 0:1024],
         inc=sg.inc())
    P.op("gpsimd", "dma_start",
         out=d_fwds[0:2048, :].rearrange("(a p) c -> p a c", p=128),
         in_=zz[:, 0:256].rearrange("p (a c) -> p a c", c=16), inc=sg.inc())
    P.op("gpsimd", "dma_start", out=d_fwds[2048:2056, :], in_=zz[0:8, 0:16],
         inc=sg.inc())
    P.op("gpsimd", "dma_start", out=d_agout[NSEL:NSEL + 1, :], in_=zz[0:1, 0:1024],
         inc=sg.inc())
    g_zf = gbar()

    P.op("vector", "tensor_mul", wqwk[:], rmsq[:], rmsk[:],
         waits=(y_loads,), inc=vc.inc())
    v_ww = vc.tok()
    P.op("gpsimd", "dma_start", out=wqwkrow2[0:1, 0:64], in_=wqwk[:],
         waits=(v_ww,), inc=sg.inc())
    P.op("gpsimd", "dma_start", out=wqwkrow2[0:1, 64:128], in_=wqwk[:],
         inc=sg.inc())
    g_ww = gbar()
    g_bdw = gbar()

    dbg_bar1 = dbg_bar2 = dbg_bar3 = dbg_bar4 = dbg_bar5 = None

    def finish_early():
        P.op("vector", "memset", otile[:, 0, :], 0.0, inc=vc.inc())
        vv = vc.tok()
        for t_ in range(NT):
            P.op("sync", "dma_start", out=d_y[128 * t_:128 * t_ + 128, :],
                 in_=otile[:, 0, :], waits=(vv,) if t_ == 0 else (),
                 inc=sy.inc())
        ybar()
        return _emit(nc, P)

    if upto <= 0:
        return finish_early()

    def proj_fm(wget, biasrow, tok_cols, out_write, sumsq_write, prewaits,
                tc_waits=None):
        last = {}
        pe_ev = {}
        it = 0
        for tci, (tc0, tn) in enumerate(tok_cols):
            pe = psE if tci % 2 == 0 else psF
            for oc in range(8):
                pp = psC if it % 2 == 0 else psD
                sq_ = sqscr[:, it % 2, 0:tn]
                tw = list(prewaits) if it < 2 else list(last[it - 2])
                if tc_waits and oc == 0 and tci in tc_waits:
                    tw = tw + [tc_waits[tci]]
                for dc in range(8):
                    P.op("tensor", "matmul", pp[:, 0:tn],
                         wget(dc, oc), xhat[:, dc, tc0:tc0 + tn],
                         start=(dc == 0), stop=False,
                         waits=tw if dc == 0 else ())
                P.op("tensor", "matmul", pp[:, 0:tn],
                     biasrow(oc), onesrow[0:1, 0:tn],
                     start=False, stop=True, inc=mm.inc())
                t_mm = mm.tok()
                P.op("scalar", "activation", out=sq_, in_=pp[:, 0:tn],
                     func=AF.Square, waits=(t_mm,), inc=sc.inc())
                s_sq = sc.tok()
                P.op("scalar", "copy", out=out_write(oc, tc0, tn),
                     in_=pp[:, 0:tn], inc=sc.inc())
                v_ev = sc.tok()
                ssw = [s_sq]
                if oc == 0 and tci >= 2:
                    ssw.append(pe_ev[tci - 2])
                P.op("tensor", "matmul", pe[0:16, 0:tn], bdsq[:, oc, :], sq_,
                     start=(oc == 0), stop=(oc == 7), waits=ssw,
                     inc=mm.inc() if oc == 7 else None)
                last[it] = (s_sq, v_ev)
                it += 1
            t_ss = mm.tok()
            P.op("scalar", "copy", out=sumsq_write(tc0, tn),
                 in_=pe[0:16, 0:tn], waits=(t_ss,), inc=sc.inc())
            pe_ev[tci] = sc.tok()
        return (sc.tok(), vc.tok())

    own_cols = [(HALO + 512 * t, 512) for t in range(4)]
    all_cols = [(512 * t, 512) for t in range(5)]

    qtail = proj_fm(
        lambda dc, oc: wsb[:, dc, 128 * oc:128 * oc + 128],
        lambda oc: browsb[0:1, 128 * oc:128 * oc + 128],
        own_cols,
        lambda oc, tc0, tn: qhat[:, oc, tc0 - HALO:tc0 - HALO + tn],
        lambda tc0, tn: sumsqQ[0:16, tc0 - HALO:tc0 - HALO + tn],
        [v_init, xtok[0]],
        tc_waits={1: xtok[1], 2: xtok[2], 3: xtok[3]})
    if upto <= 1.1:
        return finish_early()
    P.op("tensor", "transpose", psG0[0:128, 0:1], wqwkrow2[:, :],
         IDENTF[0:1, 0:1], waits=(g_ww,), inc=mm.inc())
    t_ww = mm.tok()
    P.op("scalar", "copy", out=wqwkcol[:], in_=psG0[0:128, 0:1], waits=(t_ww,),
         inc=sc.inc())
    s_wwc = sc.tok()
    q_mm_done = mm.tok()
    P.op("gpsimd", "dma_start", out=wsb[:],
         in_=d_wk[:, :].rearrange("(dc p) o -> p dc o", p=128),
         waits=(q_mm_done,), inc=sg.inc())
    g_wk = gbar()
    ktail = proj_fm(
        lambda dc, oc: wsb[:, dc, 128 * oc:128 * oc + 128],
        lambda oc: brows2[0:1, 128 * oc:128 * oc + 128],
        all_cols,
        lambda oc, tc0, tn: khat[:, oc, tc0:tc0 + tn],
        lambda tc0, tn: sumsqK[0:16, tc0:tc0 + tn],
        [g_wk, qtail[0], qtail[1]])
    if upto <= 1.2:
        return finish_early()
    k_mm_done = mm.tok()
    P.op("gpsimd", "dma_start", out=wvv,
         in_=d_wv[:, :].rearrange("(dc p) o -> p dc o", p=128),
         waits=(k_mm_done,), inc=sg.inc())
    obv, nbv = br["bv"]
    P.op("gpsimd", "dma_start", out=browsb[0:1, 0:256],
         in_=d_brows[0:1, obv:obv + nbv], inc=sg.inc())
    g_wv = gbar()

    if upto <= 1.3:
        return finish_early()
    q_sq_all = qtail[0]
    P.op("scalar", "activation", out=sumsqQ[:, :], in_=sumsqQ[:, :], func=AF.Sqrt,
         scale=cpool[0:16, 2:3], bias=cpool[0:16, 0:1],
         waits=(v_cpool, sc.tok()), inc=sc.inc())
    P.op("scalar", "activation", out=sumsqK[:, :], in_=sumsqK[:, :], func=AF.Sqrt,
         scale=cpool[0:16, 2:3], bias=cpool[0:16, 0:1], inc=sc.inc())
    s_sqrt = sc.tok()
    P.op("vector", "reciprocal", out=sumsqQ[:, :], in_=sumsqQ[:, :],
         waits=(s_sqrt,), inc=vc.inc())
    P.op("vector", "reciprocal", out=sumsqK[:, :], in_=sumsqK[:, :], inc=vc.inc())
    P.op("vector", "tensor_copy", out=invkb[:, :], in_=sumsqK[:, :],
         waits=(vc.tok(),), inc=vc.inc())
    v_inv = vc.tok()
    invq8_ev = {}
    for t in range(NT):
        pe = psE if t % 2 == 0 else psF
        tw = [v_inv] if t < 2 else [invq8_ev[t - 2]]
        P.op("tensor", "transpose", pe[0:128, 0:16],
             sumsqQ[0:16, 128 * t:128 * t + 128], IDENTF[0:16, 0:16],
             waits=tw, inc=mm.inc())
        t_tr = mm.tok()
        P.op("scalar", "activation", out=invq8[:, t, :], in_=pe[0:128, 0:16],
             func=AF.Copy, scale=cpool[:, 3:4], waits=(t_tr, v_cpool),
             inc=sc.inc())
        invq8_ev[t] = sc.tok()
    s_invq8 = sc.tok()

    if upto <= 1.4:
        return finish_early()
    klast = {}
    it = 0
    for oc in range(8):
        for tc in range(5):
            pp = psC if it % 2 == 0 else psD
            tw = [v_inv, g_bdw] if it < 2 else [klast[it - 2]]
            P.op("tensor", "matmul", pp[:, :], bdsel[:, oc, :],
                 invkb[:, 512 * tc:512 * tc + 512],
                 start=True, stop=True, waits=tw, inc=mm.inc())
            t_rep = mm.tok()
            P.op("vector", "scalar_tensor_tensor",
                 out=khat[:, oc, 512 * tc:512 * tc + 512],
                 in0=khat[:, oc, 512 * tc:512 * tc + 512], scalar=wqwkcol[:],
                 in1=pp[:, :], op0=ALU.mult, op1=ALU.mult,
                 waits=(t_rep, s_wwc), inc=vc.inc())
            klast[it] = vc.tok()
            it += 1
    v_khat = vc.tok()

    if debug:
        P.op("sync", "dma_start", out=dbg["dbg_qhat"][:, :],
             in_=qhat[:].rearrange("p a t -> p (a t)"), waits=(v_done,), inc=sy.inc())
        P.op("sync", "dma_start", out=dbg["dbg_khat"][:, :],
             in_=khat[:].rearrange("p a t -> p (a t)"), waits=(v_khat,), inc=sy.inc())
        P.op("sync", "dma_start", out=dbg["dbg_vhat"][:, :],
             in_=vhat[:].rearrange("p a t -> p (a t)"), waits=(v_done,), inc=sy.inc())
    dbg_bar1 = ybar() if debug else None

    if upto <= 2:
        return finish_early()

    slot_last = [None, None, None]
    rd_tok = {}
    r = 0
    it = 0
    for qt in range(NT):
        k0 = 128 * qt
        lo0 = max(0, 512 - 128 * qt)
        for hp in range(8):
            sS = [SRING[r], SRING[(r + 1) % 3]]
            t_mm = [None, None]
            tws = []
            for e in (0, 1):
                tw = list([v_khat, s_invq8] if it == 0 else [])
                if slot_last[(r + e) % 3] is not None:
                    tw.append(slot_last[(r + e) % 3])
                tws.append(tw)
            if lo0 < 512:
                for e in (0, 1):
                    hb = 64 * e
                    P.op("tensor", "matmul", sS[e][:, lo0:512],
                         qhat[hb:hb + 64, hp, 128 * qt:128 * qt + 128],
                         khat[hb:hb + 64, hp, k0 + lo0:k0 + 512],
                         start=True, stop=(qt < 4), waits=tws[e])
                    tws[e] = []
            for e in (0, 1):
                hb = 64 * e
                P.op("tensor", "matmul", sS[e][:, 512:640],
                     qhat[hb:hb + 64, hp, 128 * qt:128 * qt + 128],
                     khat[hb:hb + 64, hp, k0 + 512:k0 + 640],
                     start=True, stop=False, waits=tws[e])
            for e in (0, 1):
                sA = sS[e]
                if qt >= 4:
                    P.op("tensor", "matmul", sA[:, 0:128], IDENTB, MASKL,
                         start=False, stop=True)
                P.op("tensor", "matmul", sA[:, 512:640], IDENTB, MASKR,
                     start=False, stop=True, inc=mm.inc())
                t_mm[e] = mm.tok()
            for e in (0, 1):
                h = 2 * hp + e
                P.op("scalar", "activation", out=sigP[:, e, 0:640 - lo0],
                     in_=sS[e][:, lo0:640],
                     func=AF.Sigmoid, scale=invq8[:, qt, h:h + 1],
                     accum_out=sigs3[:, qt, h:h + 1],
                     waits=(t_mm[e], s_invq8) if it == 0 else (t_mm[e],),
                     inc=sc.inc())
                slot_last[(r + e) % 3] = sc.tok()
            if hp == 7:
                last_sig = sc.tok()
                P.op("vector", "tensor_scalar", out=recden[:, qt, :],
                     in0=sigs3[:, qt, :], scalar1=cfc("corr2", qt, 1),
                     scalar2=1.0 / T, op0=ALU.add, op1=ALU.mult,
                     waits=(last_sig,), inc=vc.inc())
                P.op("vector", "reciprocal", out=recden[:, qt, :],
                     in_=recden[:, qt, :], inc=vc.inc())
                rd_tok[qt] = vc.tok()
            if hp == 3 and qt >= 1:
                P.op("tensor", "matmul", PSB[7][0:1, 0:16], onescf[:],
                     recden[:, qt - 1, :], start=(qt == 1), stop=False,
                     waits=(rd_tok[qt - 1],))
            r = (r + 2) % 3
            it += 1
    P.op("tensor", "matmul", PSB[7][0:1, 0:16], onescf[:],
         recden[:, NT - 1, :], start=False, stop=True,
         waits=(rd_tok[NT - 1],), inc=mm.inc())
    t_part = mm.tok()
    s_pass1 = sc.tok()
    if debug:
        P.op("sync", "dma_start", out=dbg["dbg_sigs"][:, :],
             in_=sigs3[:].rearrange("p a t -> p (a t)"), waits=(s_pass1,), inc=sy.inc())

    vlast = {}
    for t in range(NKT):
        pp = psC if t % 2 == 0 else psD
        tw = [g_wv, s_pass1] if t < 2 else [vlast[t - 2]]
        for dc in range(8):
            P.op("tensor", "matmul", pp[:, 0:256],
                 xhat[:, dc, 128 * t:128 * t + 128], wvv[:, dc, :],
                 start=(dc == 0), stop=False, waits=tw if dc == 0 else ())
        P.op("tensor", "matmul", pp[:, 0:256], onesrow[0:1, 0:128],
             browsb[0:1, 0:256], start=False, stop=True, inc=mm.inc())
        t_mm = mm.tok()
        P.op("vector", "tensor_copy",
             out=vhat[:, t, :].rearrange("p (kv c) -> p kv c", c=65)[:, :, 0:64],
             in_=pp[:, 0:256].rearrange("p (kv c) -> p kv c", c=64),
             waits=(t_mm,), inc=vc.inc())
        vlast[t] = vc.tok()
    v_done = vc.tok()
    v_mm_done = mm.tok()
    P.op("gpsimd", "dma_start", out=wsb[:],
         in_=d_wo[:, :].rearrange("(dc p) o -> p dc o", p=128),
         waits=(v_mm_done,), inc=sg.inc())
    obo, nbo = br["bo"]
    P.op("gpsimd", "dma_start", out=brows2[0:1, 0:1024],
         in_=d_brows[0:1, obo:obo + nbo], inc=sg.inc())
    P.op("gpsimd", "dma_start", out=sw1sb[:],
         in_=d_sw1T[:, :].rearrange("(dc p) h -> p dc h", p=128),
         waits=(v_mm_done,), inc=sg.inc())
    g_wo = gbar()

    P.op("scalar", "copy", out=denrow[:], in_=PSB[7][0:1, 0:16], waits=(t_part,),
         inc=sc.inc())
    s_den = sc.tok()
    P.op("sync", "dma_start", out=d_denin[:, :], in_=denrow[:], waits=(s_den,),
         inc=sy.inc())
    g_den = ybar()
    P.op("gpsimd", "collective_compute", "AllGather", ALU.bypass,
         replica_groups=[list(range(8))], ins=[d_denin.opt()],
         outs=[d_denout.opt()], waits=(g_den,), inc=cc.inc())
    c_den = cc.tok()
    P.op("sync", "dma_start", out=densb[:], in_=d_denout[:, :], waits=(c_den,),
         inc=sy.inc())
    g_dsb = ybar()
    P.op("tensor", "matmul", psF[0:16, 0:1], densb[:, :],
         cfc("pairsel", 0, 1)[0:8, :], start=True, stop=True, waits=(g_dsb, s_den),
         inc=mm.inc())
    t_sminv = mm.tok()
    P.op("vector", "reciprocal", out=sminv[:], in_=psF[0:16, 0:1],
         waits=(t_sminv,), inc=vc.inc())
    v_r1 = vc.tok()
    P.op("vector", "tensor_copy", out=rsrep1[:, :],
         in_=sminv[:].broadcast_to([16, 128]), waits=(v_r1,), inc=vc.inc())
    v_rs1 = vc.tok()
    P.op("tensor", "transpose", psF[0:128, 0:16], rsrep1[:, :],
         IDENTF[0:16, 0:16], waits=(v_rs1,), inc=mm.inc())
    t_rs = mm.tok()
    P.op("scalar", "copy", out=rsumrep[:], in_=psF[0:128, 0:16], waits=(t_rs,),
         inc=sc.inc())
    s_rs = sc.tok()
    P.op("vector", "tensor_tensor",
         out=recden[:].rearrange("p a t -> p (a t)"),
         in0=recden[:].rearrange("p a t -> p (a t)"),
         in1=invq8[:].rearrange("p a t -> p (a t)"), op=ALU.mult,
         waits=(s_invq8,), inc=vc.inc())
    P.op("vector", "tensor_tensor", out=recden[:, :, :],
         in0=recden[:, :, :],
         in1=rsumrep[:].rearrange("p (a t) -> p a t", a=1).broadcast_to(
             [128, NT, 16]),
         op=ALU.mult, waits=(s_rs,), inc=vc.inc())
    v_c2all = vc.tok()
    c2ev = {}
    for qt in range(NT):
        pe = psE if qt % 2 == 0 else psF
        P.op("tensor", "transpose", pe[0:16, 0:128], recden[:, qt, :], IDENTF,
             waits=((v_c2all,) if qt < 2 else (c2ev[qt - 2],)), inc=mm.inc())
        t_c2 = mm.tok()
        P.op("scalar", "copy", out=c2fm[0:16, 128 * qt:128 * qt + 128],
             in_=pe[0:16, 0:128], waits=(t_c2,), inc=sc.inc())
        c2ev[qt] = sc.tok()
    s_c2 = sc.tok()
    if debug:
        P.op("sync", "dma_start", out=dbg["dbg_recden"][:, :],
             in_=recden[:].rearrange("p a t -> p (a t)"), waits=(v_rec,), inc=sy.inc())
        P.op("sync", "dma_start", out=dbg["dbg_c2"][:, :], in_=c2fm[:, :],
             waits=(s_c2,), inc=sy.inc())

    q2last = {}
    it = 0
    for oc in range(8):
        for tc in range(4):
            pp = psC if it % 2 == 0 else psD
            tw = ([s_c2, vlast[NKT - 2], vlast[NKT - 1]]
                  if it < 2 else [q2last[it - 2]])
            if debug and it == 0:
                tw.append(dbg_bar1)
            P.op("tensor", "matmul", pp[:, :], bdsel[:, oc, :],
                 c2fm[:, 512 * tc:512 * tc + 512],
                 start=True, stop=True, waits=tw, inc=mm.inc())
            t_rep = mm.tok()
            P.op("vector", "tensor_tensor",
                 out=qhat[:, oc, 512 * tc:512 * tc + 512],
                 in0=qhat[:, oc, 512 * tc:512 * tc + 512], in1=pp[:, :],
                 op=ALU.mult, waits=(t_rep,), inc=vc.inc())
            q2last[it] = vc.tok()
            it += 1
    v_q2 = vc.tok()

    if upto <= 3:
        return finish_early()

    spans = {}
    kt_of_qc = {qc: [] for qc in range(4)}
    for kt in range(NKT):
        qlo = max(0, 128 * kt - 512)
        qhi = min(NOWN, 128 * kt + 128)
        spans[kt] = (qlo, qhi)
        for qc in range(4):
            if qlo < 512 * qc + 512 and qhi > 512 * qc:
                kt_of_qc[qc].append(kt)
    first_kt = {qc: ks[0] for qc, ks in kt_of_qc.items()}
    last_kt = {qc: ks[-1] for qc, ks in kt_of_qc.items()}

    ZROW = cbf[0:1, CBF["maskL"]:CBF["maskL"] + 128]
    SS = {0: psum_all[:, 0:640], 1: psum_all[:, 3072:3712]}
    OBK = {(0, 0): PSB[2], (0, 1): PSB[3], (1, 0): PSB[4], (1, 1): PSB[5]}
    rs_dma = {0: None, 1: None}
    ob_dma = {0: None, 1: None}
    explast = {}
    ev_tok = {}
    for hp in range(8):
        kv = hp // 2
        for kt in range(NKT):
            qlo, qhi = spans[kt]
            w = qhi - qlo
            n0 = min(w, 512)
            for eo in (0, 1):
                h = 2 * hp + eo
                hb = 64 * eo
                if hp == 0 and kt == 0:
                    tw = [v_q2, v_done]
                elif kt == 0:
                    tw = [explast[h - 2]]
                else:
                    tw = [explast[h]]
                P.op("tensor", "matmul", SS[eo][:, 0:n0],
                     khat[hb:hb + 64, hp, 128 * kt:128 * kt + 128],
                     qhat[hb:hb + 64, hp, qlo:qlo + n0],
                     start=True, stop=False, waits=tw)
            if w > 512:
                for eo in (0, 1):
                    hb = 64 * eo
                    P.op("tensor", "matmul", SS[eo][:, 512:w],
                         khat[hb:hb + 64, hp, 128 * kt:128 * kt + 128],
                         qhat[hb:hb + 64, hp, qlo + 512:qlo + w],
                         start=True, stop=False)
            t_mm = {}
            for eo in (0, 1):
                sA = SS[eo]
                b0open = True
                mr_in_b0 = (kt <= 15) and (w <= 512)
                ml = (kt >= 4)
                if ml and not mr_in_b0:
                    P.op("tensor", "matmul", sA[:, 0:128], IDENTB, MASKL,
                         start=False, stop=True)
                    b0open = False
                elif ml and mr_in_b0:
                    P.op("tensor", "matmul", sA[:, 0:128], IDENTB, MASKL,
                         start=False, stop=False)
                if mr_in_b0:
                    P.op("tensor", "matmul", sA[:, w - 128:w], IDENTB, MASKR,
                         start=False, stop=True)
                    b0open = False
                if b0open:
                    P.op("tensor", "matmul", sA[:, 0:n0],
                         halocolb[0:1, 0:128], onesrow[0:1, 0:n0],
                         start=False, stop=True)
                if w > 512:
                    P.op("tensor", "matmul", sA[:, 512:w], IDENTB, MASKR,
                         start=False, stop=True)
                t_mm[eo] = P.tag_last("tensor", mm)
            for qc in range(4):
                if kt != first_kt[qc]:
                    continue
                for eo in (0, 1):
                    h = 2 * hp + eo
                    prev = (h, qc - 2) if qc >= 2 else (h - 2, qc + 2)
                    zw = [] if prev in ev_tok else [v_q2, v_done]
                    if prev in ev_tok:
                        zw.append(ev_tok[prev])
                    P.op("tensor", "matmul", OBK[(eo, qc % 2)][0:65, 0:512],
                         ZROW[0:1, 0:65], onesrow[0:1, 0:512],
                         start=True, stop=False, waits=zw)
            for eo in (0, 1):
                h = 2 * hp + eo
                if kt < 4:
                    P.op("scalar", "activation", out=sigP[:, eo, 0:w],
                         in_=SS[eo][:, 0:w], func=AF.Exp,
                         bias=cfc("haloc", kt, 1),
                         waits=(t_mm[eo],), inc=sc.inc())
                else:
                    P.op("scalar", "activation", out=sigP[:, eo, 0:w],
                         in_=SS[eo][:, 0:w], func=AF.Exp,
                         waits=(t_mm[eo],), inc=sc.inc())
                explast[h] = sc.tok()
            for qc in range(4):
                if kt not in kt_of_qc[qc]:
                    continue
                lo = max(qlo, 512 * qc)
                hi = min(qhi, 512 * qc + 512)
                for eo in (0, 1):
                    h = 2 * hp + eo
                    P.op("tensor", "matmul",
                         OBK[(eo, qc % 2)][0:65, lo - 512 * qc:hi - 512 * qc],
                         vhat[:, kt, 65 * kv:65 * kv + 65],
                         sigP[:, eo, lo - qlo:hi - qlo],
                         start=False, stop=(kt == last_kt[qc]),
                         waits=[explast[h]])
                if kt != last_kt[qc]:
                    continue
                t_pv = P.tag_last("tensor", mm)
                for eo in (0, 1):
                    h = 2 * hp + eo
                    ob = OBK[(eo, qc % 2)]
                    rw = [t_pv] if eo == 0 else []
                    if qc == 0 and rs_dma[eo] is not None:
                        rw.append(rs_dma[eo])
                    P.op("scalar", "copy",
                         out=rscr[0:1, eo, 512 * qc:512 * qc + 512],
                         in_=ob[64:65, 0:512], waits=tuple(rw), inc=sc.inc())
                    if eo == 0:
                        P.op("scalar", "copy",
                             out=ahat[0:64, hp, 512 * qc:512 * qc + 512],
                             in_=ob[0:64, 0:512], inc=sc.inc())
                    else:
                        obw = [ob_dma[qc % 2]] if ob_dma[qc % 2] is not None else []
                        P.op("scalar", "copy", out=obounce[0:64, qc % 2, :],
                             in_=ob[0:64, 0:512], waits=obw, inc=sc.inc())
                        s_ob = sc.tok()
                        P.op("sync", "dma_start",
                             out=ahat[64:128, hp, 512 * qc:512 * qc + 512],
                             in_=obounce[0:64, qc % 2, :], waits=(s_ob,),
                             inc=sy.inc())
                        ob_dma[qc % 2] = ybar()
                    ev_tok[(h, qc)] = sc.tok()
                    if qc == 3:
                        P.op("sync", "dma_start", out=rrall[h:h + 1, :],
                             in_=rscr[0:1, eo, :], waits=(ev_tok[(h, qc)],),
                             inc=sy.inc())
                        rs_dma[eo] = ybar()
    s_ahat = sc.tok()
    g_ahat = ybar()
    P.op("vector", "reciprocal", out=rrinv[:, :], in_=rrall[:, :],
         waits=(s_ahat, rs_dma[0], rs_dma[1]), inc=vc.inc())
    P.op("vector", "tensor_copy", out=rrallb[:, :], in_=rrinv[:, :], inc=vc.inc())
    v_rr = vc.tok()
    norm_tok = {}
    it = 0
    for oc in range(8):
        for tc in range(4):
            pp = psC if it % 2 == 0 else psD
            tw = [v_rr, g_ahat]
            if it >= 2:
                tw.append(norm_tok[it - 2])
            P.op("tensor", "matmul", pp[:, :], bdsel[:, oc, :],
                 rrallb[:, 512 * tc:512 * tc + 512],
                 start=True, stop=True, waits=tw, inc=mm.inc())
            t_rep = mm.tok()
            P.op("vector", "tensor_tensor",
                 out=ahat[:, oc, 512 * tc:512 * tc + 512],
                 in0=ahat[:, oc, 512 * tc:512 * tc + 512], in1=pp[:, :],
                 op=ALU.mult, waits=(t_rep,), inc=vc.inc())
            norm_tok[it] = vc.tok()
            it += 1
    v_ahat = vc.tok()
    if debug:
        P.op("sync", "dma_start", out=dbg["dbg_ahat"][:, :],
             in_=ahat[:].rearrange("p a t -> p (a t)"),
             waits=(v_ahat, g_ahat), inc=sy.inc())
    dbg_bar2 = ybar() if debug else None

    p2_mm_done = mm.tok()
    P.op("gpsimd", "dma_start", out=sw2sb[:],
         in_=d_sw2T[:, :].rearrange("(dc p) h -> p dc h", p=128),
         waits=(p2_mm_done,), inc=sg.inc())
    g_sw12 = gbar()

    if upto <= 4:
        return finish_early()

    P.op("sync", "dma_start", out=xring[:, 0, :], in_=d_xres[0:128, :],
         waits=(p2_mm_done,), inc=sy.inc())
    xres_tok = {0: ybar()}

    v_ln = {}
    zh_ev = []

    def emit_tr(t_):
        for g_ in range(2):
            tw_ = [v_ln[t_]] if g_ == 0 else []
            if zh_ev:
                tw_.append(zh_ev[-1])
            for j_ in range(4):
                dc_ = 4 * g_ + j_
                P.op("tensor", "transpose", psG0[:, 128 * j_:128 * j_ + 128],
                     strm[:, t_, 128 * dc_:128 * dc_ + 128], IDENTF,
                     waits=tw_ if j_ == 0 else (),
                     inc=mm.inc() if j_ == 3 else None)
            t_tr_ = mm.tok()
            P.op("scalar", "copy",
                 out=zhatfm[:, 4 * g_:4 * g_ + 4, 128 * t_:128 * t_ + 128],
                 in_=psG0[:, :].rearrange("p (a c) -> p a c", c=128),
                 waits=(t_tr_,), inc=sc.inc())
            zh_ev.append(sc.tok())

    ln_done = {}
    scat_done = {}
    for t in range(NT):
        if t + 1 < NT:
            P.op("sync", "dma_start", out=xring[:, (t + 1) % 2, :],
                 in_=d_xres[128 * (t + 1):128 * (t + 1) + 128, :],
                 waits=(ln_done[t - 1],) if t >= 1 else (), inc=sy.inc())
            xres_tok[t + 1] = ybar()
        banks = (psC, psD) if t % 2 == 0 else (psE, psF)
        t_ops = []
        for oci, pp in enumerate(banks):
            tw = [g_wo, g_ahat, v_ahat] if t < 2 else [ln_done[t - 2]]
            for dc in range(8):
                P.op("tensor", "matmul", pp[:, :],
                     ahat[:, dc, 128 * t:128 * t + 128],
                     wsb[:, dc, 512 * oci:512 * oci + 512],
                     start=(dc == 0), stop=False,
                     waits=tw if dc == 0 else ())
            P.op("tensor", "matmul", pp[:, :], onesrow[0:1, 0:128],
                 brows2[0:1, 512 * oci:512 * oci + 512],
                 start=False, stop=True, inc=mm.inc())
            t_ops.append(mm.tok())
        if t >= 1:
            emit_tr(t - 1)
        P.op("vector", "tensor_add", strm[:, t, 0:512], banks[0][:, :],
             xring[:, t % 2, 0:512], waits=(t_ops[0], xres_tok[t]), inc=vc.inc())
        P.op("vector", "tensor_add", strm[:, t, 512:1024], banks[1][:, :],
             xring[:, t % 2, 512:1024], waits=(t_ops[1],), inc=vc.inc())
        P.op("vector", "reduce_sum", out=lncol[:, 0:1], in_=strm[:, t, :],
             axis=AXX, inc=vc.inc())
        P.op("vector", "tensor_scalar", out=lncol[:, 1:2], in0=lncol[:, 0:1],
             scalar1=1.0 / D, scalar2=None, op0=ALU.mult, inc=vc.inc())
        v_m = vc.tok()
        P.op("scalar", "activation", out=sqdummy[:, :], in_=strm[:, t, :],
             func=AF.Square, accum_out=lncol[:, 2:3], waits=(v_m,), inc=sc.inc())
        s_sq = sc.tok()
        P.op("vector", "tensor_scalar", out=lncol[:, 3:4], in0=lncol[:, 2:3],
             scalar1=1.0 / D, scalar2=None, op0=ALU.mult, waits=(s_sq,),
             inc=vc.inc())
        P.op("vector", "tensor_mul", lncol[:, 4:5], lncol[:, 1:2], lncol[:, 1:2],
             inc=vc.inc())
        P.op("vector", "tensor_sub", lncol[:, 3:4], lncol[:, 3:4], lncol[:, 4:5],
             inc=vc.inc())
        v_var = vc.tok()
        P.op("scalar", "activation", out=lncol[:, 5:6], in_=lncol[:, 3:4],
             func=AF.Sqrt, bias=cpool[:, 1:2], waits=(v_var,), inc=sc.inc())
        s_sd = sc.tok()
        P.op("vector", "reciprocal", out=lncol[:, 6:7], in_=lncol[:, 5:6],
             waits=(s_sd,), inc=vc.inc())
        v_iv = vc.tok()
        P.op("vector", "tensor_scalar", out=strm[:, t, :], in0=strm[:, t, :],
             scalar1=lncol[:, 1:2], scalar2=lncol[:, 6:7],
             op0=ALU.subtract, op1=ALU.mult, waits=(v_iv,), inc=vc.inc())
        v_ln[t] = vc.tok()
        zw = list(scat_done[t - 2]) if t >= 2 else []
        P.op("vector", "tensor_copy", out=z1ring[:, t % 2, 0:1024],
             in_=strm[:, t, :], waits=zw, inc=vc.inc())
        ln_done[t] = vc.tok()
        P.op("gpsimd", "indirect_dma_start",
             out=d_fwdx[:, :],
             out_offset=IndirectOffsetOnAxis(idxs[:, t:t + 1], 0),
             in_=z1ring[:, t % 2, 0:1024], in_offset=None,
             waits=(ln_done[t],), inc=sg.inc())
        scat_done[t] = (gbar(),)
    emit_tr(NT - 1)
    s_zh = sc.tok()
    g_scat = gbar()
    if debug:
        P.op("sync", "dma_start", out=dbg["dbg_strm"][:, :],
             in_=strm[:].rearrange("p a t -> p (a t)"), waits=(ln_done[NT - 1],),
             inc=sy.inc())
        P.op("sync", "dma_start", out=dbg["dbg_zhat"][:, :],
             in_=zhatfm[:].rearrange("p a t -> p (a t)"),
             waits=(s_zh,), inc=sy.inc())
    dbg_bar3 = ybar() if debug else None

    g_evac = {}
    g_trd = {}
    side_tok = {}
    s_gt_tok = {}
    t_gate = {}

    def gate_mm(tc):
        pg = PSB[1] if tc % 2 == 0 else PSB[3]
        tw = [s_zh, ln_done[NT - 1]]
        if tc >= 2:
            tw.append(g_evac[tc - 2])
        for dc in range(8):
            P.op("tensor", "matmul", pg[0:9, :], egwsb[:, dc, :],
                 zhatfm[:, dc, 512 * tc:512 * tc + 512],
                 start=(dc == 0), stop=(dc == 7), waits=tw if dc == 0 else (),
                 inc=mm.inc() if dc == 7 else None)
        t_gate[tc] = mm.tok()

    def gchain(tc):
        pg = PSB[1] if tc % 2 == 0 else PSB[3]
        t_g = t_gate[tc]
        grw = [t_g] if tc < 2 else [t_g, g_trd[tc - 2]]
        P.op("scalar", "activation", out=gring[0:9, tc % 2, :], in_=pg[0:9, :],
             func=AF.Sigmoid, bias=cfc("egbc", 0, 1)[0:9, :], waits=grw,
             inc=sc.inc())
        g_evac[tc] = sc.tok()
        for ti in range(4):
            t = 4 * tc + ti
            pe = PSB[5] if t % 2 == 0 else PSB[7]
            trw = [g_evac[tc]]
            if t >= 2:
                trw.append(s_gt_tok[t - 2])
            P.op("tensor", "transpose", pe[0:128, 0:9],
                 gring[0:9, tc % 2, 128 * ti:128 * ti + 128], IDENTF[0:9, 0:9],
                 waits=trw, inc=mm.inc())
            t_tr = mm.tok()
            P.op("scalar", "copy", out=gstt[:, t % 2, 0:9], in_=pe[0:128, 0:9],
                 waits=(t_tr,), inc=sc.inc())
            s_gt = sc.tok()
            s_gt_tok[t] = s_gt
            P.op("vector", "tensor_add", scr8[:], gstt[:, t % 2, 1:9],
                 cfc("expbr", 0, 8), waits=(s_gt,), inc=vc.inc())
            P.op("vector", "reduce_max", out=mxcol[:], in_=scr8[:], axis=AXX,
                 inc=vc.inc())
            vb = vc.tok()
            P.op("vector", "tensor_single_scalar", out=scr8[:], in_=scr8[:],
                 scalar=mxcol[:], op=ALU.is_equal, waits=(vb,), inc=vc.inc())
            P.op("vector", "scalar_tensor_tensor", out=scrsm[:], in0=scr8[:],
                 scalar=1.0, in1=gstt[:, t % 2, 1:9], op0=ALU.mult, op1=ALU.mult,
                 accum_out=t1col[:], inc=vc.inc())
            vb2 = vc.tok()
            P.op("vector", "tensor_scalar", out=dcol[:, 0:1],
                 in0=gstt[:, t % 2, 0:1], scalar1=t1col[:], scalar2=None,
                 op0=ALU.subtract, waits=(vb2,), inc=vc.inc())
            P.op("vector", "tensor_scalar", out=dcol[:, 1:2], in0=dcol[:, 0:1],
                 scalar1=-1.0, scalar2=None, op0=ALU.mult, inc=vc.inc())
            v_d = vc.tok()
            P.op("scalar", "activation", out=wst[:], in_=dcol[:], func=AF.Sigmoid,
                 waits=(v_d,), inc=sc.inc())
            s_w = sc.tok()
            P.op("vector", "tensor_scalar", out=wsmcol[:, t:t + 1],
                 in0=wst[:, 0:1], scalar1=1.0, scalar2=None, op0=ALU.subtract,
                 waits=(s_w,), inc=vc.inc())
            P.op("vector", "tensor_mul", wsmcol[:, t:t + 1], wsmcol[:, t:t + 1],
                 cfc("selmask", t, 1), inc=vc.inc())
            P.op("vector", "tensor_scalar", out=wsmcol[:, t:t + 1],
                 in0=wsmcol[:, t:t + 1], scalar1=1.0, scalar2=None, op0=ALU.add,
                 inc=vc.inc())
            sw_ = [side_tok[t - 2]] if t >= 2 else []
            P.op("vector", "tensor_copy", out=sidescr[:, t % 2, 0:8],
                 in_=scr8[:], waits=sw_, inc=vc.inc())
            P.op("vector", "tensor_copy", out=sidescr[:, t % 2, 8:9],
                 in_=wst[:, 1:2], inc=vc.inc())
            v_side = vc.tok()
            P.op("gpsimd", "indirect_dma_start",
                 out=d_fwds[:, :],
                 out_offset=IndirectOffsetOnAxis(idxs[:, t:t + 1], 0),
                 in_=sidescr[:, t % 2, :], in_offset=None,
                 waits=(v_side,), inc=sg.inc())
            side_tok[t] = gbar()
        g_trd[tc] = mm.tok()
    ffn_z = {}

    def x12block(tc):
        for hc in range(4):
            it = 4 * tc + hc
            p1 = psC if it % 2 == 0 else psD
            p2_ = psE if it % 2 == 0 else psF
            tw = [g_sw12, g_wo, s_zh, ln_done[NT - 1]]
            if debug and it == 0:
                tw.append(dbg_bar2)
                tw.append(dbg_bar3)
            if it >= 2:
                tw.append(ffn_z[it - 2])
            for dc in range(8):
                P.op("tensor", "matmul", p1[:, :],
                     sw1sb[:, dc, 128 * hc:128 * hc + 128],
                     zhatfm[:, dc, 512 * tc:512 * tc + 512],
                     start=(dc == 0), stop=(dc == 7), waits=tw if dc == 0 else ())
            for dc in range(8):
                P.op("tensor", "matmul", p2_[:, :],
                     sw2sb[:, dc, 128 * hc:128 * hc + 128],
                     zhatfm[:, dc, 512 * tc:512 * tc + 512],
                     start=(dc == 0), stop=(dc == 7),
                     inc=mm.inc() if dc == 7 else None)
            t_x = mm.tok()
            P.op("scalar", "activation", out=ssping[:, it % 2, :], in_=p1[:, :],
                 func=AF.Silu, bias=cfc("sb1c", hc, 1), waits=(t_x,),
                 inc=sc.inc())
            s_sil = sc.tok()
            P.op("vector", "scalar_tensor_tensor",
                 out=zeb[:, hc, 512 * tc:512 * tc + 512], in0=p2_[:, :],
                 scalar=cfc("sb2c", hc, 1), in1=ssping[:, it % 2, :],
                 op0=ALU.add, op1=ALU.mult, waits=(s_sil,), inc=vc.inc())
            ffn_z[it] = vc.tok()

    gate_mm(0)
    gate_mm(1)
    gchain(0)
    x12block(0)
    gate_mm(2)
    gchain(1)
    x12block(1)
    gate_mm(3)
    gchain(2)
    x12block(2)
    gchain(3)
    x12block(3)
    v_z = vc.tok()

    g_side = gbar()
    v_wsm = vc.tok()
    if debug:
        P.op("sync", "dma_start", out=dbg["dbg_wsm"][:, :], in_=wsmcol[:, :],
             waits=(v_wsm,), inc=sy.inc())

    P.op("gpsimd", "collective_compute", "ReduceScatter", ALU.add,
         replica_groups=[list(range(8))], ins=[d_fwds[0:NSEL, :].opt()],
         outs=[d_rss.opt()], waits=(g_side,), inc=cc.inc())
    P.op("gpsimd", "collective_compute", "ReduceScatter", ALU.add,
         replica_groups=[list(range(8))], ins=[d_fwdx[0:NSEL, :].opt()],
         outs=[d_rsx.opt()], inc=cc.inc())
    c_rs = cc.tok()

    o_mm_done = mm.tok()

    if upto <= 5:
        return finish_early()

    P.op("gpsimd", "dma_start", out=sw3sb[:],
         in_=d_sw3T[:, :].rearrange("(hc p) o -> p hc o", p=128),
         waits=(mm.tok(),), inc=sg.inc())
    ob3, nb3 = br["sb3"]
    P.op("gpsimd", "dma_start", out=brows2[0:1, 0:1024],
         in_=d_brows[0:1, ob3:ob3 + nb3], inc=sg.inc())
    g_sw3 = gbar()
    for half in range(2):
        pe = psE if half == 0 else psF
        for hc in range(4):
            P.op("tensor", "matmul", pe[0:1, :],
                 cbfc("slnb", hc, 1), sw3sb[:, hc, 512 * half:512 * half + 512],
                 start=(hc == 0), stop=False,
                 waits=(g_sw3, ffn_z[14], ffn_z[15]) if hc == 0 else ())
        P.op("tensor", "matmul", pe[0:1, :], onesrow[0:1, 0:1],
             brows2[0:1, 512 * half:512 * half + 512], start=False, stop=True,
             inc=mm.inc())
        t_b = mm.tok()
        P.op("scalar", "copy", out=bias3e[0:1, 512 * half:512 * half + 512],
             in_=pe[0:1, :], waits=(t_b,), inc=sc.inc())
    s_b3 = sc.tok()

    sh_done = {}
    sqmm_tok = {}
    s2_done = {}
    rows_tok = {}

    def ffn_stats_a(tc):
        for hc in range(4):
            g = 4 * tc + hc
            sqw = [v_z]
            if g >= 2:
                sqw.append(sqmm_tok[g - 2])
            P.op("scalar", "activation", out=sqscr[:, hc % 2, :],
                 in_=zeb[:, hc, 512 * tc:512 * tc + 512], func=AF.Square,
                 waits=sqw, inc=sc.inc())
            s_sq = sc.tok()
            stw = [v_z, s_b3] if tc == 0 else [sh_done[tc - 1]]
            P.op("tensor", "matmul", psE[0:1, :], ONESCB,
                 zeb[:, hc, 512 * tc:512 * tc + 512],
                 start=(hc == 0), stop=(hc == 3),
                 waits=stw if hc == 0 else ())
            P.op("tensor", "matmul", psF[0:1, :], ONESCB, sqscr[:, hc % 2, :],
                 start=(hc == 0), stop=(hc == 3), waits=(s_sq,), inc=mm.inc())
            sqmm_tok[g] = mm.tok()
        t_st = mm.tok()
        P.op("scalar", "activation", out=frows[0:1, 0, :], in_=psE[0:1, :],
             func=AF.Copy, scale=cpool[0:1, 4:5], waits=(t_st,), inc=sc.inc())
        P.op("scalar", "activation", out=frows[0:1, 1, :], in_=psF[0:1, :],
             func=AF.Copy, scale=cpool[0:1, 4:5], inc=sc.inc())
        s_mu = sc.tok()
        P.op("vector", "tensor_mul", frows[0:1, 2, :], frows[0:1, 0, :],
             frows[0:1, 0, :], waits=(s_mu,), inc=vc.inc())
        P.op("vector", "tensor_sub", frows[0:1, 1, :], frows[0:1, 1, :],
             frows[0:1, 2, :], inc=vc.inc())
        v_v = vc.tok()
        P.op("scalar", "activation", out=frows[0:1, 1, :], in_=frows[0:1, 1, :],
             func=AF.Sqrt, bias=cpool[0:1, 1:2], waits=(v_v,), inc=sc.inc())
        s_s = sc.tok()
        P.op("vector", "reciprocal", out=frows[0:1, 1, :], in_=frows[0:1, 1, :],
             waits=(s_s,), inc=vc.inc())
        P.op("vector", "tensor_copy", out=invrowb[:], in_=frows[0:1, 1, :],
             inc=vc.inc())
        P.op("vector", "tensor_mul", frows[0:1, 2, :], frows[0:1, 0, :],
             frows[0:1, 1, :], inc=vc.inc())
        P.op("vector", "tensor_scalar", out=frows[0:1, 2, :],
             in0=frows[0:1, 2, :], scalar1=-1.0, scalar2=None, op0=ALU.mult,
             inc=vc.inc())
        P.op("vector", "tensor_copy", out=minrowb[:], in_=frows[0:1, 2, :],
             inc=vc.inc())
        rows_tok[tc] = vc.tok()

    def ffn_stats_b(tc):
        P.op("tensor", "matmul", psE[0:128, :], onesrow[0:1, 0:128], invrowb[:],
             start=True, stop=True, waits=(rows_tok[tc],))
        P.op("tensor", "matmul", psF[0:128, :], onesrow[0:1, 0:128], minrowb[:],
             start=True, stop=True)
        t_rep = P.tag_last("tensor", mm)
        for hc in range(4):
            P.op("vector", "scalar_tensor_tensor", out=ttscr[:, :],
                 in0=zeb[:, hc, 512 * tc:512 * tc + 512],
                 scalar=cfc("slng", hc, 1), in1=psE[0:128, :],
                 op0=ALU.mult, op1=ALU.mult,
                 waits=(t_rep,) if hc == 0 else (), inc=vc.inc())
            P.op("vector", "scalar_tensor_tensor",
                 out=sheb[:, hc, 512 * tc:512 * tc + 512],
                 in0=psF[0:128, :], scalar=cfc("slng", hc, 1),
                 in1=ttscr[:, :], op0=ALU.mult, op1=ALU.add, inc=vc.inc())
        sh_done[tc] = vc.tok()

    def ffn_s2(tc):
        for ti in range(4):
            t = 4 * tc + ti
            for oci in range(2):
                it = 2 * t + oci
                pp = psC if it % 2 == 0 else psD
                tw = [sh_done[tc], s_b3]
                if it >= 2:
                    tw.append(s2_done[it - 2])
                for hc in range(4):
                    P.op("tensor", "matmul", pp[:, :],
                         sheb[:, hc, 128 * t:128 * t + 128],
                         sw3sb[:, hc, 512 * oci:512 * oci + 512],
                         start=(hc == 0), stop=False, waits=tw if hc == 0 else ())
                P.op("tensor", "matmul", pp[:, :], onesrow[0:1, 0:128],
                     bias3e[0:1, 512 * oci:512 * oci + 512], start=False,
                     stop=True, inc=mm.inc())
                t_s = mm.tok()
                P.op("vector", "scalar_tensor_tensor",
                     out=strm[:, t, 512 * oci:512 * oci + 512], in0=pp[:, :],
                     scalar=wsmcol[:, t:t + 1],
                     in1=strm[:, t, 512 * oci:512 * oci + 512],
                     op0=ALU.mult, op1=ALU.add, waits=(t_s,), inc=vc.inc())
                s2_done[it] = vc.tok()

    for tc in range(4):
        ffn_stats_a(tc)
        if tc >= 1:
            ffn_s2(tc - 1)
        ffn_stats_b(tc)
    ffn_s2(3)
    v_sh = vc.tok()
    v_s2 = vc.tok()
    if debug:
        P.op("sync", "dma_start", out=dbg["dbg_s2"][:, :],
             in_=strm[:].rearrange("p a t -> p (a t)"), waits=(v_s2,), inc=sy.inc())
    dbg_bar4 = ybar() if debug else None

    if upto <= 6:
        return finish_early()

    P.op("gpsimd", "dma_start", out=xsel[:],
         in_=d_rsx[:, :].rearrange("(a p) c -> p a c", p=128),
         waits=(c_rs, v_s2), inc=sg.inc())
    P.op("gpsimd", "dma_start", out=xsels[:],
         in_=d_rss[:, :].rearrange("(a p) c -> p a c", p=128), inc=sg.inc())
    g_xsel = gbar()
    for ts in range(2):
        P.op("vector", "tensor_copy", out=indf[:, ts, :],
             in_=xsels[:, ts, 0:8], waits=(g_xsel,) if ts == 0 else (),
             inc=vc.inc())
        P.op("vector", "tensor_copy", out=weff[:, ts, :],
             in_=xsels[:, ts, 8:9], inc=vc.inc())
    v_ind = vc.tok()
    for ts in range(2):
        for dc in range(8):
            P.op("sync", "dma_start_transpose",
                 out=xselfm[:, dc, 128 * ts:128 * ts + 128],
                 in_=xsel[:, ts, 128 * dc:128 * dc + 128],
                 waits=(g_xsel, v_z) if (ts == 0 and dc == 0) else (),
                 inc=sy.inc())
    y_xsel = ybar()

    def load_expert12(e, slot, waits):
        e1, e2, _ = ewS[slot]
        P.op("gpsimd", "dma_start", out=e1[:],
             in_=d_ew1T[e, :, :].rearrange("(dc p) h -> p dc h", p=128),
             waits=waits, inc=sg.inc())
        P.op("gpsimd", "dma_start", out=e2[:],
             in_=d_ew2T[e, :, :].rearrange("(dc p) h -> p dc h", p=128),
             inc=sg.inc())
        return (gbar(),)

    zfree = mm.tok()
    sqmm_tok = {}
    g_e12 = {0: load_expert12(0, 0, (zfree,) + ((dbg_bar3,) if debug else ()))}
    g_e3 = {}
    t_h12 = {}
    t_out = {}
    h12_done = {}
    v_ze = {}
    st = {"s_be_prev": s_b3, "v_she_prev": v_s2, "prev_acc": {},
          "acc_last": None}

    def emit_h12(e):
        slot = e % 2
        e1, e2, _ = ewS[slot]
        zb = zebm if slot == 0 else zebm2
        h12 = {}
        for hc in range(4):
            p1 = PSB[0] if hc % 2 == 0 else PSB[1]
            p2_ = PSB[2] if hc % 2 == 0 else PSB[3]
            tw = [*g_e12[e], y_xsel]
            if hc >= 2:
                tw.append(h12[hc - 2])
            elif e >= 1:
                tw.append(h12_done[e - 1])
            for dc in range(8):
                P.op("tensor", "matmul", p1[:, 0:256],
                     e1[:, dc, 128 * hc:128 * hc + 128], xselfm[:, dc, :],
                     start=(dc == 0), stop=(dc == 7), waits=tw if dc == 0 else ())
            for dc in range(8):
                P.op("tensor", "matmul", p2_[:, 0:256],
                     e2[:, dc, 128 * hc:128 * hc + 128], xselfm[:, dc, :],
                     start=(dc == 0), stop=(dc == 7),
                     inc=mm.inc() if dc == 7 else None)
            t_h = mm.tok()
            P.op("scalar", "activation", out=ssping[:, hc % 2, 0:256],
                 in_=p1[:, 0:256], func=AF.Silu, bias=cfc("eb1c", 4 * e + hc, 1),
                 waits=(t_h,), inc=sc.inc())
            s_sil = sc.tok()
            P.op("vector", "scalar_tensor_tensor", out=zb[:, hc, :],
                 in0=p2_[:, 0:256], scalar=cfc("eb2c", 4 * e + hc, 1),
                 in1=ssping[:, hc % 2, 0:256], op0=ALU.add, op1=ALU.mult,
                 waits=(s_sil,), inc=vc.inc())
            h12[hc] = vc.tok()
        v_ze[e] = vc.tok()
        t_h12[e] = mm.tok()
        h12_done[e] = vc.tok()

    def tail_a(e):
        slot = e % 2
        _, _, e3 = ewS[slot]
        zb = zebm if slot == 0 else zebm2
        for hc in range(4):
            g = 4 * e + hc
            sqw = [v_ze[e]]
            if g >= 2:
                sqw.append(sqmm_tok[g - 2])
            P.op("scalar", "activation", out=sqscr[:, hc % 2, 0:256],
                 in_=zb[:, hc, :], func=AF.Square, waits=sqw, inc=sc.inc())
            s_sq = sc.tok()
            P.op("tensor", "matmul", PSB[4][0:1, 0:256], ONESCB, zb[:, hc, :],
                 start=(hc == 0), stop=(hc == 3),
                 waits=(v_ze[e], st["v_she_prev"]) if hc == 0 else ())
            P.op("tensor", "matmul", PSB[5][0:1, 0:256], ONESCB,
                 sqscr[:, hc % 2, 0:256],
                 start=(hc == 0), stop=(hc == 3), waits=(s_sq,), inc=mm.inc())
            sqmm_tok[g] = mm.tok()
        t_st = mm.tok()
        P.op("scalar", "activation", out=frows[0:1, 0, 0:256],
             in_=PSB[4][0:1, 0:256], func=AF.Copy, scale=cpool[0:1, 4:5],
             waits=(t_st,), inc=sc.inc())
        P.op("scalar", "activation", out=frows[0:1, 1, 0:256],
             in_=PSB[5][0:1, 0:256], func=AF.Copy, scale=cpool[0:1, 4:5],
             inc=sc.inc())
        s_mu = sc.tok()
        P.op("vector", "tensor_mul", frows[0:1, 2, 0:256], frows[0:1, 0, 0:256],
             frows[0:1, 0, 0:256], waits=(s_mu,), inc=vc.inc())
        P.op("vector", "tensor_sub", frows[0:1, 1, 0:256], frows[0:1, 1, 0:256],
             frows[0:1, 2, 0:256], inc=vc.inc())
        v_v = vc.tok()
        P.op("scalar", "activation", out=frows[0:1, 1, 0:256],
             in_=frows[0:1, 1, 0:256], func=AF.Sqrt, bias=cpool[0:1, 1:2],
             waits=(v_v,), inc=sc.inc())
        s_s = sc.tok()
        P.op("vector", "reciprocal", out=frows[0:1, 1, 0:256],
             in_=frows[0:1, 1, 0:256], waits=(s_s,), inc=vc.inc())
        P.op("vector", "tensor_copy", out=invrowb[0:1, 0:256],
             in_=frows[0:1, 1, 0:256], inc=vc.inc())
        P.op("vector", "tensor_mul", frows[0:1, 2, 0:256], frows[0:1, 0, 0:256],
             frows[0:1, 1, 0:256], inc=vc.inc())
        P.op("vector", "tensor_scalar", out=frows[0:1, 2, 0:256],
             in0=frows[0:1, 2, 0:256], scalar1=-1.0, scalar2=None, op0=ALU.mult,
             inc=vc.inc())
        P.op("vector", "tensor_copy", out=minrowb[0:1, 0:256],
             in_=frows[0:1, 2, 0:256], inc=vc.inc())
        st["v_rows"] = vc.tok()
        for half in range(2):
            pe = PSB[6] if half == 0 else PSB[7]
            bw = [g_e3[e]]
            if e == 0:
                bw += [v_s2, s_b3]
            else:
                bw.append(st["prev_acc"][2 + half])
            for hc in range(4):
                P.op("tensor", "matmul", pe[0:1, :], cbfc("selnb", hc, 1),
                     e3[:, hc, 512 * half:512 * half + 512],
                     start=(hc == 0), stop=False,
                     waits=bw if hc == 0 else ())
            P.op("tensor", "matmul", pe[0:1, :], onesrow[0:1, 0:1],
                 brows2[0:1, 512 * half:512 * half + 512], start=False,
                 stop=True, inc=mm.inc())
            t_b = mm.tok()
            P.op("scalar", "copy", out=bias3e[0:1, 512 * half:512 * half + 512],
                 in_=pe[0:1, :], waits=(t_b,), inc=sc.inc())
        st["s_be"] = sc.tok()
        st["s_be_prev"] = st["s_be"]

    def tail_b(e):
        slot = e % 2
        _, _, e3 = ewS[slot]
        zb = zebm if slot == 0 else zebm2
        P.op("tensor", "matmul", PSB[4][0:128, 0:256], onesrow[0:1, 0:128],
             invrowb[0:1, 0:256], start=True, stop=True, waits=(st["v_rows"],))
        P.op("tensor", "matmul", PSB[5][0:128, 0:256], onesrow[0:1, 0:128],
             minrowb[0:1, 0:256], start=True, stop=True)
        t_rep = P.tag_last("tensor", mm)
        for hc in range(4):
            P.op("vector", "scalar_tensor_tensor", out=ttscr[:, 0:256],
                 in0=zb[:, hc, :], scalar=cfc("selng", hc, 1),
                 in1=PSB[4][0:128, 0:256], op0=ALU.mult, op1=ALU.mult,
                 waits=(t_rep,) if hc == 0 else (), inc=vc.inc())
            P.op("vector", "scalar_tensor_tensor", out=shebm[:, hc, :],
                 in0=PSB[5][0:128, 0:256], scalar=cfc("selng", hc, 1),
                 in1=ttscr[:, 0:256], op0=ALU.mult, op1=ALU.add, inc=vc.inc())
        v_she = vc.tok()
        st["v_she_prev"] = v_she
        acc_now = {}
        for ts in range(2):
            for oci in range(2):
                it = 2 * ts + oci
                pp = PSB[6] if it % 2 == 0 else PSB[7]
                tw = [v_she, st["s_be"]]
                if e > 0:
                    tw.append(st["prev_acc"][it])
                if it >= 2:
                    tw.append(acc_now[it - 2])
                for hc in range(4):
                    P.op("tensor", "matmul", pp[:, :],
                         shebm[:, hc, 128 * ts:128 * ts + 128],
                         e3[:, hc, 512 * oci:512 * oci + 512],
                         start=(hc == 0), stop=False, waits=tw if hc == 0 else ())
                P.op("tensor", "matmul", pp[:, :], onesrow[0:1, 0:128],
                     bias3e[0:1, 512 * oci:512 * oci + 512], start=False,
                     stop=True, inc=mm.inc())
                t_o = mm.tok()
                if e == 0:
                    P.op("vector", "tensor_scalar",
                         out=eacc[:, ts, 512 * oci:512 * oci + 512],
                         in0=pp[:, :], scalar1=indf[:, ts, e:e + 1],
                         scalar2=None, op0=ALU.mult, waits=(t_o, v_ind),
                         inc=vc.inc())
                else:
                    P.op("vector", "scalar_tensor_tensor",
                         out=eacc[:, ts, 512 * oci:512 * oci + 512],
                         in0=pp[:, :], scalar=indf[:, ts, e:e + 1],
                         in1=eacc[:, ts, 512 * oci:512 * oci + 512],
                         op0=ALU.mult, op1=ALU.add, waits=(t_o,), inc=vc.inc())
                acc_now[it] = vc.tok()
        t_out[e] = mm.tok()
        st["prev_acc"] = acc_now
        st["acc_last"] = vc.tok()

    emit_h12(0)
    o3, _ = br["eb3"]
    for e in range(8):
        if e + 1 < 8:
            g_e12[e + 1] = load_expert12(
                e + 1, (e + 1) % 2, (t_h12[e - 1],) if e >= 1 else (zfree,))
        P.op("gpsimd", "dma_start", out=ewS[e % 2][2][:],
             in_=d_ew3T[e, :, :].rearrange("(hc p) o -> p hc o", p=128),
             waits=(t_h12[e],) if e < 2 else (t_out[e - 2],), inc=sg.inc())
        P.op("gpsimd", "dma_start", out=brows2[0:1, 0:1024],
             in_=d_brows[0:1, o3 + 1024 * e:o3 + 1024 * e + 1024],
             waits=(st["s_be_prev"],), inc=sg.inc())
        g_e3[e] = gbar()
        tail_a(e)
        if e + 1 < 8:
            emit_h12(e + 1)
        tail_b(e)
    moe_prev_out = st["acc_last"]
    for ts in range(2):
        P.op("vector", "tensor_scalar", out=eacc[:, ts, :], in0=eacc[:, ts, :],
             scalar1=weff[:, ts, 0:1], scalar2=None, op0=ALU.mult,
             waits=(moe_prev_out,), inc=vc.inc())
    v_eacc = vc.tok()
    if debug:
        P.op("sync", "dma_start", out=dbg["dbg_eacc"][:, :],
             in_=eacc[:].rearrange("p a t -> p (a t)"), waits=(v_eacc,), inc=sy.inc())
    dbg_bar5 = ybar() if debug else None
    P.op("gpsimd", "dma_start",
         out=d_agin[:, :].rearrange("(a p) t -> p a t", p=128),
         in_=eacc[:], waits=(v_eacc,), inc=sg.inc())
    g_agin = gbar()
    P.op("gpsimd", "collective_compute", "AllGather", ALU.bypass,
         replica_groups=[list(range(8))], ins=[d_agin.opt()],
         outs=[d_agout[0:NSEL, :].opt()], waits=(g_agin,), inc=cc.inc())
    c_ag = cc.tok()

    if upto <= 7:
        return finish_early()

    radd = {}
    v_iv2h = {}
    for tp in range(NT // 2):
        for sub in range(2):
            t = 2 * tp + sub
            gw = []
            if t == 0:
                gw = [c_ag] + ([dbg_bar4] if debug else [])
            if t >= 2:
                gw.append(radd[t - 2])
            P.op("gpsimd", "indirect_dma_start",
                 out=retbuf[:, t % 2, :], out_offset=None,
                 in_=d_agout[:, :],
                 in_offset=IndirectOffsetOnAxis(idxs[:, 16 + t:16 + t + 1], 0),
                 waits=gw, inc=sg.inc())
        g_t = gbar()
        for sub in range(2):
            t = 2 * tp + sub
            P.op("vector", "tensor_add", strm[:, t, :], strm[:, t, :],
                 retbuf[:, t % 2, :], waits=(g_t, v_s2) if t == 0 else
                 ((g_t,) if sub == 0 else ()),
                 inc=vc.inc())
            radd[t] = vc.tok()
            P.op("vector", "reduce_sum", out=ln2c[:, 0, t:t + 1],
                 in_=strm[:, t, :], axis=AXX, inc=vc.inc())
            P.op("scalar", "activation", out=sqdummy[:, :], in_=strm[:, t, :],
                 func=AF.Square, accum_out=ln2c[:, 1, t:t + 1],
                 waits=(radd[t],), inc=sc.inc())
        t = 2 * tp + 1
        if t in (7, 15):
            hf = t // 8
            cs = slice(8 * hf, 8 * hf + 8)
            s_st = sc.tok()
            P.op("vector", "tensor_scalar", out=ln2c[:, 2, cs],
                 in0=ln2c[:, 0, cs], scalar1=1.0 / D, scalar2=None,
                 op0=ALU.mult, waits=(s_st,), inc=vc.inc())
            P.op("vector", "tensor_scalar", out=ln2c[:, 3, cs],
                 in0=ln2c[:, 1, cs], scalar1=1.0 / D, scalar2=None,
                 op0=ALU.mult, inc=vc.inc())
            P.op("vector", "tensor_mul", ln2c[:, 4, cs], ln2c[:, 2, cs],
                 ln2c[:, 2, cs], inc=vc.inc())
            P.op("vector", "tensor_sub", ln2c[:, 3, cs], ln2c[:, 3, cs],
                 ln2c[:, 4, cs], inc=vc.inc())
            v_var = vc.tok()
            P.op("scalar", "activation", out=ln2c[:, 4, cs], in_=ln2c[:, 3, cs],
                 func=AF.Sqrt, bias=cpool[:, 1:2], waits=(v_var,), inc=sc.inc())
            s_sd = sc.tok()
            P.op("vector", "reciprocal", out=ln2c[:, 5, cs], in_=ln2c[:, 4, cs],
                 waits=(s_sd,), inc=vc.inc())
            v_iv2h[hf] = vc.tok()
    st_tok = {}
    for t in range(NT):
        ow = [v_iv2h[t // 8]]
        if debug and t == 0:
            ow.append(dbg_bar5)
        if t >= 2:
            ow.append(st_tok[t - 2])
        P.op("vector", "tensor_scalar", out=otile[:, t % 2, :],
             in0=strm[:, t, :], scalar1=ln2c[:, 2, t:t + 1],
             scalar2=ln2c[:, 5, t:t + 1],
             op0=ALU.subtract, op1=ALU.mult, waits=ow, inc=vc.inc())
        v_o = vc.tok()
        if t % 2 == 0:
            P.op("sync", "dma_start", out=d_y[128 * t:128 * t + 128, :],
                 in_=otile[:, t % 2, :], waits=(v_o,), inc=sy.inc())
            st_tok[t] = ybar()
        else:
            P.op("scalar", "dma_start", out=d_y[128 * t:128 * t + 128, :],
                 in_=otile[:, t % 2, :], waits=(v_o,), inc=ssd.inc())
            st_tok[t] = sbar()
    ybar()
    sbar()

    return _emit(nc, P)


def _emit(nc, P):
    from contextlib import ExitStack
    es = ExitStack()
    with es:
        for s in P.sems:
            s.h = es.enter_context(nc.semaphore(s.name))
        block = es.enter_context(nc.Block())

        @block.tensor
        def _(eng):
            P.replay(eng, "tensor")

        @block.scalar
        def _(eng):
            P.replay(eng, "scalar")

        @block.vector
        def _(eng):
            P.replay(eng, "vector")

        @block.gpsimd
        def _(eng):
            P.replay(eng, "gpsimd")

        @block.sync
        def _(eng):
            P.replay(eng, "sync")

    return nc


def kernel(**inputs):
    ims = []
    br = None
    for c in range(8):
        im, br = _hostpack(inputs, c)
        ims.append(im)
    nc = build(br)
    res = run_bass_kernel_spmd(nc, ims, core_ids=list(range(8)))
    out = np.zeros((B, T, D), np.float32)
    for c in range(8):
        b_, half = c // 2, c % 2
        out[b_, half * NOWN:(half + 1) * NOWN] = res.results[c]["y"]
    return out


# version 21 (final)
# speedup vs baseline: 1.0483x; 1.0084x over previous
import numpy as np
import ml_dtypes
import concourse.bass as bass
import concourse.mybir as mybir
from concourse.bass import IndirectOffsetOnAxis
from concourse.bass_utils import run_bass_kernel_spmd

F32 = mybir.dt.float32
BF16 = mybir.dt.bfloat16
I32 = mybir.dt.int32
AF = mybir.ActivationFunctionType
ALU = mybir.AluOpType
AXX = mybir.AxisListType.X

B, T, D = 4, 4096, 1024
H, HV, HD, WIN = 16, 4, 64, 512
NOWN, HALO, NKEY = 2048, 512, 2560
NT = NOWN // 128
NKT = NKEY // 128
NSEL = 2048
NBLK = NSEL // 8
RMS_EPS = 1.1920929e-07
LN_EPS = 1e-5
BIG = 1.0e30
ENGS = ("tensor", "scalar", "vector", "gpsimd", "sync")


class Sem:
    __slots__ = ("name", "step", "n", "h")

    def __init__(self, name, step):
        self.name, self.step, self.n, self.h = name, step, 0, None

    def inc(self):
        self.n += self.step
        return self

    def tok(self):
        return (self, self.n)


class Prog:
    def __init__(self):
        self.ops = {e: [] for e in ENGS}
        self.sems = []
        self.autobar = {}

    def sem(self, name, step):
        s = Sem(name, step)
        self.sems.append(s)
        return s

    def op(self, eng, meth, *args, waits=(), inc=None, **kw):
        waits = tuple(waits)
        ab = self.autobar.get(eng)
        if ab is not None and meth not in (None, "nop"):
            abv = ab.n - (ab.step if inc is ab else 0)
            if abv > 0:
                waits = waits + ((ab, abv),)
        for (s, v) in waits:
            assert v <= s.n, f"wait on future {s.name}:{v}>{s.n}"
        self.ops[eng].append([waits, meth, args, kw, inc])

    def tag_last(self, eng, sem):
        row = self.ops[eng][-1]
        assert row[4] is None
        row[4] = sem.inc()
        return sem.tok()

    def replay(self, eng_obj, ename):
        for waits, meth, args, kw, inc in self.ops[ename]:
            for (s, v) in waits:
                eng_obj.wait_ge(s.h, v)
            if meth is None:
                continue
            ins = getattr(eng_obj, meth)(*args, **kw)
            if inc is not None:
                ins.then_inc(inc.h, inc.step)


def _hostpack(inputs, core):
    x = inputs["L0_attn_x"]
    f32 = np.float32
    b_, half = core // 2, core % 2
    t0 = half * NOWN

    xb = np.zeros((NKEY, D), f32)
    xb[HALO:] = x[b_, t0:t0 + NOWN]
    if half == 1:
        xb[:HALO] = x[b_, t0 - HALO:t0]
    im = {}
    im["xT"] = np.ascontiguousarray(xb.T)
    im["xres"] = np.ascontiguousarray(x[b_, t0:t0 + NOWN])
    im["wq"] = inputs["L0_attn_w_q"]
    im["wk"] = inputs["L0_attn_w_k"]
    im["wv"] = inputs["L0_attn_w_v"]
    im["wo"] = inputs["L0_attn_w_o"]
    im["sw1T"] = np.ascontiguousarray(inputs["L0_ffn_sw1"].T)
    im["sw2T"] = np.ascontiguousarray(inputs["L0_ffn_sw2"].T)
    im["sw3T"] = np.ascontiguousarray(inputs["L0_ffn_sw3"].T)
    im["egwT"] = np.ascontiguousarray(
        np.concatenate([inputs["L0_ffn_sg_w"], inputs["L0_ffn_eg_w"]], axis=0).T)
    im["ew1T"] = np.ascontiguousarray(inputs["L0_ffn_ew1"].transpose(0, 2, 1))
    im["ew2T"] = np.ascontiguousarray(inputs["L0_ffn_ew2"].transpose(0, 2, 1))
    im["ew3T"] = np.ascontiguousarray(inputs["L0_ffn_ew3"].transpose(0, 2, 1))

    br = {}
    cur = [0]
    parts = []

    def badd(name, arr):
        a = np.asarray(arr, f32).reshape(-1)
        br[name] = (cur[0], a.size)
        cur[0] += a.size
        parts.append(a)
    badd("bq", inputs["L0_attn_b_q"])
    badd("bk", inputs["L0_attn_b_k"])
    badd("bv", inputs["L0_attn_b_v"])
    badd("bo", inputs["L0_attn_b_o"])
    badd("sb1", inputs["L0_ffn_sb1"])
    badd("sb2", inputs["L0_ffn_sb2"])
    badd("sb3", inputs["L0_ffn_sb3"])
    badd("egb", np.concatenate([inputs["L0_ffn_sg_b"], inputs["L0_ffn_eg_b"]]))
    badd("eb1", inputs["L0_ffn_eb1"])
    badd("eb2", inputs["L0_ffn_eb2"])
    badd("eb3", inputs["L0_ffn_eb3"])
    im["brows"] = np.concatenate(parts).reshape(1, -1)

    p = np.arange(128)
    maskL = np.where(p[None, :] < p[:, None], -BIG, 0.0).astype(f32)
    maskR = np.where(p[None, :] > p[:, None], -BIG, 0.0).astype(f32)
    ident = np.eye(128, dtype=f32)
    bd2 = np.zeros((128, 2), f32)
    bd2[0:64, 0] = 1.0
    bd2[64:128, 1] = 1.0
    onescolv = np.ones((128, 1), f32)
    slnb_c = np.asarray(inputs["L0_ffn_s_ln_b"], f32).reshape(4, 128).T
    selnb_c = np.asarray(inputs["L0_ffn_sel_ln_b"], f32).reshape(4, 128).T
    cbf = np.concatenate([maskL, maskR, ident, bd2, onescolv,
                          slnb_c, selnb_c], axis=1)
    im["cbf_src"] = np.ascontiguousarray(cbf)

    gi = t0 + np.arange(NOWN)
    nwin = np.minimum(gi, WIN) + 1
    corr2 = (0.5 * (T - nwin) + 0.5 * T).astype(f32).reshape(NT, 128).T
    ib = np.asarray(inputs["L0_ffn_index_b"]).astype(np.int64)
    it = np.asarray(inputs["L0_ffn_index_t"]).astype(np.int64)
    oc = ib * 2 + (it // NOWN)
    fwd = np.full(NOWN, NSEL, np.int64)
    selm = np.zeros(NOWN, f32)
    for m in range(NSEL):
        if oc[m] == core:
            r = int(it[m] % NOWN)
            fwd[r] = m
            selm[r] = 1.0
    selmask = selm.reshape(NT, 128).T
    pairsel = np.zeros((128, 1), f32)
    pairsel[core, 0] = 1.0
    pairsel[core ^ 1, 0] = 1.0
    expbr = np.broadcast_to(
        np.asarray(inputs["L0_ffn_exp_bias"], f32)[None, :], (128, 8)).copy()
    slng_c = np.asarray(inputs["L0_ffn_s_ln_g"], f32).reshape(4, 128).T
    selng_c = np.asarray(inputs["L0_ffn_sel_ln_g"], f32).reshape(4, 128).T
    sb1c = np.asarray(inputs["L0_ffn_sb1"], f32).reshape(4, 128).T
    sb2c = np.asarray(inputs["L0_ffn_sb2"], f32).reshape(4, 128).T
    egbc = np.zeros((128, 1), f32)
    egbc[0:9, 0] = np.concatenate([np.asarray(inputs["L0_ffn_sg_b"], f32),
                                   np.asarray(inputs["L0_ffn_eg_b"], f32)])
    eb1c = np.asarray(inputs["L0_ffn_eb1"], f32).reshape(8, 4, 128)
    eb1c = eb1c.transpose(2, 0, 1).reshape(128, 32)
    eb2c = np.asarray(inputs["L0_ffn_eb2"], f32).reshape(8, 4, 128)
    eb2c = eb2c.transpose(2, 0, 1).reshape(128, 32)
    haloc = np.zeros((128, 4), f32)
    if half == 0:
        haloc[:] = -BIG
    cfm = np.concatenate([corr2, selmask, pairsel, expbr, slng_c, selng_c,
                          ident, sb1c, sb2c, egbc, eb1c, eb2c, haloc], axis=1)
    im["cf_src"] = np.ascontiguousarray(cfm)
    im["idxg"] = np.ascontiguousarray(np.concatenate(
        [fwd.reshape(NT, 128).T, fwd.reshape(NT, 128).T],
        axis=1).astype(np.int32))
    halocol = np.zeros((1, HALO), f32)
    if half == 0:
        halocol[:] = -BIG
    im["halocol"] = halocol
    bdsel = np.zeros((16, 1024), f32)
    for occ in range(8):
        for pp_ in range(128):
            bdsel[2 * occ + pp_ // 64, 128 * occ + pp_] = 1.0
    im["bdsel"] = bdsel
    bdsq = np.zeros((128, 8, 16), f32)
    for occ in range(8):
        for pp_ in range(128):
            bdsq[pp_, occ, 2 * occ + pp_ // 64] = 1.0
    im["bdsq"] = bdsq.reshape(128, 128)
    im["rmsrow"] = np.concatenate([
        np.asarray(inputs["L0_attn_rms_q_w"], f32),
        np.asarray(inputs["L0_attn_rms_k_w"], f32)]).reshape(1, 128)
    return im, br


CBF = {"maskL": 0, "maskR": 128, "identb": 256,
       "bd2": 384, "onescol": 386, "slnb": 387, "selnb": 391}
NCBF = 395
CF = {"corr2": 0, "selmask": 16, "pairsel": 32, "expbr": 33,
      "slng": 41, "selng": 45, "identf": 49,
      "sb1c": 177, "sb2c": 181, "egbc": 185, "eb1c": 186, "eb2c": 218,
      "haloc": 250}
NCF = 254


def build(br, debug=False, upto=9):
    nc = bass.Bass("TRN2", target_bir_lowering=False, debug=False)
    P = Prog()

    d_xT = nc.dram_tensor("xT", [D, NKEY], F32, kind="ExternalInput").ap()
    d_xres = nc.dram_tensor("xres", [NOWN, D], F32, kind="ExternalInput").ap()
    d_wq = nc.dram_tensor("wq", [D, D], F32, kind="ExternalInput").ap()
    d_wk = nc.dram_tensor("wk", [D, D], F32, kind="ExternalInput").ap()
    d_wv = nc.dram_tensor("wv", [D, 256], F32, kind="ExternalInput").ap()
    d_wo = nc.dram_tensor("wo", [D, D], F32, kind="ExternalInput").ap()
    d_sw1T = nc.dram_tensor("sw1T", [D, 512], F32, kind="ExternalInput").ap()
    d_sw2T = nc.dram_tensor("sw2T", [D, 512], F32, kind="ExternalInput").ap()
    d_sw3T = nc.dram_tensor("sw3T", [512, D], F32, kind="ExternalInput").ap()
    d_egwT = nc.dram_tensor("egwT", [D, 9], F32, kind="ExternalInput").ap()
    d_ew1T = nc.dram_tensor("ew1T", [8, D, 512], F32, kind="ExternalInput").ap()
    d_ew2T = nc.dram_tensor("ew2T", [8, D, 512], F32, kind="ExternalInput").ap()
    d_ew3T = nc.dram_tensor("ew3T", [8, 512, D], F32, kind="ExternalInput").ap()
    NB = sum(n for (_, n) in br.values())
    d_brows = nc.dram_tensor("brows", [1, NB], F32, kind="ExternalInput").ap()
    d_cbf = nc.dram_tensor("cbf_src", [128, NCBF], F32, kind="ExternalInput").ap()
    d_cf = nc.dram_tensor("cf_src", [128, NCF], F32, kind="ExternalInput").ap()
    d_idx = nc.dram_tensor("idxg", [128, 32], I32, kind="ExternalInput").ap()
    d_halocol = nc.dram_tensor("halocol", [1, HALO], F32, kind="ExternalInput").ap()
    d_rms = nc.dram_tensor("rmsrow", [1, 128], F32, kind="ExternalInput").ap()
    d_bdsel = nc.dram_tensor("bdsel", [16, 1024], F32, kind="ExternalInput").ap()
    d_bdsq = nc.dram_tensor("bdsq", [128, 128], F32, kind="ExternalInput").ap()
    d_y = nc.dram_tensor("y", [NOWN, D], F32, kind="ExternalOutput").ap()

    d_fwdx = nc.dram_tensor("fwd_x", [NSEL + 8, 1024], BF16).ap()
    d_fwds = nc.dram_tensor("fwd_s", [NSEL + 8, 16], BF16).ap()
    d_rsx = nc.dram_tensor("rs_x", [NBLK, 1024], BF16).ap()
    d_rss = nc.dram_tensor("rs_s", [NBLK, 16], BF16).ap()
    d_agin = nc.dram_tensor("ag_in", [NBLK, D], BF16).ap()
    d_agout = nc.dram_tensor("ag_out", [NSEL + 1, D], BF16).ap()
    d_denin = nc.dram_tensor("den_in", [1, 16], F32).ap()
    d_denout = nc.dram_tensor("den_out", [2, 16], F32).ap()
    dbg = {}
    if debug:
        for nm, shp, dt in [
                ("dbg_qhat", [128, 8 * NOWN], BF16), ("dbg_khat", [128, 8 * NKEY], BF16),
                ("dbg_vhat", [128, NKT * 260], BF16), ("dbg_sigs", [128, NT * 16], F32),
                ("dbg_recden", [128, NT * 16], F32), ("dbg_c2", [16, NOWN], BF16),
                ("dbg_ahat", [128, 8 * NOWN], BF16), ("dbg_strm", [128, NT * D], F32),
                ("dbg_wsm", [128, NT], F32), ("dbg_zhat", [128, 8 * NOWN], BF16),
                ("dbg_sheb", [128, 4 * NOWN], BF16), ("dbg_eacc", [128, 2 * D], F32),
                ("dbg_s2", [128, NT * D], F32)]:
            dbg[nm] = nc.dram_tensor(nm, shp, dt, kind="ExternalOutput").ap()

    SB0 = 16512
    TOP = 229344
    R1 = SB0
    R2 = R1 + 40960
    R3 = R2 + 73728
    R4 = R3 + 10432
    R5 = R4 + 32768
    R6 = R5 + 16384
    SM = R6 + 26624
    A = {}

    def mk(name, shape, dtype, at):
        A[name] = nc.alloc_sbuf_tensor_at(name, list(shape), dtype, offset=at)
        return A[name]

    xhat = mk("xhat", [128, 8, NKEY], BF16, R1)
    ahat = mk("ahat", [128, 8, NOWN], BF16, R1)
    zeb = mk("zeb", [128, 4, NOWN], BF16, R1)
    sheb = mk("sheb", [128, 4, NOWN], BF16, R1 + 16384)

    qhat = mk("qhat", [128, 8, NOWN], BF16, R2)
    khat = mk("khat", [128, 8, NKEY], BF16, R2 + 32768)
    strm = mk("strm", [128, NT, D], F32, R2)
    xring = mk("xring", [128, 2, D], F32, R2 + 65536)

    vhat = mk("vhat", [128, NKT, 260], BF16, R3)
    xselfm = mk("xselfm", [128, 8, NBLK], BF16, R3)
    zebm = mk("zebm", [128, 4, NBLK], BF16, R3 + 4096)
    shebm = mk("shebm", [128, 4, NBLK], BF16, R3 + 6144)
    zebm2 = mk("zebm2", [128, 4, NBLK], BF16, R3 + 8192)

    sumsqK = mk("sumsqK", [16, NKEY], F32, R4)
    invkb = mk("invkb", [16, NKEY], BF16, R4 + 10240)
    sumsqQ = mk("sumsqQ", [16, NOWN], F32, R4 + 15360)
    sigP = mk("sigP", [128, 2, 640], BF16, R4 + 23552)
    c2fm = mk("c2fm", [16, NOWN], BF16, R4 + 26112)
    zhatfm = mk("zhatfm", [128, 8, NOWN], BF16, R4)
    ewS = [
        (mk("ew1S0", [128, 8, 512], BF16, R4),
         mk("ew2S0", [128, 8, 512], BF16, R4 + 8192),
         mk("ew3S0", [128, 4, 1024], BF16, R2 + 65536)),
        (mk("ew1S1", [128, 8, 512], BF16, R4 + 16384),
         mk("ew2S1", [128, 8, 512], BF16, R4 + 24576),
         mk("ew3S1", [128, 4, 1024], BF16, R1 + 32768)),
    ]

    wsb = mk("wsb", [128, 8, 1024], BF16, R5)
    sw1sb = mk("sw1sb", [128, 8, 512], BF16, R1 + 32768)
    sw2sb = mk("sw2sb", [128, 8, 512], BF16, R3)
    sw3sb = mk("sw3sb", [128, 4, 1024], BF16, R5)
    wvv = wsb[:].rearrange("p a o -> p (a o)")[:, 0:2048].rearrange(
        "p (dc o) -> p dc o", o=256)

    repscr = mk("repscr", [64, 512], F32, R6)
    sigs3 = mk("sigs3", [128, NT, 16], F32, R6 + 2048)
    recden = mk("recden", [128, NT, 16], F32, R6 + 3072)
    invq8 = mk("invq8", [128, NT, 16], F32, R6 + 4096)
    gring = mk("gring", [9, 2, 512], F32, R6)
    z1ring = mk("z1ring", [128, 2, 1040], BF16, R6 + 4160)
    xsel = mk("xsel", [128, 2, 1024], BF16, R6)
    xsels = mk("xsels", [128, 2, 16], BF16, R6 + 4128)
    indf = mk("indf", [128, 2, 8], F32, R6 + 4192)
    weff = mk("weff", [128, 2, 1], F32, R6 + 4256)
    frows = mk("frows", [1, 3, 512], F32, R6 + 8192)
    eacc = mk("eacc", [128, 2, D], F32, R5)
    otile = mk("otile", [128, 2, D], F32, R5)
    ssping = mk("ssping", [128, 2, 512], BF16, R6 + 16384)
    sqdummy = mk("sqdummy", [128, 1024], BF16, R6 + 16384)
    zz = mk("zz", [128, 1040], BF16, R6 + 16384)
    sqscr = mk("sqscr", [128, 2, 512], BF16, R6 + 18496)
    bias3e = mk("bias3e", [1, 1024], BF16, R6 + 20544)
    obounce = mk("obounce", [64, 2, 512], BF16, R6 + 20544)
    invrowb = mk("invrowb", [1, 512], BF16, R6 + 22592)
    minrowb = mk("minrowb", [1, 512], BF16, R6 + 23616)
    rscr = mk("rscr", [1, 2, NOWN], BF16, R4)
    rrall = mk("rrall", [16, NOWN], BF16, R4 + 8192)
    rrallb = mk("rrallb", [16, NOWN], BF16, R4 + 12288)
    rrinv = mk("rrinv", [16, NOWN], F32, R4)
    ttscr = mk("ttscr", [128, 512], BF16, R6 + 24640)
    gstt = mk("gstt", [128, 2, 12], F32, R6 + 25664)
    ln2c = mk("ln2c", [128, 6, 16], F32, R6 + 25760)
    retbuf = mk("retbuf", [128, 2, D], BF16, R6 + 4160)

    cur = [SM]

    def sbal(name, shape, dtype):
        at = cur[0]
        nb = int(np.prod(shape[1:])) * mybir.dt.size(dtype)
        cur[0] = (at + nb + 31) & ~31
        return mk(name, shape, dtype, at)

    cbf = sbal("cbf", [128, NCBF], BF16)
    cf = sbal("cf", [128, NCF], F32)
    idxs = sbal("idxs", [128, 32], I32)
    browsb = sbal("browsb", [1, 1024], BF16)
    brows2 = sbal("brows2", [1, 1024], BF16)
    onesrow = sbal("onesrow", [1, 640], BF16)
    halocolb = sbal("halocolb", [1, HALO], BF16)
    bdsel = sbal("bdsel", [16, 8, 128], BF16)
    wqwkrow2 = sbal("wqwkrow2", [1, 128], F32)
    wqwkcol = sbal("wqwkcol", [128, 1], F32)
    rmsq = sbal("rmsq", [1, 64], F32)
    rmsk = sbal("rmsk", [1, 64], F32)
    wqwk = sbal("wqwk", [1, 64], F32)
    pk = [R6 + 14336]

    def pkmk(name, shape, dtype):
        at = pk[0]
        nb = int(np.prod(shape[1:])) * mybir.dt.size(dtype)
        pk[0] = (at + nb + 31) & ~31
        assert pk[0] <= R6 + 16384, name
        return mk(name, shape, dtype, at)

    wsmcol = pkmk("wsmcol", [128, NT], F32)
    scr8 = pkmk("scr8", [128, 8], F32)
    scrsm = pkmk("scrsm", [128, 8], F32)
    mxcol = pkmk("mxcol", [128, 1], F32)
    t1col = pkmk("t1col", [128, 1], F32)
    wst = pkmk("wst", [128, 2], F32)
    dcol = pkmk("dcol", [128, 2], F32)
    sidescr = pkmk("sidescr", [128, 2, 16], BF16)
    densb = pkmk("densb", [2, 16], F32)
    denrow = pkmk("denrow", [1, 16], F32)
    sminv = pkmk("sminv", [16, 1], F32)
    rsumrep = pkmk("rsumrep", [128, 16], F32)
    c2scr = pkmk("c2scr", [128, 16], F32)
    lncol = pkmk("lncol", [128, 8], F32)
    onescf = pkmk("onescf", [128, 1], F32)
    cpool = pkmk("cpool", [128, 5], F32)
    bdsq = pkmk("bdsq", [128, 8, 16], BF16)
    rsrep1 = pkmk("rsrep1", [16, 128], F32)
    egwsb = pkmk("egwsb", [128, 8, 9], BF16)
    assert cur[0] <= TOP, f"SBUF overflow {cur[0]} > {TOP}"

    psum_all = nc.alloc_psum_tensor("psum_all", [128, 4096], F32)
    assert nc.psum_banks_remaining == 0
    PSB = [psum_all[:, 512 * i:512 * i + 512] for i in range(8)]
    psA = psum_all[:, 0:640]
    psB = psum_all[:, 1024:1664]
    psG = psum_all[:, 2048:2688]
    psC = PSB[0]
    psD = PSB[2]
    psE = PSB[4]
    psF = PSB[6]
    psG0 = PSB[1]
    SRING = [psA, psB, psG]

    sy = P.sem("s_sy", 16)
    sg = P.sem("s_sg", 16)
    ssd = P.sem("s_ssd", 16)
    syd = P.sem("s_syd", 1)
    sgd = P.sem("s_sgd", 1)
    ssdd = P.sem("s_ssdd", 1)

    def ybar():
        P.op("sync", "nop", waits=((sy, sy.n),), inc=syd.inc())
        return syd.tok()

    def gbar():
        P.op("gpsimd", "nop", waits=((sg, sg.n),), inc=sgd.inc())
        return sgd.tok()

    def sbar():
        P.op("scalar", "nop", waits=((ssd, ssd.n),), inc=ssdd.inc())
        return ssdd.tok()

    mm = P.sem("s_mm", 1)
    sc = P.sem("s_sc", 1)
    vc = P.sem("s_vc", 1)
    cc = P.sem("s_cc", 1)
    P.autobar = {"vector": vc, "scalar": sc}

    MASKL = cbf[:, CBF["maskL"]:CBF["maskL"] + 128]
    MASKR = cbf[:, CBF["maskR"]:CBF["maskR"] + 128]
    IDENTB = cbf[:, CBF["identb"]:CBF["identb"] + 128]
    BD2 = cbf[:, CBF["bd2"]:CBF["bd2"] + 2]
    ONESCB = cbf[:, CBF["onescol"]:CBF["onescol"] + 1]
    IDENTF = cf[:, CF["identf"]:CF["identf"] + 128]

    def cfc(name, o, n):
        return cf[:, CF[name] + o:CF[name] + o + n]

    def cbfc(name, o, n):
        return cbf[:, CBF[name] + o:CBF[name] + o + n]

    P.op("gpsimd", "dma_start", out=cbf[:], in_=d_cbf[:, :], inc=sg.inc())
    P.op("sync", "dma_start", out=cf[:], in_=d_cf[:, :], inc=sy.inc())
    P.op("sync", "dma_start", out=idxs[:], in_=d_idx[:, :], inc=sy.inc())
    P.op("sync", "dma_start", out=rmsq[:], in_=d_rms[0:1, 0:64], inc=sy.inc())
    P.op("sync", "dma_start", out=rmsk[:], in_=d_rms[0:1, 64:128], inc=sy.inc())
    P.op("gpsimd", "dma_start", out=halocolb[:], in_=d_halocol[:, :], inc=sg.inc())
    obq, nbq = br["bq"]
    P.op("gpsimd", "dma_start", out=browsb[0:1, 0:1024],
         in_=d_brows[0:1, obq:obq + nbq], inc=sg.inc())
    obk, nbk = br["bk"]
    P.op("gpsimd", "dma_start", out=brows2[0:1, 0:1024],
         in_=d_brows[0:1, obk:obk + nbk], inc=sg.inc())
    P.op("gpsimd", "dma_start", out=egwsb[:],
         in_=d_egwT[:, :].rearrange("(dc p) e -> p dc e", p=128), inc=sg.inc())
    P.op("gpsimd", "dma_start", out=bdsel[:],
         in_=d_bdsel[:, :].rearrange("k (oc p) -> k oc p", p=128), inc=sg.inc())
    P.op("gpsimd", "dma_start", out=bdsq[:],
         in_=d_bdsq[:, :].rearrange("p (oc j) -> p oc j", j=16), inc=sg.inc())
    xTfm = d_xT[:, :].rearrange("(dc p) t -> p dc t", p=128)
    P.op("gpsimd", "dma_start", out=wsb[:],
         in_=d_wq[:, :].rearrange("(dc p) o -> p dc o", p=128), inc=sg.inc())
    xtok = []
    for (xa, xb) in [(512, 1024), (1024, 1536), (1536, 2048), (2048, 2560),
                     (0, 512)]:
        P.op("gpsimd", "dma_start", out=xhat[:, :, xa:xb], in_=xTfm[:, :, xa:xb],
             inc=sg.inc())
        xtok.append(gbar())
    g_loads = xtok[3]
    y_loads = ybar()

    P.op("vector", "memset", onesrow[:], 1.0, inc=vc.inc())
    P.op("vector", "memset", onescf[:], 1.0, inc=vc.inc())
    P.op("vector", "memset", zz[:], 0.0, inc=vc.inc())
    P.op("vector", "memset", sidescr[:], 0.0, inc=vc.inc())
    P.op("vector", "memset",
         vhat[:].rearrange("p t (kv c) -> p t kv c", c=65)[:, :, :, 64:65],
         1.0, inc=vc.inc())
    P.op("vector", "memset", cpool[:, 0:1], RMS_EPS, inc=vc.inc())
    P.op("vector", "memset", cpool[:, 1:2], LN_EPS, inc=vc.inc())
    P.op("vector", "memset", cpool[:, 2:3], 1.0 / 64.0, inc=vc.inc())
    P.op("vector", "memset", cpool[:, 3:4], 0.125, inc=vc.inc())
    P.op("vector", "memset", cpool[:, 4:5], 1.0 / 512.0, inc=vc.inc())
    v_cpool = vc.tok()
    v_init = vc.tok()
    for i in range(16):
        P.op("gpsimd", "dma_start", out=d_fwdx[128 * i:128 * i + 128, :],
             in_=zz[:, 0:1024], waits=(v_init,) if i == 0 else (), inc=sg.inc())
    P.op("gpsimd", "dma_start", out=d_fwdx[2048:2056, :], in_=zz[0:8, 0:1024],
         inc=sg.inc())
    P.op("gpsimd", "dma_start",
         out=d_fwds[0:2048, :].rearrange("(a p) c -> p a c", p=128),
         in_=zz[:, 0:256].rearrange("p (a c) -> p a c", c=16), inc=sg.inc())
    P.op("gpsimd", "dma_start", out=d_fwds[2048:2056, :], in_=zz[0:8, 0:16],
         inc=sg.inc())
    P.op("gpsimd", "dma_start", out=d_agout[NSEL:NSEL + 1, :], in_=zz[0:1, 0:1024],
         inc=sg.inc())
    g_zf = gbar()

    P.op("vector", "tensor_mul", wqwk[:], rmsq[:], rmsk[:],
         waits=(y_loads,), inc=vc.inc())
    v_ww = vc.tok()
    P.op("gpsimd", "dma_start", out=wqwkrow2[0:1, 0:64], in_=wqwk[:],
         waits=(v_ww,), inc=sg.inc())
    P.op("gpsimd", "dma_start", out=wqwkrow2[0:1, 64:128], in_=wqwk[:],
         inc=sg.inc())
    g_ww = gbar()
    g_bdw = gbar()

    dbg_bar1 = dbg_bar2 = dbg_bar3 = dbg_bar4 = dbg_bar5 = None

    def finish_early():
        P.op("vector", "memset", otile[:, 0, :], 0.0, inc=vc.inc())
        vv = vc.tok()
        for t_ in range(NT):
            P.op("sync", "dma_start", out=d_y[128 * t_:128 * t_ + 128, :],
                 in_=otile[:, 0, :], waits=(vv,) if t_ == 0 else (),
                 inc=sy.inc())
        ybar()
        return _emit(nc, P)

    if upto <= 0:
        return finish_early()

    def proj_fm(wget, biasrow, tok_cols, out_write, sumsq_write, prewaits,
                tc_waits=None):
        last = {}
        pe_ev = {}
        it = 0
        for tci, (tc0, tn) in enumerate(tok_cols):
            pe = psE if tci % 2 == 0 else psF
            for oc in range(8):
                pp = psC if it % 2 == 0 else psD
                sq_ = sqscr[:, it % 2, 0:tn]
                tw = list(prewaits) if it < 2 else list(last[it - 2])
                if tc_waits and oc == 0 and tci in tc_waits:
                    tw = tw + [tc_waits[tci]]
                for dc in range(8):
                    P.op("tensor", "matmul", pp[:, 0:tn],
                         wget(dc, oc), xhat[:, dc, tc0:tc0 + tn],
                         start=(dc == 0), stop=False,
                         waits=tw if dc == 0 else ())
                P.op("tensor", "matmul", pp[:, 0:tn],
                     biasrow(oc), onesrow[0:1, 0:tn],
                     start=False, stop=True, inc=mm.inc())
                t_mm = mm.tok()
                P.op("scalar", "activation", out=sq_, in_=pp[:, 0:tn],
                     func=AF.Square, waits=(t_mm,), inc=sc.inc())
                s_sq = sc.tok()
                P.op("scalar", "copy", out=out_write(oc, tc0, tn),
                     in_=pp[:, 0:tn], inc=sc.inc())
                v_ev = sc.tok()
                ssw = [s_sq]
                if oc == 0 and tci >= 2:
                    ssw.append(pe_ev[tci - 2])
                P.op("tensor", "matmul", pe[0:16, 0:tn], bdsq[:, oc, :], sq_,
                     start=(oc == 0), stop=(oc == 7), waits=ssw,
                     inc=mm.inc() if oc == 7 else None)
                last[it] = (s_sq, v_ev)
                it += 1
            t_ss = mm.tok()
            P.op("scalar", "copy", out=sumsq_write(tc0, tn),
                 in_=pe[0:16, 0:tn], waits=(t_ss,), inc=sc.inc())
            pe_ev[tci] = sc.tok()
        return (sc.tok(), vc.tok())

    own_cols = [(HALO + 512 * t, 512) for t in range(4)]
    all_cols = [(512 * t, 512) for t in range(5)]

    qtail = proj_fm(
        lambda dc, oc: wsb[:, dc, 128 * oc:128 * oc + 128],
        lambda oc: browsb[0:1, 128 * oc:128 * oc + 128],
        own_cols,
        lambda oc, tc0, tn: qhat[:, oc, tc0 - HALO:tc0 - HALO + tn],
        lambda tc0, tn: sumsqQ[0:16, tc0 - HALO:tc0 - HALO + tn],
        [v_init, xtok[0]],
        tc_waits={1: xtok[1], 2: xtok[2], 3: xtok[3]})
    if upto <= 1.1:
        return finish_early()
    P.op("tensor", "transpose", psG0[0:128, 0:1], wqwkrow2[:, :],
         IDENTF[0:1, 0:1], waits=(g_ww,), inc=mm.inc())
    t_ww = mm.tok()
    P.op("scalar", "copy", out=wqwkcol[:], in_=psG0[0:128, 0:1], waits=(t_ww,),
         inc=sc.inc())
    s_wwc = sc.tok()
    q_mm_done = mm.tok()
    P.op("gpsimd", "dma_start", out=wsb[:],
         in_=d_wk[:, :].rearrange("(dc p) o -> p dc o", p=128),
         waits=(q_mm_done,), inc=sg.inc())
    g_wk = gbar()
    ktail = proj_fm(
        lambda dc, oc: wsb[:, dc, 128 * oc:128 * oc + 128],
        lambda oc: brows2[0:1, 128 * oc:128 * oc + 128],
        all_cols,
        lambda oc, tc0, tn: khat[:, oc, tc0:tc0 + tn],
        lambda tc0, tn: sumsqK[0:16, tc0:tc0 + tn],
        [g_wk, qtail[0], qtail[1]])
    if upto <= 1.2:
        return finish_early()
    k_mm_done = mm.tok()
    P.op("gpsimd", "dma_start", out=wvv,
         in_=d_wv[:, :].rearrange("(dc p) o -> p dc o", p=128),
         waits=(k_mm_done,), inc=sg.inc())
    obv, nbv = br["bv"]
    P.op("gpsimd", "dma_start", out=browsb[0:1, 0:256],
         in_=d_brows[0:1, obv:obv + nbv], inc=sg.inc())
    g_wv = gbar()

    if upto <= 1.3:
        return finish_early()
    q_sq_all = qtail[0]
    P.op("scalar", "activation", out=sumsqQ[:, :], in_=sumsqQ[:, :], func=AF.Sqrt,
         scale=cpool[0:16, 2:3], bias=cpool[0:16, 0:1],
         waits=(v_cpool, sc.tok()), inc=sc.inc())
    P.op("scalar", "activation", out=sumsqK[:, :], in_=sumsqK[:, :], func=AF.Sqrt,
         scale=cpool[0:16, 2:3], bias=cpool[0:16, 0:1], inc=sc.inc())
    s_sqrt = sc.tok()
    P.op("vector", "reciprocal", out=sumsqQ[:, :], in_=sumsqQ[:, :],
         waits=(s_sqrt,), inc=vc.inc())
    P.op("vector", "reciprocal", out=sumsqK[:, :], in_=sumsqK[:, :], inc=vc.inc())
    P.op("vector", "tensor_copy", out=invkb[:, :], in_=sumsqK[:, :],
         waits=(vc.tok(),), inc=vc.inc())
    v_inv = vc.tok()
    invq8_ev = {}
    for t in range(NT):
        pe = psE if t % 2 == 0 else psF
        tw = [v_inv] if t < 2 else [invq8_ev[t - 2]]
        P.op("tensor", "transpose", pe[0:128, 0:16],
             sumsqQ[0:16, 128 * t:128 * t + 128], IDENTF[0:16, 0:16],
             waits=tw, inc=mm.inc())
        t_tr = mm.tok()
        P.op("scalar", "activation", out=invq8[:, t, :], in_=pe[0:128, 0:16],
             func=AF.Copy, scale=cpool[:, 3:4], waits=(t_tr, v_cpool),
             inc=sc.inc())
        invq8_ev[t] = sc.tok()
    s_invq8 = sc.tok()

    if upto <= 1.4:
        return finish_early()
    klast = {}
    it = 0
    for oc in range(8):
        for tc in range(5):
            pp = psC if it % 2 == 0 else psD
            tw = [v_inv, g_bdw] if it < 2 else [klast[it - 2]]
            P.op("tensor", "matmul", pp[:, :], bdsel[:, oc, :],
                 invkb[:, 512 * tc:512 * tc + 512],
                 start=True, stop=True, waits=tw, inc=mm.inc())
            t_rep = mm.tok()
            P.op("vector", "scalar_tensor_tensor",
                 out=khat[:, oc, 512 * tc:512 * tc + 512],
                 in0=khat[:, oc, 512 * tc:512 * tc + 512], scalar=wqwkcol[:],
                 in1=pp[:, :], op0=ALU.mult, op1=ALU.mult,
                 waits=(t_rep, s_wwc), inc=vc.inc())
            klast[it] = vc.tok()
            it += 1
    v_khat = vc.tok()

    if debug:
        P.op("sync", "dma_start", out=dbg["dbg_qhat"][:, :],
             in_=qhat[:].rearrange("p a t -> p (a t)"), waits=(v_done,), inc=sy.inc())
        P.op("sync", "dma_start", out=dbg["dbg_khat"][:, :],
             in_=khat[:].rearrange("p a t -> p (a t)"), waits=(v_khat,), inc=sy.inc())
        P.op("sync", "dma_start", out=dbg["dbg_vhat"][:, :],
             in_=vhat[:].rearrange("p a t -> p (a t)"), waits=(v_done,), inc=sy.inc())
    dbg_bar1 = ybar() if debug else None

    if upto <= 2:
        return finish_early()

    slot_last = [None, None, None]
    rd_tok = {}
    r = 0
    it = 0
    for qt in range(NT):
        k0 = 128 * qt
        lo0 = max(0, 512 - 128 * qt)
        for hp in range(8):
            sS = [SRING[r], SRING[(r + 1) % 3]]
            t_mm = [None, None]
            tws = []
            for e in (0, 1):
                tw = list([v_khat, s_invq8] if it == 0 else [])
                if slot_last[(r + e) % 3] is not None:
                    tw.append(slot_last[(r + e) % 3])
                tws.append(tw)
            if lo0 < 512:
                for e in (0, 1):
                    hb = 64 * e
                    P.op("tensor", "matmul", sS[e][:, lo0:512],
                         qhat[hb:hb + 64, hp, 128 * qt:128 * qt + 128],
                         khat[hb:hb + 64, hp, k0 + lo0:k0 + 512],
                         start=True, stop=(qt < 4), waits=tws[e])
                    tws[e] = []
            for e in (0, 1):
                hb = 64 * e
                P.op("tensor", "matmul", sS[e][:, 512:640],
                     qhat[hb:hb + 64, hp, 128 * qt:128 * qt + 128],
                     khat[hb:hb + 64, hp, k0 + 512:k0 + 640],
                     start=True, stop=False, waits=tws[e])
            for e in (0, 1):
                sA = sS[e]
                if qt >= 4:
                    P.op("tensor", "matmul", sA[:, 0:128], IDENTB, MASKL,
                         start=False, stop=True)
                P.op("tensor", "matmul", sA[:, 512:640], IDENTB, MASKR,
                     start=False, stop=True, inc=mm.inc())
                t_mm[e] = mm.tok()
            for e in (0, 1):
                h = 2 * hp + e
                P.op("scalar", "activation", out=sigP[:, e, 0:640 - lo0],
                     in_=sS[e][:, lo0:640],
                     func=AF.Sigmoid, scale=invq8[:, qt, h:h + 1],
                     accum_out=sigs3[:, qt, h:h + 1],
                     waits=(t_mm[e], s_invq8) if it == 0 else (t_mm[e],),
                     inc=sc.inc())
                slot_last[(r + e) % 3] = sc.tok()
            if hp == 7:
                last_sig = sc.tok()
                P.op("vector", "tensor_scalar", out=recden[:, qt, :],
                     in0=sigs3[:, qt, :], scalar1=cfc("corr2", qt, 1),
                     scalar2=1.0 / T, op0=ALU.add, op1=ALU.mult,
                     waits=(last_sig,), inc=vc.inc())
                P.op("vector", "reciprocal", out=recden[:, qt, :],
                     in_=recden[:, qt, :], inc=vc.inc())
                rd_tok[qt] = vc.tok()
            if hp == 3 and qt >= 1:
                P.op("tensor", "matmul", PSB[7][0:1, 0:16], onescf[:],
                     recden[:, qt - 1, :], start=(qt == 1), stop=False,
                     waits=(rd_tok[qt - 1],))
            r = (r + 2) % 3
            it += 1
    P.op("tensor", "matmul", PSB[7][0:1, 0:16], onescf[:],
         recden[:, NT - 1, :], start=False, stop=True,
         waits=(rd_tok[NT - 1],), inc=mm.inc())
    t_part = mm.tok()
    s_pass1 = sc.tok()
    if debug:
        P.op("sync", "dma_start", out=dbg["dbg_sigs"][:, :],
             in_=sigs3[:].rearrange("p a t -> p (a t)"), waits=(s_pass1,), inc=sy.inc())

    vlast = {}
    for t in range(NKT):
        pp = psC if t % 2 == 0 else psD
        tw = [g_wv, s_pass1] if t < 2 else [vlast[t - 2]]
        for dc in range(8):
            P.op("tensor", "matmul", pp[:, 0:256],
                 xhat[:, dc, 128 * t:128 * t + 128], wvv[:, dc, :],
                 start=(dc == 0), stop=False, waits=tw if dc == 0 else ())
        P.op("tensor", "matmul", pp[:, 0:256], onesrow[0:1, 0:128],
             browsb[0:1, 0:256], start=False, stop=True, inc=mm.inc())
        t_mm = mm.tok()
        P.op("vector", "tensor_copy",
             out=vhat[:, t, :].rearrange("p (kv c) -> p kv c", c=65)[:, :, 0:64],
             in_=pp[:, 0:256].rearrange("p (kv c) -> p kv c", c=64),
             waits=(t_mm,), inc=vc.inc())
        vlast[t] = vc.tok()
    v_done = vc.tok()
    v_mm_done = mm.tok()
    P.op("gpsimd", "dma_start", out=wsb[:],
         in_=d_wo[:, :].rearrange("(dc p) o -> p dc o", p=128),
         waits=(v_mm_done,), inc=sg.inc())
    obo, nbo = br["bo"]
    P.op("gpsimd", "dma_start", out=brows2[0:1, 0:1024],
         in_=d_brows[0:1, obo:obo + nbo], inc=sg.inc())
    P.op("gpsimd", "dma_start", out=sw1sb[:],
         in_=d_sw1T[:, :].rearrange("(dc p) h -> p dc h", p=128),
         waits=(v_mm_done,), inc=sg.inc())
    g_wo = gbar()

    P.op("scalar", "copy", out=denrow[:], in_=PSB[7][0:1, 0:16], waits=(t_part,),
         inc=sc.inc())
    s_den = sc.tok()
    P.op("sync", "dma_start", out=d_denin[:, :], in_=denrow[:], waits=(s_den,),
         inc=sy.inc())
    g_den = ybar()
    P.op("gpsimd", "collective_compute", "AllGather", ALU.bypass,
         replica_groups=[[0, 1], [2, 3], [4, 5], [6, 7]], ins=[d_denin.opt()],
         outs=[d_denout.opt()], waits=(g_den,), inc=cc.inc())
    c_den = cc.tok()
    P.op("sync", "dma_start", out=densb[:], in_=d_denout[:, :], waits=(c_den,),
         inc=sy.inc())
    g_dsb = ybar()
    P.op("tensor", "matmul", psF[0:16, 0:1], densb[0:2, 0:16],
         onescf[0:2, 0:1], start=True, stop=True, waits=(g_dsb, s_den),
         inc=mm.inc())
    t_sminv = mm.tok()
    P.op("vector", "reciprocal", out=sminv[:], in_=psF[0:16, 0:1],
         waits=(t_sminv,), inc=vc.inc())
    v_r1 = vc.tok()
    P.op("vector", "tensor_copy", out=rsrep1[:, :],
         in_=sminv[:].broadcast_to([16, 128]), waits=(v_r1,), inc=vc.inc())
    v_rs1 = vc.tok()
    P.op("tensor", "transpose", psF[0:128, 0:16], rsrep1[:, :],
         IDENTF[0:16, 0:16], waits=(v_rs1,), inc=mm.inc())
    t_rs = mm.tok()
    P.op("scalar", "copy", out=rsumrep[:], in_=psF[0:128, 0:16], waits=(t_rs,),
         inc=sc.inc())
    s_rs = sc.tok()
    P.op("vector", "tensor_tensor",
         out=recden[:].rearrange("p a t -> p (a t)"),
         in0=recden[:].rearrange("p a t -> p (a t)"),
         in1=invq8[:].rearrange("p a t -> p (a t)"), op=ALU.mult,
         waits=(s_invq8,), inc=vc.inc())
    P.op("vector", "tensor_tensor", out=recden[:, :, :],
         in0=recden[:, :, :],
         in1=rsumrep[:].rearrange("p (a t) -> p a t", a=1).broadcast_to(
             [128, NT, 16]),
         op=ALU.mult, waits=(s_rs,), inc=vc.inc())
    v_c2all = vc.tok()
    c2ev = {}
    for qt in range(NT):
        pe = psE if qt % 2 == 0 else psF
        P.op("tensor", "transpose", pe[0:16, 0:128], recden[:, qt, :], IDENTF,
             waits=((v_c2all,) if qt < 2 else (c2ev[qt - 2],)), inc=mm.inc())
        t_c2 = mm.tok()
        P.op("scalar", "copy", out=c2fm[0:16, 128 * qt:128 * qt + 128],
             in_=pe[0:16, 0:128], waits=(t_c2,), inc=sc.inc())
        c2ev[qt] = sc.tok()
    s_c2 = sc.tok()
    if debug:
        P.op("sync", "dma_start", out=dbg["dbg_recden"][:, :],
             in_=recden[:].rearrange("p a t -> p (a t)"), waits=(v_rec,), inc=sy.inc())
        P.op("sync", "dma_start", out=dbg["dbg_c2"][:, :], in_=c2fm[:, :],
             waits=(s_c2,), inc=sy.inc())

    q2last = {}
    it = 0
    for oc in range(8):
        for tc in range(4):
            pp = psC if it % 2 == 0 else psD
            tw = ([s_c2, vlast[NKT - 2], vlast[NKT - 1]]
                  if it < 2 else [q2last[it - 2]])
            if debug and it == 0:
                tw.append(dbg_bar1)
            P.op("tensor", "matmul", pp[:, :], bdsel[:, oc, :],
                 c2fm[:, 512 * tc:512 * tc + 512],
                 start=True, stop=True, waits=tw, inc=mm.inc())
            t_rep = mm.tok()
            P.op("vector", "tensor_tensor",
                 out=qhat[:, oc, 512 * tc:512 * tc + 512],
                 in0=qhat[:, oc, 512 * tc:512 * tc + 512], in1=pp[:, :],
                 op=ALU.mult, waits=(t_rep,), inc=vc.inc())
            q2last[it] = vc.tok()
            it += 1
    v_q2 = vc.tok()

    if upto <= 3:
        return finish_early()

    spans = {}
    kt_of_qc = {qc: [] for qc in range(4)}
    for kt in range(NKT):
        qlo = max(0, 128 * kt - 512)
        qhi = min(NOWN, 128 * kt + 128)
        spans[kt] = (qlo, qhi)
        for qc in range(4):
            if qlo < 512 * qc + 512 and qhi > 512 * qc:
                kt_of_qc[qc].append(kt)
    first_kt = {qc: ks[0] for qc, ks in kt_of_qc.items()}
    last_kt = {qc: ks[-1] for qc, ks in kt_of_qc.items()}

    ZROW = cbf[0:1, CBF["maskL"]:CBF["maskL"] + 128]
    SS = {0: psum_all[:, 0:640], 1: psum_all[:, 3072:3712]}
    OBK = {(0, 0): PSB[2], (0, 1): PSB[3], (1, 0): PSB[4], (1, 1): PSB[5]}
    rs_dma = {0: None, 1: None}
    ob_dma = {0: None, 1: None}
    explast = {}
    ev_tok = {}
    for hp in range(8):
        kv = hp // 2
        for kt in range(NKT):
            qlo, qhi = spans[kt]
            w = qhi - qlo
            n0 = min(w, 512)
            for eo in (0, 1):
                h = 2 * hp + eo
                hb = 64 * eo
                if hp == 0 and kt == 0:
                    tw = [v_q2, v_done]
                elif kt == 0:
                    tw = [explast[h - 2]]
                else:
                    tw = [explast[h]]
                P.op("tensor", "matmul", SS[eo][:, 0:n0],
                     khat[hb:hb + 64, hp, 128 * kt:128 * kt + 128],
                     qhat[hb:hb + 64, hp, qlo:qlo + n0],
                     start=True, stop=False, waits=tw)
            if w > 512:
                for eo in (0, 1):
                    hb = 64 * eo
                    P.op("tensor", "matmul", SS[eo][:, 512:w],
                         khat[hb:hb + 64, hp, 128 * kt:128 * kt + 128],
                         qhat[hb:hb + 64, hp, qlo + 512:qlo + w],
                         start=True, stop=False)
            t_mm = {}
            for eo in (0, 1):
                sA = SS[eo]
                b0open = True
                mr_in_b0 = (kt <= 15) and (w <= 512)
                ml = (kt >= 4)
                if ml and not mr_in_b0:
                    P.op("tensor", "matmul", sA[:, 0:128], IDENTB, MASKL,
                         start=False, stop=True)
                    b0open = False
                elif ml and mr_in_b0:
                    P.op("tensor", "matmul", sA[:, 0:128], IDENTB, MASKL,
                         start=False, stop=False)
                if mr_in_b0:
                    P.op("tensor", "matmul", sA[:, w - 128:w], IDENTB, MASKR,
                         start=False, stop=True)
                    b0open = False
                if b0open:
                    P.op("tensor", "matmul", sA[:, 0:n0],
                         halocolb[0:1, 0:128], onesrow[0:1, 0:n0],
                         start=False, stop=True)
                if w > 512:
                    P.op("tensor", "matmul", sA[:, 512:w], IDENTB, MASKR,
                         start=False, stop=True)
                t_mm[eo] = P.tag_last("tensor", mm)
            for qc in range(4):
                if kt != first_kt[qc]:
                    continue
                for eo in (0, 1):
                    h = 2 * hp + eo
                    prev = (h, qc - 2) if qc >= 2 else (h - 2, qc + 2)
                    zw = [] if prev in ev_tok else [v_q2, v_done]
                    if prev in ev_tok:
                        zw.append(ev_tok[prev])
                    P.op("tensor", "matmul", OBK[(eo, qc % 2)][0:65, 0:512],
                         ZROW[0:1, 0:65], onesrow[0:1, 0:512],
                         start=True, stop=False, waits=zw)
            for eo in (0, 1):
                h = 2 * hp + eo
                if kt < 4:
                    P.op("scalar", "activation", out=sigP[:, eo, 0:w],
                         in_=SS[eo][:, 0:w], func=AF.Exp,
                         bias=cfc("haloc", kt, 1),
                         waits=(t_mm[eo],), inc=sc.inc())
                else:
                    P.op("scalar", "activation", out=sigP[:, eo, 0:w],
                         in_=SS[eo][:, 0:w], func=AF.Exp,
                         waits=(t_mm[eo],), inc=sc.inc())
                explast[h] = sc.tok()
            for qc in range(4):
                if kt not in kt_of_qc[qc]:
                    continue
                lo = max(qlo, 512 * qc)
                hi = min(qhi, 512 * qc + 512)
                for eo in (0, 1):
                    h = 2 * hp + eo
                    P.op("tensor", "matmul",
                         OBK[(eo, qc % 2)][0:65, lo - 512 * qc:hi - 512 * qc],
                         vhat[:, kt, 65 * kv:65 * kv + 65],
                         sigP[:, eo, lo - qlo:hi - qlo],
                         start=False, stop=(kt == last_kt[qc]),
                         waits=[explast[h]])
                if kt != last_kt[qc]:
                    continue
                t_pv = P.tag_last("tensor", mm)
                for eo in (0, 1):
                    h = 2 * hp + eo
                    ob = OBK[(eo, qc % 2)]
                    rw = [t_pv] if eo == 0 else []
                    if qc == 0 and rs_dma[eo] is not None:
                        rw.append(rs_dma[eo])
                    P.op("scalar", "copy",
                         out=rscr[0:1, eo, 512 * qc:512 * qc + 512],
                         in_=ob[64:65, 0:512], waits=tuple(rw), inc=sc.inc())
                    if eo == 0:
                        P.op("scalar", "copy",
                             out=ahat[0:64, hp, 512 * qc:512 * qc + 512],
                             in_=ob[0:64, 0:512], inc=sc.inc())
                    else:
                        obw = [ob_dma[qc % 2]] if ob_dma[qc % 2] is not None else []
                        P.op("scalar", "copy", out=obounce[0:64, qc % 2, :],
                             in_=ob[0:64, 0:512], waits=obw, inc=sc.inc())
                        s_ob = sc.tok()
                        P.op("sync", "dma_start",
                             out=ahat[64:128, hp, 512 * qc:512 * qc + 512],
                             in_=obounce[0:64, qc % 2, :], waits=(s_ob,),
                             inc=sy.inc())
                        ob_dma[qc % 2] = ybar()
                    ev_tok[(h, qc)] = sc.tok()
                    if qc == 3:
                        P.op("sync", "dma_start", out=rrall[h:h + 1, :],
                             in_=rscr[0:1, eo, :], waits=(ev_tok[(h, qc)],),
                             inc=sy.inc())
                        rs_dma[eo] = ybar()
    s_ahat = sc.tok()
    g_ahat = ybar()
    P.op("vector", "reciprocal", out=rrinv[:, :], in_=rrall[:, :],
         waits=(s_ahat, rs_dma[0], rs_dma[1]), inc=vc.inc())
    P.op("vector", "tensor_copy", out=rrallb[:, :], in_=rrinv[:, :], inc=vc.inc())
    v_rr = vc.tok()
    norm_tok = {}
    it = 0
    for oc in range(8):
        for tc in range(4):
            pp = psC if it % 2 == 0 else psD
            tw = [v_rr, g_ahat]
            if it >= 2:
                tw.append(norm_tok[it - 2])
            P.op("tensor", "matmul", pp[:, :], bdsel[:, oc, :],
                 rrallb[:, 512 * tc:512 * tc + 512],
                 start=True, stop=True, waits=tw, inc=mm.inc())
            t_rep = mm.tok()
            P.op("vector", "tensor_tensor",
                 out=ahat[:, oc, 512 * tc:512 * tc + 512],
                 in0=ahat[:, oc, 512 * tc:512 * tc + 512], in1=pp[:, :],
                 op=ALU.mult, waits=(t_rep,), inc=vc.inc())
            norm_tok[it] = vc.tok()
            it += 1
    v_ahat = vc.tok()
    if debug:
        P.op("sync", "dma_start", out=dbg["dbg_ahat"][:, :],
             in_=ahat[:].rearrange("p a t -> p (a t)"),
             waits=(v_ahat, g_ahat), inc=sy.inc())
    dbg_bar2 = ybar() if debug else None

    p2_mm_done = mm.tok()
    P.op("gpsimd", "dma_start", out=sw2sb[:],
         in_=d_sw2T[:, :].rearrange("(dc p) h -> p dc h", p=128),
         waits=(p2_mm_done,), inc=sg.inc())
    g_sw12 = gbar()

    if upto <= 4:
        return finish_early()

    P.op("sync", "dma_start", out=xring[:, 0, :], in_=d_xres[0:128, :],
         waits=(p2_mm_done,), inc=sy.inc())
    xres_tok = {0: ybar()}

    v_ln = {}
    zh_ev = []

    def emit_tr(t_):
        for g_ in range(2):
            tw_ = [v_ln[t_]] if g_ == 0 else []
            if zh_ev:
                tw_.append(zh_ev[-1])
            for j_ in range(4):
                dc_ = 4 * g_ + j_
                P.op("tensor", "transpose", psG0[:, 128 * j_:128 * j_ + 128],
                     strm[:, t_, 128 * dc_:128 * dc_ + 128], IDENTF,
                     waits=tw_ if j_ == 0 else (),
                     inc=mm.inc() if j_ == 3 else None)
            t_tr_ = mm.tok()
            P.op("scalar", "copy",
                 out=zhatfm[:, 4 * g_:4 * g_ + 4, 128 * t_:128 * t_ + 128],
                 in_=psG0[:, :].rearrange("p (a c) -> p a c", c=128),
                 waits=(t_tr_,), inc=sc.inc())
            zh_ev.append(sc.tok())

    ln_done = {}
    scat_done = {}
    for t in range(NT):
        if t + 1 < NT:
            P.op("sync", "dma_start", out=xring[:, (t + 1) % 2, :],
                 in_=d_xres[128 * (t + 1):128 * (t + 1) + 128, :],
                 waits=(ln_done[t - 1],) if t >= 1 else (), inc=sy.inc())
            xres_tok[t + 1] = ybar()
        banks = (psC, psD) if t % 2 == 0 else (psE, psF)
        t_ops = []
        for oci, pp in enumerate(banks):
            tw = [g_wo, g_ahat, v_ahat] if t < 2 else [ln_done[t - 2]]
            for dc in range(8):
                P.op("tensor", "matmul", pp[:, :],
                     ahat[:, dc, 128 * t:128 * t + 128],
                     wsb[:, dc, 512 * oci:512 * oci + 512],
                     start=(dc == 0), stop=False,
                     waits=tw if dc == 0 else ())
            P.op("tensor", "matmul", pp[:, :], onesrow[0:1, 0:128],
                 brows2[0:1, 512 * oci:512 * oci + 512],
                 start=False, stop=True, inc=mm.inc())
            t_ops.append(mm.tok())
        if t >= 1:
            emit_tr(t - 1)
        P.op("vector", "tensor_add", strm[:, t, 0:512], banks[0][:, :],
             xring[:, t % 2, 0:512], waits=(t_ops[0], xres_tok[t]), inc=vc.inc())
        P.op("vector", "tensor_add", strm[:, t, 512:1024], banks[1][:, :],
             xring[:, t % 2, 512:1024], waits=(t_ops[1],), inc=vc.inc())
        P.op("vector", "reduce_sum", out=lncol[:, 0:1], in_=strm[:, t, :],
             axis=AXX, inc=vc.inc())
        P.op("vector", "tensor_scalar", out=lncol[:, 1:2], in0=lncol[:, 0:1],
             scalar1=1.0 / D, scalar2=None, op0=ALU.mult, inc=vc.inc())
        v_m = vc.tok()
        P.op("scalar", "activation", out=sqdummy[:, :], in_=strm[:, t, :],
             func=AF.Square, accum_out=lncol[:, 2:3], waits=(v_m,), inc=sc.inc())
        s_sq = sc.tok()
        P.op("vector", "tensor_scalar", out=lncol[:, 3:4], in0=lncol[:, 2:3],
             scalar1=1.0 / D, scalar2=None, op0=ALU.mult, waits=(s_sq,),
             inc=vc.inc())
        P.op("vector", "tensor_mul", lncol[:, 4:5], lncol[:, 1:2], lncol[:, 1:2],
             inc=vc.inc())
        P.op("vector", "tensor_sub", lncol[:, 3:4], lncol[:, 3:4], lncol[:, 4:5],
             inc=vc.inc())
        v_var = vc.tok()
        P.op("scalar", "activation", out=lncol[:, 5:6], in_=lncol[:, 3:4],
             func=AF.Sqrt, bias=cpool[:, 1:2], waits=(v_var,), inc=sc.inc())
        s_sd = sc.tok()
        P.op("vector", "reciprocal", out=lncol[:, 6:7], in_=lncol[:, 5:6],
             waits=(s_sd,), inc=vc.inc())
        v_iv = vc.tok()
        P.op("vector", "tensor_scalar", out=strm[:, t, :], in0=strm[:, t, :],
             scalar1=lncol[:, 1:2], scalar2=lncol[:, 6:7],
             op0=ALU.subtract, op1=ALU.mult, waits=(v_iv,), inc=vc.inc())
        v_ln[t] = vc.tok()
        zw = list(scat_done[t - 2]) if t >= 2 else []
        P.op("vector", "tensor_copy", out=z1ring[:, t % 2, 0:1024],
             in_=strm[:, t, :], waits=zw, inc=vc.inc())
        ln_done[t] = vc.tok()
        P.op("gpsimd", "indirect_dma_start",
             out=d_fwdx[:, :],
             out_offset=IndirectOffsetOnAxis(idxs[:, t:t + 1], 0),
             in_=z1ring[:, t % 2, 0:1024], in_offset=None,
             waits=(ln_done[t],), inc=sg.inc())
        scat_done[t] = (gbar(),)
    emit_tr(NT - 1)
    s_zh = sc.tok()
    g_scat = gbar()
    if debug:
        P.op("sync", "dma_start", out=dbg["dbg_strm"][:, :],
             in_=strm[:].rearrange("p a t -> p (a t)"), waits=(ln_done[NT - 1],),
             inc=sy.inc())
        P.op("sync", "dma_start", out=dbg["dbg_zhat"][:, :],
             in_=zhatfm[:].rearrange("p a t -> p (a t)"),
             waits=(s_zh,), inc=sy.inc())
    dbg_bar3 = ybar() if debug else None

    g_evac = {}
    g_trd = {}
    side_tok = {}
    s_gt_tok = {}
    t_gate = {}

    def gate_mm(tc):
        pg = PSB[1] if tc % 2 == 0 else PSB[3]
        tw = [s_zh, ln_done[NT - 1]]
        if tc >= 2:
            tw.append(g_evac[tc - 2])
        for dc in range(8):
            P.op("tensor", "matmul", pg[0:9, :], egwsb[:, dc, :],
                 zhatfm[:, dc, 512 * tc:512 * tc + 512],
                 start=(dc == 0), stop=(dc == 7), waits=tw if dc == 0 else (),
                 inc=mm.inc() if dc == 7 else None)
        t_gate[tc] = mm.tok()

    def gchain(tc):
        pg = PSB[1] if tc % 2 == 0 else PSB[3]
        t_g = t_gate[tc]
        grw = [t_g] if tc < 2 else [t_g, g_trd[tc - 2]]
        P.op("scalar", "activation", out=gring[0:9, tc % 2, :], in_=pg[0:9, :],
             func=AF.Sigmoid, bias=cfc("egbc", 0, 1)[0:9, :], waits=grw,
             inc=sc.inc())
        g_evac[tc] = sc.tok()
        for ti in range(4):
            t = 4 * tc + ti
            pe = PSB[5] if t % 2 == 0 else PSB[7]
            trw = [g_evac[tc]]
            if t >= 2:
                trw.append(s_gt_tok[t - 2])
            P.op("tensor", "transpose", pe[0:128, 0:9],
                 gring[0:9, tc % 2, 128 * ti:128 * ti + 128], IDENTF[0:9, 0:9],
                 waits=trw, inc=mm.inc())
            t_tr = mm.tok()
            P.op("scalar", "copy", out=gstt[:, t % 2, 0:9], in_=pe[0:128, 0:9],
                 waits=(t_tr,), inc=sc.inc())
            s_gt = sc.tok()
            s_gt_tok[t] = s_gt
            P.op("vector", "tensor_add", scr8[:], gstt[:, t % 2, 1:9],
                 cfc("expbr", 0, 8), waits=(s_gt,), inc=vc.inc())
            P.op("vector", "reduce_max", out=mxcol[:], in_=scr8[:], axis=AXX,
                 inc=vc.inc())
            vb = vc.tok()
            P.op("vector", "tensor_single_scalar", out=scr8[:], in_=scr8[:],
                 scalar=mxcol[:], op=ALU.is_equal, waits=(vb,), inc=vc.inc())
            P.op("vector", "scalar_tensor_tensor", out=scrsm[:], in0=scr8[:],
                 scalar=1.0, in1=gstt[:, t % 2, 1:9], op0=ALU.mult, op1=ALU.mult,
                 accum_out=t1col[:], inc=vc.inc())
            vb2 = vc.tok()
            P.op("vector", "tensor_scalar", out=dcol[:, 0:1],
                 in0=gstt[:, t % 2, 0:1], scalar1=t1col[:], scalar2=None,
                 op0=ALU.subtract, waits=(vb2,), inc=vc.inc())
            P.op("vector", "tensor_scalar", out=dcol[:, 1:2], in0=dcol[:, 0:1],
                 scalar1=-1.0, scalar2=None, op0=ALU.mult, inc=vc.inc())
            v_d = vc.tok()
            P.op("scalar", "activation", out=wst[:], in_=dcol[:], func=AF.Sigmoid,
                 waits=(v_d,), inc=sc.inc())
            s_w = sc.tok()
            P.op("vector", "tensor_scalar", out=wsmcol[:, t:t + 1],
                 in0=wst[:, 0:1], scalar1=1.0, scalar2=None, op0=ALU.subtract,
                 waits=(s_w,), inc=vc.inc())
            P.op("vector", "tensor_mul", wsmcol[:, t:t + 1], wsmcol[:, t:t + 1],
                 cfc("selmask", t, 1), inc=vc.inc())
            P.op("vector", "tensor_scalar", out=wsmcol[:, t:t + 1],
                 in0=wsmcol[:, t:t + 1], scalar1=1.0, scalar2=None, op0=ALU.add,
                 inc=vc.inc())
            sw_ = [side_tok[t - 2]] if t >= 2 else []
            P.op("vector", "tensor_copy", out=sidescr[:, t % 2, 0:8],
                 in_=scr8[:], waits=sw_, inc=vc.inc())
            P.op("vector", "tensor_copy", out=sidescr[:, t % 2, 8:9],
                 in_=wst[:, 1:2], inc=vc.inc())
            v_side = vc.tok()
            P.op("gpsimd", "indirect_dma_start",
                 out=d_fwds[:, :],
                 out_offset=IndirectOffsetOnAxis(idxs[:, t:t + 1], 0),
                 in_=sidescr[:, t % 2, :], in_offset=None,
                 waits=(v_side,), inc=sg.inc())
            side_tok[t] = gbar()
        g_trd[tc] = mm.tok()
    ffn_z = {}

    def x12block(tc):
        for hc in range(4):
            it = 4 * tc + hc
            p1 = psC if it % 2 == 0 else psD
            p2_ = psE if it % 2 == 0 else psF
            tw = [g_sw12, g_wo, s_zh, ln_done[NT - 1]]
            if debug and it == 0:
                tw.append(dbg_bar2)
                tw.append(dbg_bar3)
            if it >= 2:
                tw.append(ffn_z[it - 2])
            for dc in range(8):
                P.op("tensor", "matmul", p1[:, :],
                     sw1sb[:, dc, 128 * hc:128 * hc + 128],
                     zhatfm[:, dc, 512 * tc:512 * tc + 512],
                     start=(dc == 0), stop=(dc == 7), waits=tw if dc == 0 else ())
            for dc in range(8):
                P.op("tensor", "matmul", p2_[:, :],
                     sw2sb[:, dc, 128 * hc:128 * hc + 128],
                     zhatfm[:, dc, 512 * tc:512 * tc + 512],
                     start=(dc == 0), stop=(dc == 7),
                     inc=mm.inc() if dc == 7 else None)
            t_x = mm.tok()
            P.op("scalar", "activation", out=ssping[:, it % 2, :], in_=p1[:, :],
                 func=AF.Silu, bias=cfc("sb1c", hc, 1), waits=(t_x,),
                 inc=sc.inc())
            s_sil = sc.tok()
            P.op("vector", "scalar_tensor_tensor",
                 out=zeb[:, hc, 512 * tc:512 * tc + 512], in0=p2_[:, :],
                 scalar=cfc("sb2c", hc, 1), in1=ssping[:, it % 2, :],
                 op0=ALU.add, op1=ALU.mult, waits=(s_sil,), inc=vc.inc())
            ffn_z[it] = vc.tok()

    gate_mm(0)
    gate_mm(1)
    gchain(0)
    x12block(0)
    gate_mm(2)
    gchain(1)
    x12block(1)
    gate_mm(3)
    gchain(2)
    x12block(2)
    gchain(3)
    x12block(3)
    v_z = vc.tok()

    g_side = gbar()
    v_wsm = vc.tok()
    if debug:
        P.op("sync", "dma_start", out=dbg["dbg_wsm"][:, :], in_=wsmcol[:, :],
             waits=(v_wsm,), inc=sy.inc())

    P.op("gpsimd", "collective_compute", "ReduceScatter", ALU.add,
         replica_groups=[list(range(8))], ins=[d_fwds[0:NSEL, :].opt()],
         outs=[d_rss.opt()], waits=(g_side,), inc=cc.inc())
    P.op("gpsimd", "collective_compute", "ReduceScatter", ALU.add,
         replica_groups=[list(range(8))], ins=[d_fwdx[0:NSEL, :].opt()],
         outs=[d_rsx.opt()], inc=cc.inc())
    c_rs = cc.tok()

    o_mm_done = mm.tok()

    if upto <= 5:
        return finish_early()

    P.op("gpsimd", "dma_start", out=sw3sb[:],
         in_=d_sw3T[:, :].rearrange("(hc p) o -> p hc o", p=128),
         waits=(mm.tok(),), inc=sg.inc())
    ob3, nb3 = br["sb3"]
    P.op("gpsimd", "dma_start", out=brows2[0:1, 0:1024],
         in_=d_brows[0:1, ob3:ob3 + nb3], inc=sg.inc())
    g_sw3 = gbar()
    for half in range(2):
        pe = psE if half == 0 else psF
        for hc in range(4):
            P.op("tensor", "matmul", pe[0:1, :],
                 cbfc("slnb", hc, 1), sw3sb[:, hc, 512 * half:512 * half + 512],
                 start=(hc == 0), stop=False,
                 waits=(g_sw3, ffn_z[14], ffn_z[15]) if hc == 0 else ())
        P.op("tensor", "matmul", pe[0:1, :], onesrow[0:1, 0:1],
             brows2[0:1, 512 * half:512 * half + 512], start=False, stop=True,
             inc=mm.inc())
        t_b = mm.tok()
        P.op("scalar", "copy", out=bias3e[0:1, 512 * half:512 * half + 512],
             in_=pe[0:1, :], waits=(t_b,), inc=sc.inc())
    s_b3 = sc.tok()

    sh_done = {}
    sqmm_tok = {}
    s2_done = {}
    rows_tok = {}

    def ffn_stats_a(tc):
        for hc in range(4):
            g = 4 * tc + hc
            sqw = [v_z]
            if g >= 2:
                sqw.append(sqmm_tok[g - 2])
            P.op("scalar", "activation", out=sqscr[:, hc % 2, :],
                 in_=zeb[:, hc, 512 * tc:512 * tc + 512], func=AF.Square,
                 waits=sqw, inc=sc.inc())
            s_sq = sc.tok()
            stw = [v_z, s_b3] if tc == 0 else [sh_done[tc - 1]]
            P.op("tensor", "matmul", psE[0:1, :], ONESCB,
                 zeb[:, hc, 512 * tc:512 * tc + 512],
                 start=(hc == 0), stop=(hc == 3),
                 waits=stw if hc == 0 else ())
            P.op("tensor", "matmul", psF[0:1, :], ONESCB, sqscr[:, hc % 2, :],
                 start=(hc == 0), stop=(hc == 3), waits=(s_sq,), inc=mm.inc())
            sqmm_tok[g] = mm.tok()
        t_st = mm.tok()
        P.op("scalar", "activation", out=frows[0:1, 0, :], in_=psE[0:1, :],
             func=AF.Copy, scale=cpool[0:1, 4:5], waits=(t_st,), inc=sc.inc())
        P.op("scalar", "activation", out=frows[0:1, 1, :], in_=psF[0:1, :],
             func=AF.Copy, scale=cpool[0:1, 4:5], inc=sc.inc())
        s_mu = sc.tok()
        P.op("vector", "tensor_mul", frows[0:1, 2, :], frows[0:1, 0, :],
             frows[0:1, 0, :], waits=(s_mu,), inc=vc.inc())
        P.op("vector", "tensor_sub", frows[0:1, 1, :], frows[0:1, 1, :],
             frows[0:1, 2, :], inc=vc.inc())
        v_v = vc.tok()
        P.op("scalar", "activation", out=frows[0:1, 1, :], in_=frows[0:1, 1, :],
             func=AF.Sqrt, bias=cpool[0:1, 1:2], waits=(v_v,), inc=sc.inc())
        s_s = sc.tok()
        P.op("vector", "reciprocal", out=frows[0:1, 1, :], in_=frows[0:1, 1, :],
             waits=(s_s,), inc=vc.inc())
        P.op("vector", "tensor_copy", out=invrowb[:], in_=frows[0:1, 1, :],
             inc=vc.inc())
        P.op("vector", "tensor_mul", frows[0:1, 2, :], frows[0:1, 0, :],
             frows[0:1, 1, :], inc=vc.inc())
        P.op("vector", "tensor_scalar", out=frows[0:1, 2, :],
             in0=frows[0:1, 2, :], scalar1=-1.0, scalar2=None, op0=ALU.mult,
             inc=vc.inc())
        P.op("vector", "tensor_copy", out=minrowb[:], in_=frows[0:1, 2, :],
             inc=vc.inc())
        rows_tok[tc] = vc.tok()

    def ffn_stats_b(tc):
        P.op("tensor", "matmul", psE[0:128, :], onesrow[0:1, 0:128], invrowb[:],
             start=True, stop=True, waits=(rows_tok[tc],))
        P.op("tensor", "matmul", psF[0:128, :], onesrow[0:1, 0:128], minrowb[:],
             start=True, stop=True)
        t_rep = P.tag_last("tensor", mm)
        for hc in range(4):
            P.op("vector", "scalar_tensor_tensor", out=ttscr[:, :],
                 in0=zeb[:, hc, 512 * tc:512 * tc + 512],
                 scalar=cfc("slng", hc, 1), in1=psE[0:128, :],
                 op0=ALU.mult, op1=ALU.mult,
                 waits=(t_rep,) if hc == 0 else (), inc=vc.inc())
            P.op("vector", "scalar_tensor_tensor",
                 out=sheb[:, hc, 512 * tc:512 * tc + 512],
                 in0=psF[0:128, :], scalar=cfc("slng", hc, 1),
                 in1=ttscr[:, :], op0=ALU.mult, op1=ALU.add, inc=vc.inc())
        sh_done[tc] = vc.tok()

    def ffn_s2(tc):
        for ti in range(4):
            t = 4 * tc + ti
            for oci in range(2):
                it = 2 * t + oci
                pp = psC if it % 2 == 0 else psD
                tw = [sh_done[tc], s_b3]
                if it >= 2:
                    tw.append(s2_done[it - 2])
                for hc in range(4):
                    P.op("tensor", "matmul", pp[:, :],
                         sheb[:, hc, 128 * t:128 * t + 128],
                         sw3sb[:, hc, 512 * oci:512 * oci + 512],
                         start=(hc == 0), stop=False, waits=tw if hc == 0 else ())
                P.op("tensor", "matmul", pp[:, :], onesrow[0:1, 0:128],
                     bias3e[0:1, 512 * oci:512 * oci + 512], start=False,
                     stop=True, inc=mm.inc())
                t_s = mm.tok()
                P.op("vector", "scalar_tensor_tensor",
                     out=strm[:, t, 512 * oci:512 * oci + 512], in0=pp[:, :],
                     scalar=wsmcol[:, t:t + 1],
                     in1=strm[:, t, 512 * oci:512 * oci + 512],
                     op0=ALU.mult, op1=ALU.add, waits=(t_s,), inc=vc.inc())
                s2_done[it] = vc.tok()

    for tc in range(4):
        ffn_stats_a(tc)
        if tc >= 1:
            ffn_s2(tc - 1)
        ffn_stats_b(tc)
    ffn_s2(3)
    v_sh = vc.tok()
    v_s2 = vc.tok()
    if debug:
        P.op("sync", "dma_start", out=dbg["dbg_s2"][:, :],
             in_=strm[:].rearrange("p a t -> p (a t)"), waits=(v_s2,), inc=sy.inc())
    dbg_bar4 = ybar() if debug else None

    if upto <= 6:
        return finish_early()

    P.op("gpsimd", "dma_start", out=xsel[:],
         in_=d_rsx[:, :].rearrange("(a p) c -> p a c", p=128),
         waits=(c_rs, v_s2), inc=sg.inc())
    P.op("gpsimd", "dma_start", out=xsels[:],
         in_=d_rss[:, :].rearrange("(a p) c -> p a c", p=128), inc=sg.inc())
    g_xsel = gbar()
    for ts in range(2):
        P.op("vector", "tensor_copy", out=indf[:, ts, :],
             in_=xsels[:, ts, 0:8], waits=(g_xsel,) if ts == 0 else (),
             inc=vc.inc())
        P.op("vector", "tensor_copy", out=weff[:, ts, :],
             in_=xsels[:, ts, 8:9], inc=vc.inc())
    v_ind = vc.tok()
    for ts in range(2):
        for dc in range(8):
            P.op("sync", "dma_start_transpose",
                 out=xselfm[:, dc, 128 * ts:128 * ts + 128],
                 in_=xsel[:, ts, 128 * dc:128 * dc + 128],
                 waits=(g_xsel, v_z) if (ts == 0 and dc == 0) else (),
                 inc=sy.inc())
    y_xsel = ybar()

    def load_expert12(e, slot, waits):
        e1, e2, _ = ewS[slot]
        P.op("gpsimd", "dma_start", out=e1[:],
             in_=d_ew1T[e, :, :].rearrange("(dc p) h -> p dc h", p=128),
             waits=waits, inc=sg.inc())
        P.op("gpsimd", "dma_start", out=e2[:],
             in_=d_ew2T[e, :, :].rearrange("(dc p) h -> p dc h", p=128),
             inc=sg.inc())
        return (gbar(),)

    zfree = mm.tok()
    sqmm_tok = {}
    g_e12 = {0: load_expert12(0, 0, (zfree,) + ((dbg_bar3,) if debug else ()))}
    g_e3 = {}
    t_h12 = {}
    t_out = {}
    h12_done = {}
    v_ze = {}
    st = {"s_be_prev": s_b3, "v_she_prev": v_s2, "prev_acc": {},
          "acc_last": None}

    def emit_h12(e):
        slot = e % 2
        e1, e2, _ = ewS[slot]
        zb = zebm if slot == 0 else zebm2
        h12 = {}
        for hc in range(4):
            p1 = PSB[0] if hc % 2 == 0 else PSB[1]
            p2_ = PSB[2] if hc % 2 == 0 else PSB[3]
            tw = [*g_e12[e], y_xsel]
            if hc >= 2:
                tw.append(h12[hc - 2])
            elif e >= 1:
                tw.append(h12_done[e - 1])
            for dc in range(8):
                P.op("tensor", "matmul", p1[:, 0:256],
                     e1[:, dc, 128 * hc:128 * hc + 128], xselfm[:, dc, :],
                     start=(dc == 0), stop=(dc == 7), waits=tw if dc == 0 else ())
            for dc in range(8):
                P.op("tensor", "matmul", p2_[:, 0:256],
                     e2[:, dc, 128 * hc:128 * hc + 128], xselfm[:, dc, :],
                     start=(dc == 0), stop=(dc == 7),
                     inc=mm.inc() if dc == 7 else None)
            t_h = mm.tok()
            P.op("scalar", "activation", out=ssping[:, hc % 2, 0:256],
                 in_=p1[:, 0:256], func=AF.Silu, bias=cfc("eb1c", 4 * e + hc, 1),
                 waits=(t_h,), inc=sc.inc())
            s_sil = sc.tok()
            P.op("vector", "scalar_tensor_tensor", out=zb[:, hc, :],
                 in0=p2_[:, 0:256], scalar=cfc("eb2c", 4 * e + hc, 1),
                 in1=ssping[:, hc % 2, 0:256], op0=ALU.add, op1=ALU.mult,
                 waits=(s_sil,), inc=vc.inc())
            h12[hc] = vc.tok()
        v_ze[e] = vc.tok()
        t_h12[e] = mm.tok()
        h12_done[e] = vc.tok()

    def tail_a(e):
        slot = e % 2
        _, _, e3 = ewS[slot]
        zb = zebm if slot == 0 else zebm2
        for hc in range(4):
            g = 4 * e + hc
            sqw = [v_ze[e]]
            if g >= 2:
                sqw.append(sqmm_tok[g - 2])
            P.op("scalar", "activation", out=sqscr[:, hc % 2, 0:256],
                 in_=zb[:, hc, :], func=AF.Square, waits=sqw, inc=sc.inc())
            s_sq = sc.tok()
            P.op("tensor", "matmul", PSB[4][0:1, 0:256], ONESCB, zb[:, hc, :],
                 start=(hc == 0), stop=(hc == 3),
                 waits=(v_ze[e], st["v_she_prev"]) if hc == 0 else ())
            P.op("tensor", "matmul", PSB[5][0:1, 0:256], ONESCB,
                 sqscr[:, hc % 2, 0:256],
                 start=(hc == 0), stop=(hc == 3), waits=(s_sq,), inc=mm.inc())
            sqmm_tok[g] = mm.tok()
        t_st = mm.tok()
        P.op("scalar", "activation", out=frows[0:1, 0, 0:256],
             in_=PSB[4][0:1, 0:256], func=AF.Copy, scale=cpool[0:1, 4:5],
             waits=(t_st,), inc=sc.inc())
        P.op("scalar", "activation", out=frows[0:1, 1, 0:256],
             in_=PSB[5][0:1, 0:256], func=AF.Copy, scale=cpool[0:1, 4:5],
             inc=sc.inc())
        s_mu = sc.tok()
        P.op("vector", "tensor_mul", frows[0:1, 2, 0:256], frows[0:1, 0, 0:256],
             frows[0:1, 0, 0:256], waits=(s_mu,), inc=vc.inc())
        P.op("vector", "tensor_sub", frows[0:1, 1, 0:256], frows[0:1, 1, 0:256],
             frows[0:1, 2, 0:256], inc=vc.inc())
        v_v = vc.tok()
        P.op("scalar", "activation", out=frows[0:1, 1, 0:256],
             in_=frows[0:1, 1, 0:256], func=AF.Sqrt, bias=cpool[0:1, 1:2],
             waits=(v_v,), inc=sc.inc())
        s_s = sc.tok()
        P.op("vector", "reciprocal", out=frows[0:1, 1, 0:256],
             in_=frows[0:1, 1, 0:256], waits=(s_s,), inc=vc.inc())
        P.op("vector", "tensor_copy", out=invrowb[0:1, 0:256],
             in_=frows[0:1, 1, 0:256], inc=vc.inc())
        P.op("vector", "tensor_mul", frows[0:1, 2, 0:256], frows[0:1, 0, 0:256],
             frows[0:1, 1, 0:256], inc=vc.inc())
        P.op("vector", "tensor_scalar", out=frows[0:1, 2, 0:256],
             in0=frows[0:1, 2, 0:256], scalar1=-1.0, scalar2=None, op0=ALU.mult,
             inc=vc.inc())
        P.op("vector", "tensor_copy", out=minrowb[0:1, 0:256],
             in_=frows[0:1, 2, 0:256], inc=vc.inc())
        st["v_rows"] = vc.tok()
        for half in range(2):
            pe = PSB[6] if half == 0 else PSB[7]
            bw = [g_e3[e]]
            if e == 0:
                bw += [v_s2, s_b3]
            else:
                bw.append(st["prev_acc"][2 + half])
            for hc in range(4):
                P.op("tensor", "matmul", pe[0:1, :], cbfc("selnb", hc, 1),
                     e3[:, hc, 512 * half:512 * half + 512],
                     start=(hc == 0), stop=False,
                     waits=bw if hc == 0 else ())
            P.op("tensor", "matmul", pe[0:1, :], onesrow[0:1, 0:1],
                 brows2[0:1, 512 * half:512 * half + 512], start=False,
                 stop=True, inc=mm.inc())
            t_b = mm.tok()
            P.op("scalar", "copy", out=bias3e[0:1, 512 * half:512 * half + 512],
                 in_=pe[0:1, :], waits=(t_b,), inc=sc.inc())
        st["s_be"] = sc.tok()
        st["s_be_prev"] = st["s_be"]

    def tail_b(e):
        slot = e % 2
        _, _, e3 = ewS[slot]
        zb = zebm if slot == 0 else zebm2
        P.op("tensor", "matmul", PSB[4][0:128, 0:256], onesrow[0:1, 0:128],
             invrowb[0:1, 0:256], start=True, stop=True, waits=(st["v_rows"],))
        P.op("tensor", "matmul", PSB[5][0:128, 0:256], onesrow[0:1, 0:128],
             minrowb[0:1, 0:256], start=True, stop=True)
        t_rep = P.tag_last("tensor", mm)
        for hc in range(4):
            P.op("vector", "scalar_tensor_tensor", out=ttscr[:, 0:256],
                 in0=zb[:, hc, :], scalar=cfc("selng", hc, 1),
                 in1=PSB[4][0:128, 0:256], op0=ALU.mult, op1=ALU.mult,
                 waits=(t_rep,) if hc == 0 else (), inc=vc.inc())
            P.op("vector", "scalar_tensor_tensor", out=shebm[:, hc, :],
                 in0=PSB[5][0:128, 0:256], scalar=cfc("selng", hc, 1),
                 in1=ttscr[:, 0:256], op0=ALU.mult, op1=ALU.add, inc=vc.inc())
        v_she = vc.tok()
        st["v_she_prev"] = v_she
        acc_now = {}
        for ts in range(2):
            for oci in range(2):
                it = 2 * ts + oci
                pp = PSB[6] if it % 2 == 0 else PSB[7]
                tw = [v_she, st["s_be"]]
                if e > 0:
                    tw.append(st["prev_acc"][it])
                if it >= 2:
                    tw.append(acc_now[it - 2])
                for hc in range(4):
                    P.op("tensor", "matmul", pp[:, :],
                         shebm[:, hc, 128 * ts:128 * ts + 128],
                         e3[:, hc, 512 * oci:512 * oci + 512],
                         start=(hc == 0), stop=False, waits=tw if hc == 0 else ())
                P.op("tensor", "matmul", pp[:, :], onesrow[0:1, 0:128],
                     bias3e[0:1, 512 * oci:512 * oci + 512], start=False,
                     stop=True, inc=mm.inc())
                t_o = mm.tok()
                if e == 0:
                    P.op("vector", "tensor_scalar",
                         out=eacc[:, ts, 512 * oci:512 * oci + 512],
                         in0=pp[:, :], scalar1=indf[:, ts, e:e + 1],
                         scalar2=None, op0=ALU.mult, waits=(t_o, v_ind),
                         inc=vc.inc())
                else:
                    P.op("vector", "scalar_tensor_tensor",
                         out=eacc[:, ts, 512 * oci:512 * oci + 512],
                         in0=pp[:, :], scalar=indf[:, ts, e:e + 1],
                         in1=eacc[:, ts, 512 * oci:512 * oci + 512],
                         op0=ALU.mult, op1=ALU.add, waits=(t_o,), inc=vc.inc())
                acc_now[it] = vc.tok()
        t_out[e] = mm.tok()
        st["prev_acc"] = acc_now
        st["acc_last"] = vc.tok()

    emit_h12(0)
    o3, _ = br["eb3"]
    for e in range(8):
        if e + 1 < 8:
            g_e12[e + 1] = load_expert12(
                e + 1, (e + 1) % 2, (t_h12[e - 1],) if e >= 1 else (zfree,))
        P.op("gpsimd", "dma_start", out=ewS[e % 2][2][:],
             in_=d_ew3T[e, :, :].rearrange("(hc p) o -> p hc o", p=128),
             waits=(t_h12[e],) if e < 2 else (t_out[e - 2],), inc=sg.inc())
        P.op("gpsimd", "dma_start", out=brows2[0:1, 0:1024],
             in_=d_brows[0:1, o3 + 1024 * e:o3 + 1024 * e + 1024],
             waits=(st["s_be_prev"],), inc=sg.inc())
        g_e3[e] = gbar()
        tail_a(e)
        if e + 1 < 8:
            emit_h12(e + 1)
        tail_b(e)
    moe_prev_out = st["acc_last"]
    for ts in range(2):
        P.op("vector", "tensor_scalar", out=eacc[:, ts, :], in0=eacc[:, ts, :],
             scalar1=weff[:, ts, 0:1], scalar2=None, op0=ALU.mult,
             waits=(moe_prev_out,), inc=vc.inc())
    v_eacc = vc.tok()
    if debug:
        P.op("sync", "dma_start", out=dbg["dbg_eacc"][:, :],
             in_=eacc[:].rearrange("p a t -> p (a t)"), waits=(v_eacc,), inc=sy.inc())
    dbg_bar5 = ybar() if debug else None
    P.op("gpsimd", "dma_start",
         out=d_agin[:, :].rearrange("(a p) t -> p a t", p=128),
         in_=eacc[:], waits=(v_eacc,), inc=sg.inc())
    g_agin = gbar()
    P.op("gpsimd", "collective_compute", "AllGather", ALU.bypass,
         replica_groups=[list(range(8))], ins=[d_agin.opt()],
         outs=[d_agout[0:NSEL, :].opt()], waits=(g_agin,), inc=cc.inc())
    c_ag = cc.tok()

    if upto <= 7:
        return finish_early()

    radd = {}
    v_iv2h = {}
    for tp in range(NT // 2):
        for sub in range(2):
            t = 2 * tp + sub
            gw = []
            if t == 0:
                gw = [c_ag] + ([dbg_bar4] if debug else [])
            if t >= 2:
                gw.append(radd[t - 2])
            P.op("gpsimd", "indirect_dma_start",
                 out=retbuf[:, t % 2, :], out_offset=None,
                 in_=d_agout[:, :],
                 in_offset=IndirectOffsetOnAxis(idxs[:, 16 + t:16 + t + 1], 0),
                 waits=gw, inc=sg.inc())
        g_t = gbar()
        for sub in range(2):
            t = 2 * tp + sub
            P.op("vector", "tensor_add", strm[:, t, :], strm[:, t, :],
                 retbuf[:, t % 2, :], waits=(g_t, v_s2) if t == 0 else
                 ((g_t,) if sub == 0 else ()),
                 inc=vc.inc())
            radd[t] = vc.tok()
            P.op("vector", "reduce_sum", out=ln2c[:, 0, t:t + 1],
                 in_=strm[:, t, :], axis=AXX, inc=vc.inc())
            P.op("scalar", "activation", out=sqdummy[:, :], in_=strm[:, t, :],
                 func=AF.Square, accum_out=ln2c[:, 1, t:t + 1],
                 waits=(radd[t],), inc=sc.inc())
        t = 2 * tp + 1
        if t in (7, 15):
            hf = t // 8
            cs = slice(8 * hf, 8 * hf + 8)
            s_st = sc.tok()
            P.op("vector", "tensor_scalar", out=ln2c[:, 2, cs],
                 in0=ln2c[:, 0, cs], scalar1=1.0 / D, scalar2=None,
                 op0=ALU.mult, waits=(s_st,), inc=vc.inc())
            P.op("vector", "tensor_scalar", out=ln2c[:, 3, cs],
                 in0=ln2c[:, 1, cs], scalar1=1.0 / D, scalar2=None,
                 op0=ALU.mult, inc=vc.inc())
            P.op("vector", "tensor_mul", ln2c[:, 4, cs], ln2c[:, 2, cs],
                 ln2c[:, 2, cs], inc=vc.inc())
            P.op("vector", "tensor_sub", ln2c[:, 3, cs], ln2c[:, 3, cs],
                 ln2c[:, 4, cs], inc=vc.inc())
            v_var = vc.tok()
            P.op("scalar", "activation", out=ln2c[:, 4, cs], in_=ln2c[:, 3, cs],
                 func=AF.Sqrt, bias=cpool[:, 1:2], waits=(v_var,), inc=sc.inc())
            s_sd = sc.tok()
            P.op("vector", "reciprocal", out=ln2c[:, 5, cs], in_=ln2c[:, 4, cs],
                 waits=(s_sd,), inc=vc.inc())
            v_iv2h[hf] = vc.tok()
    st_tok = {}
    for t in range(NT):
        ow = [v_iv2h[t // 8]]
        if debug and t == 0:
            ow.append(dbg_bar5)
        if t >= 2:
            ow.append(st_tok[t - 2])
        P.op("vector", "tensor_scalar", out=otile[:, t % 2, :],
             in0=strm[:, t, :], scalar1=ln2c[:, 2, t:t + 1],
             scalar2=ln2c[:, 5, t:t + 1],
             op0=ALU.subtract, op1=ALU.mult, waits=ow, inc=vc.inc())
        v_o = vc.tok()
        if t % 2 == 0:
            P.op("sync", "dma_start", out=d_y[128 * t:128 * t + 128, :],
                 in_=otile[:, t % 2, :], waits=(v_o,), inc=sy.inc())
            st_tok[t] = ybar()
        else:
            P.op("scalar", "dma_start", out=d_y[128 * t:128 * t + 128, :],
                 in_=otile[:, t % 2, :], waits=(v_o,), inc=ssd.inc())
            st_tok[t] = sbar()
    ybar()
    sbar()

    return _emit(nc, P)


def _emit(nc, P):
    from contextlib import ExitStack
    es = ExitStack()
    with es:
        for s in P.sems:
            s.h = es.enter_context(nc.semaphore(s.name))
        block = es.enter_context(nc.Block())

        @block.tensor
        def _(eng):
            P.replay(eng, "tensor")

        @block.scalar
        def _(eng):
            P.replay(eng, "scalar")

        @block.vector
        def _(eng):
            P.replay(eng, "vector")

        @block.gpsimd
        def _(eng):
            P.replay(eng, "gpsimd")

        @block.sync
        def _(eng):
            P.replay(eng, "sync")

    return nc


def kernel(**inputs):
    ims = []
    br = None
    for c in range(8):
        im, br = _hostpack(inputs, c)
        ims.append(im)
    nc = build(br)
    res = run_bass_kernel_spmd(nc, ims, core_ids=list(range(8)))
    out = np.zeros((B, T, D), np.float32)
    for c in range(8):
        b_, half = c // 2, c % 2
        out[b_, half * NOWN:(half + 1) * NOWN] = res.results[c]["y"]
    return out
